# version 01
# baseline (speedup 1.0000x reference)
import numpy as np
from contextlib import ExitStack

import concourse.bass as bass
import concourse.mybir as mybir
from concourse.bass_utils import run_bass_kernel_spmd
from concourse.alu_op_type import AluOpType as ALU

F32 = mybir.dt.float32
U32 = mybir.dt.uint32
AX = mybir.AxisListType
AF = mybir.ActivationFunctionType

D = 1024
NSEQ = 4
SEQ = 2048
T = NSEQ * SEQ
TILE = 512
NT = T // TILE
NCH = D // 128
ROWW = 1088
NCHUNK = 40
CSLOT = 512
NSLOT = NCHUNK * CSLOT

ENGINES = ("sync", "tensor", "vector", "scalar", "gpsimd")


class Sem:
    __slots__ = ("h", "n", "name")
    def __init__(self, h, name):
        self.h, self.n, self.name = h, 0, name
    def inc(self, k):
        self.n += k
        return self.n


class Prog:
    def __init__(self):
        self.q = {e: [] for e in ENGINES}
    def emit(self, eng, build, waits=(), inc=None):
        for (s, v) in waits:
            assert v <= s.n, f"wait on future event {s.name}:{v}>{s.n} eng={eng}"
        self.q[eng].append((tuple(waits), build, inc))
    def play(self, block):
        for eng in ENGINES:
            ops = self.q[eng]
            def body(E, ops=ops):
                for waits, build, inc in ops:
                    seen = {}
                    for s, v in waits:
                        if v > seen.get(s, -1):
                            seen[s] = v
                    for s, v in seen.items():
                        E.wait_ge(s.h, v)
                    ins = build(E)
                    if inc is not None:
                        s, k = inc
                        if ins is None:
                            E.sem_inc(s.h, k)
                        else:
                            ins.then_inc(s.h, k)
            getattr(block, eng)(body)


class B:

    def __init__(self, stages):
        self.stages = stages
        self.nc = bass.Bass("TRN2", target_bir_lowering=False)
        self.P = Prog()
        self.es = ExitStack()
        self.dram = {}
        self.sb = {}

    def mksem(self, name):
        return Sem(self.es.enter_context(self.nc.semaphore(name)), name)

    def dma(self, eng, out, in_, waits=(), slow=False, ch="misc", gather_idx=None, scatter_idx=None, scatter_bound=None):
        if ch not in self.dch:
            self.dch[ch] = self.mksem(f"ch_{ch}")
        s = self.dch[ch]
        v = s.inc(16)
        def build(E, o=out, i=in_, gi=gather_idx, si=scatter_idx, slow=slow, sb=scatter_bound):
            if callable(o):
                o = o()
            if callable(i):
                i = i()
            if gi is not None:
                return E.indirect_dma_start(out=o, out_offset=None, in_=i,
                                            in_offset=bass.IndirectOffsetOnAxis(ap=gi, axis=0))
            if si is not None:
                if sb is None:
                    return E.indirect_dma_start(out=o, out_offset=bass.IndirectOffsetOnAxis(ap=si, axis=0),
                                                in_=i, in_offset=None)
                return E.indirect_dma_start(out=o, out_offset=bass.IndirectOffsetOnAxis(ap=si, axis=0),
                                            in_=i, in_offset=None,
                                            bounds_check=sb, oob_is_err=False)
            if slow:
                return E.dma_start(out=o, in_=i, allow_slow_non_contiguous=True)
            return E.dma_start(out=o, in_=i)
        self.P.emit(eng, build, waits=[w for w in waits if w], inc=(s, 16))
        return (s, v)

    def op(self, eng, build, waits=(), track=False, serial=True):
        waits = [w for w in waits if w]
        if eng in ("vector", "scalar", "gpsimd"):
            s = self.csem[eng]
            if serial and s.n > 0:
                waits.append((s, s.n))
            v = s.inc(1)
            self.P.emit(eng, build, waits=waits, inc=(s, 1))
            return (s, v)
        if track:
            s = self.csem[eng]
            v = s.inc(1)
            self.P.emit(eng, build, waits=waits, inc=(s, 1))
            return (s, v)
        self.P.emit(eng, build, waits=waits)
        return None

    def declare(self):
        nc = self.nc
        dr = self.dram
        def di(name, shape):
            dr[name] = nc.dram_tensor(name, list(shape), F32, kind="ExternalInput").ap()
        di("x_in", (T, D))
        di("L0_attn_Wqkv", (D, D)); di("L0_attn_temp", (16, 1))
        di("L0_attn_Wout", (D, D)); di("L0_attn_bout", (D,))
        di("L0_ffn_W1", (16, D, 2048)); di("L0_ffn_b1", (16, 2048))
        di("L0_ffn_W2", (16, 2048, D)); di("L0_ffn_b2", (16, D))
        di("L1_attn_qkv_w", (3 * D, D)); di("L1_attn_proj_w", (D, D)); di("L1_attn_proj_b", (D,))
        di("L1_ffn_Wr", (8, D)); di("L1_ffn_br", (8,))
        di("L1_ffn_W1", (8, 4096, D)); di("L1_ffn_b1", (8, 4096))
        di("L1_ffn_W2", (8, D, 4096)); di("L1_ffn_b2", (8, D))
        di("L1_ffn_gamma", (D,)); di("L1_ffn_beta", (D,))
        di("c_id128", (128, 128))
        di("c_ones_col", (128, 1))
        di("c_ones_row", (1, 128))
        di("c_triu", (128, 128))
        di("c_pid", (128, 1))
        di("c_erow", (128, 8))
        di("c_sel", (128, 128))
        di("c_selT", (16, 1024))
        di("c_crow", (1, NCHUNK))
        di("c_tid", (128, 64))
        dr["y_out"] = nc.dram_tensor("y_out", [T, D], F32, kind="ExternalOutput").ap()
        def ds(name, shape, dt=F32):
            dr[name] = nc.dram_tensor(name, list(shape), dt, kind="Internal").ap()
        ds("x_fm", (D, T))
        ds("w_fm", (D, T))
        ds("wp_fm", (D, T))
        ds("tgt_rows", (T + 1, ROWW))
        ds("buf", (2 * T + 130, D))
        ds("list_hbm", (NSLOT, 1), U32)
        ds("WqkvT", (D, D)); ds("WoutT", (D, D)); ds("qkvT", (D, 3 * D)); ds("projT", (D, D))
        ds("WrT", (D, 8))
        ds("W1T", (8, D, 4096)); ds("W2T", (8, 4096, D))
        ds("stgW1", (2, D, 4096)); ds("stgW2", (2, 4096, D))
        ds("dbgL", (16, T)); ds("dbgS", (128, 8 * TILE))

    def alloc(self):
        nc, es = self.nc, self.es
        sb = self.sb
        def t(name, shape, dt=F32):
            sb[name] = es.enter_context(nc.sbuf_tensor(name, list(shape), dt))
            return sb[name]
        t("id128", (128, 128)); t("ones_col", (128, 1)); t("ones_row", (1, 128))
        t("triu", (128, 128)); t("pid", (128, 1)); t("erow", (128, 8))
        t("l0b1", (128, 16)); t("l0hb2", (128, 8))
        t("l1b1", (128, 32 * 8))
        t("l1hb2", (128, 8 * 8))
        t("gam_h", (128, 8)); t("bet_h", (128, 8))
        t("bout_c", (128, 8)); t("projb_c", (128, 8))
        t("temp16", (16, 1)); t("br8", (1, 8))
        t("xt", (128, 2, NCH, TILE))
        t("tgt", (128, NCH, TILE))
        t("big", (128, 32 * TILE))
        t("y16", (128, NCH, TILE))
        t("wsl", (128, 2, 4, 128))
        t("gat", (128, 2, ROWW))
        t("st1", (1, 5, TILE))
        t("sA", (128, TILE)); t("sB", (128, TILE))
        t("rt", (128, 4, 16))
        t("lg16", (128, 16))
        t("acc8", (128, 16)); t("inv8", (128, 16)); t("s81", (128, 8))
        t("lgtm", (128, 16)); t("Pi", (16, SEQ))
        t("sel", (128, 128)); t("selT", (16, 1024))
        t("glist", (128, NSLOT // 128), U32)
        t("e8", (1, 8)); t("e8b", (1, 8)); t("e8c", (1, 8))
        t("zrow", (1, ROWW))
        t("wrs", (128, 8, 8)); t("brb", (128, 8)); t("goffb", (128, 8))
        t("ri0", (128, 64)); t("ri1", (128, 64))
        t("ps0", (128, 64)); t("ps1", (128, 64))
        t("m0", (128, 64)); t("m1", (128, 64)); t("mk", (128, 64))
        t("sc", (128, 64)); t("mm0", (128, 64))
        t("ech", (1, NCHUNK)); t("echu", (1, NCHUNK), U32); t("crow", (1, NCHUNK))
        t("ctid", (128, 64)); t("tidu", (128, 64), U32)
        t("pu0", (128, 64), U32); t("pu1", (128, 64), U32); t("sidxu", (128, 2), U32)
        t("b1c", (128, 2, 32)); t("b2c", (128, 2, 8))
        t("attn_tm", (64, 1024)); t("ctxs", (128, 8, 66)); t("bs", (128, 64)); t("kcum", (128, 8))
        sb["L16"] = sb["sA"][0:16, :]
        sb["cb0"] = sb["big"][:, 0:1024]
        sb["cb1"] = sb["big"][:, 1024:2048]
        sb["cbt"] = sb["big"][:, 2048:3072]
        self.pb = [es.enter_context(nc.psum_tensor(f"pb{i}", [128, 512], F32)) for i in range(8)]
        self.dch = {}
        self.csem = {e: self.mksem(f"c_{e}") for e in ENGINES}
        self._bank_free = [None] * 4
        self._wslab_free = [None, None]
        self._wb = 0
        self._tgt_guard = None
        self._sq_guard = None
        self._lg_guard = None
        self._rt_guard = [None, None, None, None]
        self._tp_guard = [None, None]
        self._tp2_guard = [None, None]
        self._gat_guard = [None, None]
        self._xe_guard = None
        self._eb_guard = None
        self._y16_guard = None
        self._h_guard = None
        self._ps_guard = None
        self._gl_guard = None
        self._creg = None
        self._stg_guard = [None, None]
        self._bias_guard = [None, None]
        self._ctx_guard = None
        self._sj_guard = None
        self._ktm_guard = None
        self._att_free = [None, None]
        self._att_guard = None
        self._big_guard = None
        self._y16res = None

    def s(self, name):
        return self.sb[name]

    def xprev(self, prev_store, j):
        def L(p):
            return (list(p) if isinstance(p, list) else [p]) if p else []
        w = L(prev_store[j])
        if j < 2:
            w += L(prev_store[NT - 2 + (j % 2)])
        return w

    def emit_prep(self):
        dr, s = self.dram, self.s
        ev = []
        for cn, tn in [("c_id128", "id128"), ("c_ones_col", "ones_col"), ("c_ones_row", "ones_row"),
                       ("c_triu", "triu"), ("c_pid", "pid"), ("c_erow", "erow"),
                       ("L0_attn_temp", "temp16"), ("c_sel", "sel"), ("c_selT", "selT"),
                       ("c_crow", "crow"), ("c_tid", "ctid")]:
            ev.append(self.dma("sync", s(tn)[:], dr[cn][:, :] if dr[cn].shape[1:] else dr[cn][:], ch="param"))
        b1v = dr["L0_ffn_b1"][1, :].rearrange("(c p) -> p c", p=128)
        ev.append(self.dma("sync", s("l0b1")[:], b1v, slow=True, ch="param"))
        b2v = dr["L0_ffn_b2"][1, :].rearrange("(c p) -> p c", p=128)
        ev.append(self.dma("sync", s("l0hb2")[:], b2v, slow=True, ch="param"))
        b1l = dr["L1_ffn_b1"][:, :].rearrange("e (c p) -> p (e c)", p=128)
        ev.append(self.dma("sync", s("l1b1")[:], b1l, slow=True, ch="param"))
        b2l = dr["L1_ffn_b2"][:, :].rearrange("e (c p) -> p (e c)", p=128)
        ev.append(self.dma("sync", s("l1hb2")[:], b2l, slow=True, ch="param"))
        gv = dr["L1_ffn_gamma"][:].rearrange("(c p) -> p c", p=128)
        bv = dr["L1_ffn_beta"][:].rearrange("(c p) -> p c", p=128)
        ev.append(self.dma("sync", s("gam_h")[:], gv, slow=True, ch="param"))
        ev.append(self.dma("sync", s("bet_h")[:], bv, slow=True, ch="param"))
        bo = dr["L0_attn_bout"][:].rearrange("(c p) -> p c", p=128)
        ev.append(self.dma("sync", s("bout_c")[:], bo, slow=True, ch="param"))
        pb = dr["L1_attn_proj_b"][:].rearrange("(c p) -> p c", p=128)
        ev.append(self.dma("sync", s("projb_c")[:], pb, slow=True, ch="param"))
        ev.append(self.dma("sync", s("br8")[:], dr["L1_ffn_br"][:].rearrange("(o e) -> o e", o=1), ch="param"))
        self.params_ev = ev[-1]
        last = ev[-1]
        e1 = self.op("scalar", lambda E: E.mul(s("gam_h")[:], s("gam_h")[:], 0.5), waits=[last], track=True)
        self.op("scalar", lambda E: E.mul(s("bet_h")[:], s("bet_h")[:], 0.5))
        self.op("scalar", lambda E: E.mul(s("l0hb2")[:], s("l0hb2")[:], 0.5))
        self.op("vector", lambda E: E.memset(s("zrow")[:], 0.0))
        ev0 = self.op("vector", lambda E: E.memset(s("zrow")[:, D + 3:D + 4], float(T)), track=True)
        self.zrow_ev = self.dma("gpsimd", dr["tgt_rows"][T:T + 1, :], s("zrow")[:], waits=[ev0], ch="zrow")
        zp = self.op("vector", lambda E: E.memset(s("sB")[:, 0:64], 0.0), track=True)
        for bk in range(T // 128):
            self.zrow_ev = self.dma("gpsimd", dr["tgt_rows"][bk * 128:(bk + 1) * 128, D + 4:ROWW],
                                    s("sB")[:, 0:60], waits=[zp, self.zrow_ev], ch="zrow")
        self.prep_ev = self.op("scalar", lambda E: E.copy(s("sA")[0:1, 0:1], s("gam_h")[0:1, 0:1]),
                               waits=[e1], track=True)

    def emit_pretranspose(self):
        dr, s = self.dram, self.s
        jobs = [(dr["L0_attn_Wqkv"], dr["WqkvT"], D, D)]
        jobs.append((dr["L0_attn_Wout"], dr["WoutT"], D, D))
        jobs.append((dr["L1_attn_qkv_w"], dr["qkvT"], 3 * D, D))
        jobs.append((dr["L1_attn_proj_w"], dr["projT"], D, D))
        if "l1ffn1" in self.stages or "l1ffn2" in self.stages:
            jobs.append((dr["L1_ffn_Wr"], dr["WrT"], 8, D))
            for e in range(8):
                jobs.append((dr["L1_ffn_W1"][e], dr["W1T"][e], 4096, D))
                jobs.append((dr["L1_ffn_W2"][e], dr["W2T"][e], D, 4096))
        big, wsl = s("big"), s("wsl")
        pev = [self.prep_ev]
        lastw = None
        for (W, WT, FOUT, DIN) in jobs:
            nko = (FOUT + 127) // 128
            for ko in range(nko):
                rows = min(128, FOUT - ko * 128)
                slab_in = big[:, 0:DIN]
                evl = self.dma("sync", slab_in[0:rows, :] if rows < 128 else slab_in,
                               W[ko * 128:ko * 128 + rows, :], waits=pev + ([lastw] if lastw else []), ch="pt_in")
                outs = []
                for kd in range(DIN // 128):
                    bank = self.pb[kd % 2]
                    evt = self.op("tensor",
                                  lambda E, kd=kd, rows=rows, bank=bank, slab_in=slab_in:
                                  E.transpose(bank[:, 0:rows], slab_in[0:rows, kd * 128:(kd + 1) * 128], s("id128")[0:rows, 0:rows]),
                                  waits=[evl] + ([outs[-2]] if len(outs) >= 2 else []), track=True)
                    eve = self.op("scalar",
                                  lambda E, kd=kd, rows=rows, bank=bank:
                                  E.copy(s("y16")[:].rearrange("p c t -> p (c t)")[:, kd * 128:kd * 128 + rows], bank[:, 0:rows]),
                                  waits=[evt] + ([lastw] if (kd == 0 and lastw) else []), track=True)
                    outs.append(eve)
                for kd in range(DIN // 128):
                    lastw = self.dma("scalar", WT[kd * 128:(kd + 1) * 128, ko * 128:ko * 128 + rows],
                                     s("y16")[:].rearrange("p c t -> p (c t)")[:, kd * 128:kd * 128 + rows],
                                     waits=[outs[kd]] + ([lastw] if lastw else []), ch="pt_out")
            pev = [lastw]
        self.wT_ev = lastw if lastw else self.prep_ev
        self._sq_guard = self.wT_ev

    def emit_in_transpose(self):
        dr, s = self.dram, self.s
        xin = dr["x_in"]
        prev_store = [None, None]
        prev_evs = [None, None]
        for j in range(T // 128):
            buf = j % 2
            tm = s("gat")[:, buf, 0:D]
            w = [self.zrow_ev] if j == 0 else []
            if prev_store[buf] is not None:
                w.append(prev_store[buf])
            evl = self.dma("sync", tm, xin[j * 128:(j + 1) * 128, :], waits=w, ch=f"gat{buf}")
            evs = []
            for c in range(NCH):
                bank = self.pb[4 + (c % 2)]
                bg = [evs[-2]] if len(evs) >= 2 else list(prev_evs[c:c + 1])
                evt = self.op("tensor",
                              lambda E, tm=tm, c=c, bank=bank:
                              E.transpose(bank[:, 0:128], tm[:, c * 128:(c + 1) * 128], s("id128")[:]),
                              waits=[evl, self.params_ev] + bg, track=True)
                eve = self.op("vector",
                              lambda E, c=c, bank=bank, buf=buf:
                              E.tensor_copy(s("xt")[:, buf, c, 0:128], bank[:, 0:128]),
                              waits=[evt] + ([prev_store[buf]] if (c == 0 and prev_store[buf]) else []),
                              track=True)
                evs.append(eve)
            st = self.dma("scalar",
                          dr["x_fm"][:, :].rearrange("(c p) t -> p c t", p=128)[:, :, j * 128:(j + 1) * 128],
                          s("xt")[:, buf, :, 0:128], waits=[evs[-1], evs[-2]], ch=f"xst{buf}")
            prev_store[buf] = st
            prev_evs = [evs[-2], evs[-1]]
        self.xstore_ev = [[prev_store[0], prev_store[1]]] * NT

    def emit_ln(self, xtile, evx, banks=(4, 5, 6, 7), out=None, sq=None, track_out=True):
        s = self.s
        out = out if out is not None else s("tgt")
        sq = sq if sq is not None else s("y16")
        sA, sB = s("sA"), s("sB")
        tgt_guard = [self._tgt_guard] if getattr(self, "_tgt_guard", None) else []
        sq_guard = [self._sq_guard] if getattr(self, "_sq_guard", None) else []
        evq = []
        for c in range(NCH):
            evq.append(self.op("scalar", lambda E, c=c: E.activation(sq[:, c, :], xtile[:, c, :], AF.Square),
                               waits=([evx] + sq_guard) if c == 0 else [], track=True))
        ev_s = self.op("vector", lambda E: E.tensor_tensor(sA[:], xtile[:, 0, :], xtile[:, 1, :], op=ALU.add),
                       waits=[evx])
        for c in range(2, NCH):
            ev_s = self.op("vector", lambda E, c=c: E.tensor_tensor(sA[:], sA[:], xtile[:, c, :], op=ALU.add),
                           track=(c == NCH - 1))
        ev_q = self.op("vector", lambda E: E.tensor_tensor(sB[:], sq[:, 0, :], sq[:, 1, :], op=ALU.add),
                       waits=[evq[1]])
        for c in range(2, NCH):
            ev_q = self.op("vector", lambda E, c=c: E.tensor_tensor(sB[:], sB[:], sq[:, c, :], op=ALU.add),
                           waits=[evq[c]], track=(c == NCH - 1))
        b_sx, b_sq, b_r, b_mr = (self.pb[b] for b in banks)
        m1 = self.op("tensor", lambda E: E.matmul(b_sx[0:1, :], s("ones_col")[:], sA[:], start=True, stop=True),
                     waits=[ev_s, getattr(self, "_ln_pe_guard", None)], track=True)
        m2 = self.op("tensor", lambda E: E.matmul(b_sq[0:1, :], s("ones_col")[:], sB[:], start=True, stop=True),
                     waits=[ev_q], track=True)
        st1 = s("st1")
        mrow = st1[:, 0, :]; vrow = st1[:, 1, :]; rrow = st1[:, 2, :]; mrrow = st1[:, 3, :]
        self.op("vector", lambda E: E.tensor_scalar(mrow, b_sx[0:1, :], 1.0 / D, None, op0=ALU.mult), waits=[m1])
        self.op("vector", lambda E: E.tensor_scalar(vrow, b_sq[0:1, :], 1.0 / D, None, op0=ALU.mult), waits=[m2])
        self.op("vector", lambda E: E.tensor_tensor(s("st1")[:, 4, :], mrow, mrow, op=ALU.mult))
        self.op("vector", lambda E: E.tensor_tensor(vrow, vrow, s("st1")[:, 4, :], op=ALU.subtract))
        ev_v = self.op("vector", lambda E: E.tensor_scalar(vrow, vrow, 1e-5, None, op0=ALU.add), track=True)
        ev_sq = self.op("scalar", lambda E: E.activation(vrow, vrow, AF.Sqrt), waits=[ev_v], track=True)
        ev_r = self.op("vector", lambda E: E.reciprocal(rrow, vrow), waits=[ev_sq], track=True)
        ev_mr = self.op("vector", lambda E: E.tensor_tensor(mrrow, mrow, rrow, op=ALU.mult), track=True)
        mb1 = self.op("tensor", lambda E: E.matmul(b_r[:, :], s("ones_row")[:], rrow, start=True, stop=True),
                      waits=[ev_r], track=True)
        mb2 = self.op("tensor", lambda E: E.matmul(b_mr[:, :], s("ones_row")[:], mrrow, start=True, stop=True),
                      waits=[ev_mr], track=True)
        evs = []
        for c in range(NCH):
            self.op("vector", lambda E, c=c: E.tensor_tensor(out[:, c, :], xtile[:, c, :], b_r[:, :], op=ALU.mult),
                    waits=([mb1] + tgt_guard) if c == 0 else [])
            evs.append(self.op("vector", lambda E, c=c: E.tensor_tensor(out[:, c, :], out[:, c, :], b_mr[:, :], op=ALU.subtract),
                               waits=[mb2] if c == 0 else [], track=(track_out and c >= NCH - 2)))
        return [e for e in evs if e]

    def load_x(self, j, buf, extra_waits=()):
        dr, s = self.dram, self.s
        src = dr["x_fm"][:, :].rearrange("(c p) t -> p c t", p=128)[:, :, j * TILE:(j + 1) * TILE]
        return self.dma("sync", s("xt")[:, buf, :, :], src, waits=list(extra_waits), ch=f"xld{buf}")

    def store_x(self, j, buf, waits):
        dr, s = self.dram, self.s
        dst = dr["x_fm"][:, :].rearrange("(c p) t -> p c t", p=128)[:, :, j * TILE:(j + 1) * TILE]
        return self.dma("scalar", dst, s("xt")[:, buf, :, :], waits=waits, ch=f"xst{buf}")

    def emit_l0ffn(self):
        dr, s = self.dram, self.s
        W1 = dr["L0_ffn_W1"][1]
        W2 = dr["L0_ffn_W2"][1]
        h32 = s("big").rearrange("p (c t) -> p c t", t=TILE)
        prev_store = getattr(self, "xstore_ev", [None] * NT)
        xst = [None] * NT
        for j in range(NT):
            buf = j % 2
            w = self.xprev(prev_store, j)
            if j >= 2:
                w.append(xst[j - 2])
            evx = self.load_x(j, buf, extra_waits=w)
            ln_ev = self.emit_ln(s("xt")[:, buf, :, :], evx)
            hevac = [None] * 16
            for g in range(4):
                mmev = []
                for k in range(NCH):
                    wb = self._wb = (getattr(self, "_wb", 0) + 1) % 2
                    evw = self.dma("sync", s("wsl")[:, wb, :, :],
                                   W1[k * 128:(k + 1) * 128, g * 512:(g + 1) * 512].rearrange("p (i q) -> p i q", q=128),
                                   waits=[self._wslab_free[wb]], ch=f"wsl{wb}")
                    for i in range(4):
                        bw = [evw] + (ln_ev if k == 0 else []) + ([self._bank_free[i]] if k == 0 else [])
                        mmev.append(self.op("tensor",
                                            lambda E, i=i, k=k, wb=wb:
                                            E.matmul(self.pb[i][:, :], s("wsl")[:, wb, i, :], s("tgt")[:, k, :],
                                                     start=(k == 0), stop=(k == NCH - 1)),
                                            waits=bw, track=(i == 3)))
                    self._wslab_free[wb] = mmev[-1]
                if g == 3:
                    self._tgt_guard = mmev[-1]
                for i in range(4):
                    c = g * 4 + i
                    hevac[c] = self._bank_free[i] = self.op(
                        "scalar",
                        lambda E, i=i, c=c: E.activation(h32[:, c, :], self.pb[i][:, :], AF.Relu,
                                                         bias=s("l0b1")[:, c:c + 1]),
                        waits=[mmev[NCH * 4 - 1]], track=True)
            yev = []
            for g in range(2):
                mmev = []
                for k in range(16):
                    wb = self._wb = (self._wb + 1) % 2
                    evw = self.dma("sync", s("wsl")[:, wb, :, :],
                                   W2[k * 128:(k + 1) * 128, g * 512:(g + 1) * 512].rearrange("p (i q) -> p i q", q=128),
                                   waits=[self._wslab_free[wb]], ch=f"wsl{wb}")
                    for i in range(4):
                        bw = [evw, self._bank_free[i] if k == 0 else None, hevac[k]]
                        mmev.append(self.op("tensor",
                                            lambda E, i=i, k=k, wb=wb:
                                            E.matmul(self.pb[i][:, :], s("wsl")[:, wb, i, :], h32[:, k, :],
                                                     start=(k == 0), stop=(k == 15)),
                                            waits=bw, track=(i == 3)))
                    self._wslab_free[wb] = mmev[-1]
                for i in range(4):
                    c = g * 4 + i
                    self._bank_free[i] = self.op(
                        "vector",
                        lambda E, i=i, c=c: E.tensor_scalar(s("y16")[:, c, :], self.pb[i][:, :],
                                                            0.5, s("l0hb2")[:, c:c + 1],
                                                            op0=ALU.mult, op1=ALU.add),
                        waits=[mmev[63]], track=True)
                    yev.append(self._bank_free[i])
            evu = []
            for c in range(NCH):
                evu.append(self.op("vector",
                                   lambda E, c=c, buf=buf: E.tensor_tensor(s("xt")[:, buf, c, :], s("xt")[:, buf, c, :],
                                                                           s("y16")[:, c, :], op=ALU.add),
                                   waits=[yev[c]], track=(c >= NCH - 2)))
            self._sq_guard = evu[-1]
            xst[j] = self.store_x(j, buf, waits=[e for e in evu if e])
        self.xstore_ev = xst

    def vn(self):
        return (self.csem["vector"], self.csem["vector"].n)

    def an(self):
        return (self.csem["scalar"], self.csem["scalar"].n)

    def emit_l0attn(self):
        dr, s = self.dram, self.s
        prev_store = self.xstore_ev
        xst = [None] * NT
        w_fm = dr["w_fm"][:, :].rearrange("(c p) t -> p c t", p=128)
        wp_fm = dr["wp_fm"][:, :].rearrange("(c p) t -> p c t", p=128)
        TPS = SEQ // TILE
        pb = self.pb
        for q in range(NSEQ):
            ev_z = self.op("vector", lambda E: E.memset(s("acc8")[:], 0.0))
            wstall = [None] * TPS
            for tj in range(TPS):
                j = q * TPS + tj
                buf = j % 2
                evx = self.load_x(j, buf, extra_waits=self.xprev(prev_store, j) + [self.vn(), self.an(),
                                                                                    xst[j - 2] if j >= 2 else None])
                ln_ev = self.emit_ln(s("xt")[:, buf, :, :], evx)
                stage = s("y16")
                mmev = []
                for g in range(2):
                    for k in range(NCH):
                        wb = self._wb = (self._wb + 1) % 2
                        evw = self.dma("sync", s("wsl")[:, wb, :, :],
                                       dr["WqkvT"][k * 128:(k + 1) * 128, g * 512:(g + 1) * 512].rearrange("p (i q) -> p i q", q=128),
                                       waits=[self._wslab_free[wb],
                                              self.wT_ev if (q == 0 and tj == 0 and k == 0) else None], ch=f"wsl{wb}")
                        for i in range(4):
                            bw = [evw] + (ln_ev if k == 0 else []) + ([self._bank_free[i]] if k == 0 else [])
                            mmev.append(self.op("tensor",
                                                lambda E, i=i, k=k, wb=wb:
                                                E.matmul(pb[i][:, :], s("wsl")[:, wb, i, :], s("tgt")[:, k, :],
                                                         start=(k == 0), stop=(k == NCH - 1)),
                                                waits=bw, track=(i == 3)))
                        self._wslab_free[wb] = mmev[-1]
                    self._tgt_guard = mmev[-1]
                    for i in range(4):
                        c = g * 4 + i
                        self._bank_free[i] = self.op("vector",
                                                     lambda E, i=i, c=c, stage=stage: E.tensor_copy(stage[:, c, :], pb[i][:, :]),
                                                     waits=[mmev[(g * NCH + NCH - 1) * 4 + 3],
                                                            self._sq_guard if c == 0 else None])
                for c in range(NCH):
                    self.op("scalar", lambda E, c=c, stage=stage: E.activation(s("sB")[:], stage[:, c, :], AF.Square,
                                                                  accum_out=s("s81")[:, 0:1]),
                            waits=[self.vn(), ev_z if c == 0 else None])
                    self.op("vector", lambda E, c=c: E.tensor_tensor(s("acc8")[:, c:c + 1], s("acc8")[:, c:c + 1],
                                                                     s("s81")[:, 0:1], op=ALU.add),
                            waits=[self.an()])
                wstall[tj] = self.dma("scalar", w_fm[:, :, j * TILE:(j + 1) * TILE], stage[:, :, :],
                                      waits=[self.vn(), self.an()], ch=f"wst{tj}")
                self._sq_guard = wstall[tj]
            self.op("scalar", lambda E: E.activation(s("acc8")[:, 0:8], s("acc8")[:, 0:8], AF.Sqrt), waits=[self.vn()])
            self.op("vector", lambda E: E.tensor_scalar(s("acc8")[:, 0:8], s("acc8")[:, 0:8], 1e-12, None, op0=ALU.max),
                    waits=[self.an()])
            self.op("vector", lambda E: E.tensor_tensor(s("acc8")[:, 0:8], s("acc8")[:, 0:8], s("acc8")[:, 0:8], op=ALU.mult))
            ev_inv = self.op("vector", lambda E: E.reciprocal(s("inv8")[:, 0:8], s("acc8")[:, 0:8]))
            wpstall = [None] * TPS
            tpg = [None, None]
            for tj in range(TPS):
                j = q * TPS + tj
                buf = j % 2
                evw = self.dma("sync", s("xt")[:, buf, :, :], w_fm[:, :, j * TILE:(j + 1) * TILE],
                               waits=[wstall[tj], self.vn(), self.an()], ch=f"xld{buf}")
                wtile = s("xt")[:, buf, :, :]
                stage = s("y16")
                selev = None
                for c in range(NCH):
                    self.op("scalar", lambda E, c=c, wtile=wtile: E.activation(s("sB")[:], wtile[:, c, :], AF.Square),
                            waits=[evw, ev_inv, self.vn(), selev])
                    self.op("vector", lambda E, c=c: E.tensor_scalar(s("sB")[:], s("sB")[:], s("inv8")[:, c:c + 1], None, op0=ALU.mult),
                            waits=[self.an()])
                    selev = self.op("tensor",
                                    lambda E, c=c: E.matmul(pb[4][0:16, :], s("sel")[:, c * 16:(c + 1) * 16], s("sB")[:],
                                                            start=(c == 0), stop=(c == NCH - 1)),
                                    waits=[self.vn(), selev], track=True)
                ev_L = self.op("vector", lambda E: E.tensor_scalar(s("L16")[:, :], pb[4][0:16, :], s("temp16")[:, 0:1], None, op0=ALU.mult),
                               waits=[selev, getattr(self, "_dbg_ev", None)])
                if getattr(self, "debug", False):
                    self._dbg_ev = self.dma("gpsimd", dr["dbgL"][:, j * TILE:(j + 1) * TILE], s("L16")[:, :],
                                            waits=[ev_L], ch="dbg")
                for gq in range(4):
                    tp1 = self.op("tensor", lambda E, gq=gq: E.transpose(pb[5][:, 0:16], s("L16")[:, gq * 128:(gq + 1) * 128], s("id128")[0:16, 0:16]),
                                  waits=[ev_L, tpg[0]], track=True)
                    self.op("vector", lambda E: E.tensor_copy(s("lgtm")[:, 0:16], pb[5][:, 0:16]), waits=[tp1])
                    tpg[0] = self.vn()
                    self.op("vector", lambda E: E.reduce_max(s("s81")[:, 0:1], s("lgtm")[:, 0:16], axis=AX.X))
                    self.op("vector", lambda E: E.tensor_scalar(s("s81")[:, 1:2], s("s81")[:, 0:1], -1.0, None, op0=ALU.mult))
                    self.op("scalar", lambda E: E.activation(s("lgtm")[:, 0:16], s("lgtm")[:, 0:16], AF.Exp, bias=s("s81")[:, 1:2]),
                            waits=[self.vn()])
                    self.op("vector", lambda E: E.reduce_sum(s("s81")[:, 2:3], s("lgtm")[:, 0:16], axis=AX.X), waits=[self.an()])
                    self.op("vector", lambda E: E.reciprocal(s("s81")[:, 3:4], s("s81")[:, 2:3]))
                    evp = self.op("vector", lambda E: E.tensor_scalar(s("lgtm")[:, 0:16], s("lgtm")[:, 0:16], s("s81")[:, 3:4], None, op0=ALU.mult))
                    tp2 = self.op("tensor", lambda E: E.transpose(pb[6][0:16, 0:128], s("lgtm")[:, 0:16], s("id128")[:]),
                                  waits=[evp, tpg[1]], track=True)
                    tpg[1] = self.op("vector", lambda E, tj=tj, gq=gq: E.tensor_copy(s("Pi")[:, tj * TILE + gq * 128:tj * TILE + (gq + 1) * 128], pb[6][0:16, 0:128]),
                                     waits=[tp2])
                exv = None
                for c in range(NCH):
                    exv = self.op("tensor", lambda E, c=c, tj=tj: E.matmul(pb[7][:, :], s("selT")[:, c * 128:(c + 1) * 128],
                                                                           s("Pi")[:, tj * TILE:(tj + 1) * TILE], start=True, stop=True),
                                  waits=[self.vn(), exv], track=True)
                    self.op("vector", lambda E, c=c, stage=stage, wtile=wtile: E.tensor_tensor(stage[:, c, :], wtile[:, c, :], pb[7][:, :], op=ALU.mult),
                            waits=[exv, self._sq_guard if c == 0 else None])
                    self.op("vector", lambda E, c=c, stage=stage, wtile=wtile: E.tensor_tensor(s("sB")[:], stage[:, c, :], wtile[:, c, :], op=ALU.mult))
                    self.op("vector", lambda E, c=c: E.tensor_reduce(s("s81")[:, 4:5], s("sB")[:], axis=AX.X, op=ALU.add))
                    self.op("vector", lambda E, c=c: E.tensor_tensor(s("acc8")[:, 8 + c:9 + c], s("acc8")[:, 8 + c:9 + c],
                                                                     s("s81")[:, 4:5], op=ALU.add))
                wpstall[tj] = self.dma("scalar", wp_fm[:, :, j * TILE:(j + 1) * TILE], stage[:, :, :],
                                       waits=[self.vn()], ch=f"wst{tj}")
                self._sq_guard = wpstall[tj]
            self.op("vector", lambda E: E.reduce_sum(s("s81")[0:16, 5:6], s("Pi")[:, :], axis=AX.X))
            self.op("vector", lambda E: E.tensor_scalar(s("s81")[0:16, 5:6], s("s81")[0:16, 5:6], 1e-8, None, op0=ALU.add))
            ev_ps = self.op("vector", lambda E: E.reciprocal(s("s81")[0:16, 6:7], s("s81")[0:16, 5:6]))
            exv = None
            for c in range(NCH):
                exv = self.op("tensor", lambda E, c=c: E.matmul(pb[7][:, 0:1], s("selT")[:, c * 128:(c + 1) * 128],
                                                                s("s81")[0:16, 6:7], start=True, stop=True),
                              waits=[ev_ps, self.vn(), exv], track=True)
                self.op("vector", lambda E, c=c: E.tensor_tensor(s("inv8")[:, 8 + c:9 + c], s("acc8")[:, 8 + c:9 + c],
                                                                 pb[7][:, 0:1], op=ALU.mult), waits=[exv])
                self.op("vector", lambda E, c=c: E.tensor_scalar(s("inv8")[:, 8 + c:9 + c], s("inv8")[:, 8 + c:9 + c], 1.0, None, op0=ALU.add))
                self.op("vector", lambda E, c=c: E.reciprocal(s("inv8")[:, 8 + c:9 + c], s("inv8")[:, 8 + c:9 + c]))
            ev_att = self.op("vector", lambda E: E.tensor_scalar(s("inv8")[:, 8:16], s("inv8")[:, 8:16], -1.0, None, op0=ALU.mult))
            for tj in range(TPS):
                j = q * TPS + tj
                buf = j % 2
                evwp = self.dma("sync", s("y16")[:, :, :], wp_fm[:, :, j * TILE:(j + 1) * TILE],
                                waits=[wpstall[tj], self.vn()] + ([wpstall[2], wpstall[3]] if tj == 0 else []),
                                ch=f"wld0")
                evx = self.load_x(j, buf, extra_waits=self.xprev(prev_store, j) + [self.vn(), xst[j - 2] if j >= 2 else None])
                oev = None
                for c in range(NCH):
                    oev = self.op("vector", lambda E, c=c: E.tensor_scalar(s("tgt")[:, c, :], s("y16")[:, c, :],
                                                                           s("inv8")[:, 8 + c:9 + c], None, op0=ALU.mult),
                                  waits=[evwp, ev_att, self._tgt_guard] if c == 0 else [])
                mmev = []
                for g in range(2):
                    for k in range(NCH):
                        wb = self._wb = (self._wb + 1) % 2
                        evw = self.dma("sync", s("wsl")[:, wb, :, :],
                                       dr["WoutT"][k * 128:(k + 1) * 128, g * 512:(g + 1) * 512].rearrange("p (i q) -> p i q", q=128),
                                       waits=[self._wslab_free[wb]], ch=f"wsl{wb}")
                        for i in range(4):
                            bw = [evw] + ([oev] if k == 0 else []) + ([self._bank_free[i]] if k == 0 else [])
                            mmev.append(self.op("tensor",
                                                lambda E, i=i, k=k, wb=wb:
                                                E.matmul(pb[i][:, :], s("wsl")[:, wb, i, :], s("tgt")[:, k, :],
                                                         start=(k == 0), stop=(k == NCH - 1)),
                                                waits=bw, track=(i == 3)))
                        self._wslab_free[wb] = mmev[-1]
                    self._tgt_guard = mmev[-1]
                    for i in range(4):
                        c = g * 4 + i
                        self._bank_free[i] = self.op("vector",
                                                     lambda E, i=i, c=c: E.tensor_scalar(s("sB")[:], pb[i][:, :], s("bout_c")[:, c:c + 1], None, op0=ALU.add),
                                                     waits=[mmev[(g * NCH + NCH - 1) * 4 + 3], evx if c == 0 else None])
                        self.op("vector", lambda E, c=c, buf=buf: E.tensor_tensor(s("xt")[:, buf, c, :], s("xt")[:, buf, c, :],
                                                                                  s("sB")[:], op=ALU.add))
                xst[j] = self.store_x(j, buf, waits=[self.vn()])
        self.xstore_ev = xst
        self._sq_guard = xst[-1]

    def emit_l1ffn(self, cidx):
        dr, s = self.dram, self.s
        pb = self.pb
        prev_store = self.xstore_ev
        TR = dr["tgt_rows"]
        tag = f"f{cidx}"
        evwr = self.dma("sync", s("wrs")[:], dr["WrT"][:, :].rearrange("(k p) e -> p k e", p=128),
                        waits=[self.wT_ev], ch="wrs", slow=True)
        evbr = self.op("tensor", lambda E: E.matmul(pb[3][:, 0:8], s("ones_row")[:], s("br8")[:, :], start=True, stop=True),
                       waits=[self.params_ev, self._bank_free[3], self.vn()], track=True)
        evbrb = self.op("vector", lambda E: E.tensor_copy(s("brb")[:], pb[3][:, 0:8]), waits=[evbr])
        self._bank_free[3] = evbrb
        gstore = [None, None]
        for j in range(NT):
            buf = j % 2
            w = self.xprev(prev_store, j) + [self.vn(), self.an()]
            evx = self.load_x(j, buf, extra_waits=w)
            ln_ev = self.emit_ln(s("xt")[:, buf, :, :], evx)
            lgev = None
            for sg in range(4):
                for k in range(NCH):
                    lgev = self.op("tensor",
                                   lambda E, sg=sg, k=k: E.matmul(pb[0][:, sg * 8:(sg + 1) * 8],
                                                                  s("tgt")[:, k, sg * 128:(sg + 1) * 128],
                                                                  s("wrs")[:, k, :],
                                                                  start=(sg == 0 and k == 0), stop=(sg == 3 and k == NCH - 1)),
                                   waits=[evwr, self._lg_guard if (sg == 0 and k == 0) else None,
                                          self._bank_free[0] if (sg == 0 and k == 0) else None] +
                                         (ln_ev if (sg == 0 and k == 0) else []),
                                   track=(k == NCH - 1))
            for sg in range(4):
                rb = s("rt")[:, sg, :]
                self.op("vector", lambda E, sg=sg: E.tensor_tensor(s("lg16")[:, 0:8], pb[0][:, sg * 8:(sg + 1) * 8], s("brb")[:], op=ALU.add),
                        waits=[lgev, self._rt_guard[sg]])
                self.op("vector", lambda E: E.reduce_max(s("s81")[:, 0:1], s("lg16")[:, 0:8], axis=AX.X))
                self.op("vector", lambda E: E.tensor_scalar(s("lg16")[:, 8:16], s("lg16")[:, 0:8], s("s81")[:, 0:1], None, op0=ALU.is_equal))
                self.op("vector", lambda E: E.tensor_tensor(s("sB")[:, 0:8], s("lg16")[:, 8:16], s("erow")[:], op=ALU.mult))
                self.op("vector", lambda E, rb=rb: E.reduce_sum(rb[:, 2:3], s("sB")[:, 0:8], axis=AX.X))
                self.op("vector", lambda E: E.scalar_tensor_tensor(s("lg16")[:, 8:16], s("lg16")[:, 8:16], -1e30, s("lg16")[:, 0:8], op0=ALU.mult, op1=ALU.add))
                self.op("vector", lambda E: E.reduce_max(s("s81")[:, 1:2], s("lg16")[:, 8:16], axis=AX.X))
                self.op("vector", lambda E: E.tensor_scalar(s("lg16")[:, 8:16], s("lg16")[:, 8:16], s("s81")[:, 1:2], None, op0=ALU.is_equal))
                self.op("vector", lambda E: E.tensor_tensor(s("sB")[:, 0:8], s("lg16")[:, 8:16], s("erow")[:], op=ALU.mult))
                self.op("vector", lambda E, j=j, sg=sg: E.reduce_sum(s("ri1")[:, 4 * j + sg:4 * j + sg + 1], s("sB")[:, 0:8], axis=AX.X))
                self.op("vector", lambda E: E.tensor_tensor(s("s81")[:, 2:3], s("s81")[:, 0:1], s("s81")[:, 1:2], op=ALU.subtract))
                self.op("scalar", lambda E, rb=rb: E.activation(rb[:, 0:1], s("s81")[:, 2:3], AF.Sigmoid), waits=[self.vn()])
                self.op("vector", lambda E: E.tensor_scalar(s("s81")[:, 3:4], s("s81")[:, 2:3], -1.0, None, op0=ALU.mult), waits=[self.an()])
                self.op("scalar", lambda E, rb=rb: E.activation(rb[:, 1:2], s("s81")[:, 3:4], AF.Sigmoid), waits=[self.vn()])
                self.op("vector", lambda E, rb=rb, j=j, sg=sg: E.tensor_scalar(rb[:, 3:4], s("pid")[:], float(128 * (4 * j + sg)), None, op0=ALU.add),
                        waits=[self.an()])
                self.op("vector", lambda E, rb=rb, j=j, sg=sg: E.tensor_copy(s("ri0")[:, 4 * j + sg:4 * j + sg + 1], rb[:, 2:3]))
                buf2 = sg % 2
                tev = []
                for c in range(NCH):
                    bank = pb[1 + (c % 2)]
                    tp = self.op("tensor",
                                 lambda E, c=c, sg=sg, bank=bank: E.transpose(bank[:, 0:128], s("tgt")[:, c, sg * 128:(sg + 1) * 128], s("id128")[:]),
                                 waits=[self._tp2_guard[c % 2], lgev,
                                        self._bank_free[1 + (c % 2)] if (j == 0 and sg == 0 and c < 2) else None],
                                 track=True)
                    ev = self.op("vector",
                                 lambda E, c=c, buf2=buf2, bank=bank:
                                 E.tensor_copy(s("gat")[:, buf2, c * 128:(c + 1) * 128], bank[:, 0:128]),
                                 waits=[tp, gstore[buf2] if c == 0 else None])
                    self._tp2_guard[c % 2] = ev
                    tev.append(ev)
                t0 = j * TILE + sg * 128
                self.dma("scalar", TR[t0:t0 + 128, 0:D], s("gat")[:, sg % 2, 0:D],
                         waits=[self.vn()], ch=f"trow{sg % 2}")
                g2 = self.dma("scalar", TR[t0:t0 + 128, D:D + 4], s("rt")[:, sg, 0:4],
                              waits=[self.vn()], ch=f"trow{sg % 2}")
                gstore[sg % 2] = g2
                self._rt_guard[sg] = g2
            self._tgt_guard = self._tp2_guard[1]
            self._lg_guard = self.vn()
        ev_pre0 = self.op("vector", lambda E: E.memset(s("sB")[:, 0:160], 8192.0), waits=[self._ps_guard])
        ev_pre1 = self.op("vector", lambda E: E.tensor_copy(s("glist")[:], s("sB")[:, 0:160]))
        ev_pre = self.dma("gpsimd", dr["list_hbm"][0:NSLOT // 2, :].rearrange("(g p) o -> p (g o)", p=128),
                          s("glist")[:, 0:80], waits=[ev_pre1], ch="lpre", slow=True)
        ev_pre = self.dma("gpsimd", dr["list_hbm"][NSLOT // 2:, :].rearrange("(g p) o -> p (g o)", p=128),
                          s("glist")[:, 80:160], waits=[ev_pre1, ev_pre], ch="lpre", slow=True)
        self.op("vector", lambda E: E.memset(s("ps0")[:], 0.0))
        self.op("vector", lambda E: E.memset(s("ps1")[:], 0.0))
        evz = self.op("vector", lambda E: E.memset(s("e8")[:, :], 0.0))
        for e in range(8):
            self.op("vector", lambda E, e=e: E.tensor_scalar(s("m0")[:], s("ri0")[:], float(e), None, op0=ALU.is_equal), waits=[evz])
            self.op("vector", lambda E, e=e: E.tensor_scalar(s("m1")[:], s("ri1")[:], float(e), None, op0=ALU.is_equal))
            self.op("vector", lambda E: E.tensor_tensor(s("mk")[:], s("m0")[:], s("m1")[:], op=ALU.add))
            self.op("vector", lambda E: E.tensor_tensor_scan(s("sc")[:], s("mk")[:], s("mk")[:], 0.0, op0=ALU.add, op1=ALU.bypass))
            evtot = self.op("vector", lambda E: E.reduce_sum(s("s81")[:, 0:1], s("mk")[:], axis=AX.X))
            evpf = self.op("tensor", lambda E: E.matmul(pb[3][:, 0:1], s("triu")[:], s("s81")[:, 0:1], start=True, stop=True),
                           waits=[evtot, self._bank_free[3]], track=True)
            evne = self.op("tensor", lambda E: E.matmul(pb[3][0:1, 1:2], s("ones_col")[:], s("s81")[:, 0:1], start=False, stop=True,
                                                        skip_group_check=True),
                           waits=[evtot], track=True)
            self.op("vector", lambda E: E.tensor_tensor(s("sc")[:], s("sc")[:], s("mk")[:], op=ALU.subtract), waits=[evpf])
            self.op("vector", lambda E: E.tensor_scalar(s("sc")[:], s("sc")[:], pb[3][:, 0:1], None, op0=ALU.add))
            self.op("vector", lambda E: E.tensor_tensor(s("mm0")[:], s("sc")[:], s("m0")[:], op=ALU.mult))
            self.op("vector", lambda E: E.tensor_tensor(s("ps0")[:], s("ps0")[:], s("mm0")[:], op=ALU.add))
            self.op("vector", lambda E: E.tensor_tensor(s("mm0")[:], s("sc")[:], s("m1")[:], op=ALU.mult))
            self.op("vector", lambda E: E.tensor_tensor(s("ps1")[:], s("ps1")[:], s("mm0")[:], op=ALU.add))
            evcp = self.op("vector", lambda E, e=e: E.tensor_copy(s("e8")[0:1, e:e + 1], pb[3][0:1, 1:2]), waits=[evne])
            self._bank_free[3] = evcp
        self.op("vector", lambda E: E.memset(s("e8b")[:, :], 0.0))
        for kk in range(1, 17):
            self.op("vector", lambda E, kk=kk: E.scalar_tensor_tensor(s("e8b")[:, :], s("e8")[:, :], float(512 * (kk - 1)),
                                                                      s("e8b")[:, :], op0=ALU.is_gt, op1=ALU.add))
        self.op("vector", lambda E: E.tensor_tensor_scan(s("e8c")[:, :], s("e8b")[:, :], s("e8b")[:, :], 0.0, op0=ALU.add, op1=ALU.bypass))
        self.op("vector", lambda E: E.tensor_tensor(s("e8")[:, :], s("e8c")[:, :], s("e8b")[:, :], op=ALU.subtract))
        evgo = self.op("vector", lambda E: E.tensor_scalar(s("e8")[:, :], s("e8")[:, :], 512.0, None, op0=ALU.mult))
        self.op("vector", lambda E: E.memset(s("ech")[:, :], 0.0))
        for e in range(8):
            self.op("vector", lambda E, e=e: E.scalar_tensor_tensor(s("ech")[:, :], s("crow")[:, :], s("e8c")[0:1, e:e + 1],
                                                                    s("ech")[:, :], op0=ALU.is_ge, op1=ALU.add))
        self.op("vector", lambda E: E.tensor_scalar(s("ech")[:, :], s("ech")[:, :], 7.0, None, op0=ALU.min))
        evecu = self.op("vector", lambda E: E.tensor_copy(s("echu")[:, :], s("ech")[:, :]))
        evgb = self.op("tensor", lambda E: E.matmul(pb[3][:, 2:10], s("ones_row")[:], s("e8")[0:1, :], start=False, stop=True,
                                                    skip_group_check=True),
                       waits=[evgo], track=True)
        evgbc = self.op("vector", lambda E: E.tensor_copy(s("goffb")[:], pb[3][:, 2:10]), waits=[evgb])
        self._bank_free[3] = evgbc
        for e in range(8):
            self.op("vector", lambda E, e=e: E.tensor_scalar(s("m0")[:], s("ri0")[:], float(e), None, op0=ALU.is_equal))
            self.op("vector", lambda E, e=e: E.scalar_tensor_tensor(s("ps0")[:], s("m0")[:], s("goffb")[:, e:e + 1], s("ps0")[:],
                                                                    op0=ALU.mult, op1=ALU.add))
            self.op("vector", lambda E, e=e: E.tensor_scalar(s("m1")[:], s("ri1")[:], float(e), None, op0=ALU.is_equal))
            self.op("vector", lambda E, e=e: E.scalar_tensor_tensor(s("ps1")[:], s("m1")[:], s("goffb")[:, e:e + 1], s("ps1")[:],
                                                                    op0=ALU.mult, op1=ALU.add))
        evq0 = self.op("vector", lambda E: E.tensor_copy(s("pu0")[:], s("ps0")[:]))
        evq1 = self.op("vector", lambda E: E.tensor_copy(s("pu1")[:], s("ps1")[:]))
        evtd = self.op("vector", lambda E: E.tensor_copy(s("tidu")[:], s("ctid")[:]), waits=[self.params_ev])
        scev = []
        for g in range(64):
            scev.append(self.dma("gpsimd", dr["list_hbm"][:, :], s("tidu")[:, g:g + 1],
                                 waits=[evq0, evtd, ev_pre], ch=f"lsc{g % 4}",
                                 scatter_idx=s("pu0")[:, g:g + 1]))
        for g in range(64):
            scev.append(self.dma("gpsimd", dr["list_hbm"][:, :], s("tidu")[:, g:g + 1],
                                 waits=[evq1, evtd, ev_pre], ch=f"lsc{g % 4}",
                                 scatter_idx=s("pu1")[:, g:g + 1]))
        evgl = self.dma("sync", s("glist")[:, 0:80],
                        dr["list_hbm"][0:NSLOT // 2, :].rearrange("(g p) o -> p (g o)", p=128),
                        waits=scev[-4:] + [ev_pre1], ch="glud", slow=True)
        evgl = self.dma("sync", s("glist")[:, 80:160],
                        dr["list_hbm"][NSLOT // 2:, :].rearrange("(g p) o -> p (g o)", p=128),
                        waits=scev[-4:] + [evgl], ch="glud", slow=True)
        self._ps_guard = evgl
        ysc = [None, None]
        h32 = s("big").rearrange("p (cc t) -> p cc t", t=TILE)
        xe = s("tgt")
        for pp in range(2):
            if f"stg{pp}" not in self.dch:
                self.dch[f"stg{pp}"] = self.mksem(f"ch_stg{pp}")
        stg_ev = [None, None]
        stg_free = [None, None]
        bias_ev = [None, None]
        def emit_stage(c, waits):
            par = c % 2
            stg = self.dch[f"stg{par}"]
            v = stg.inc(16 * 2)
            if f"bia{par}" not in self.dch:
                self.dch[f"bia{par}"] = self.mksem(f"ch_bia{par}")
            bch = self.dch[f"bia{par}"]
            bv = bch.inc(16 * 2)
            def build(E, c=c, par=par):
                if self._creg is None:
                    self._creg = E.alloc_register()
                E.reg_load(self._creg, s("echu")[0:1, c:c + 1])
                for e in range(8):
                    with E.If_eq(self._creg, e):
                        E.dma_start(out=dr["stgW1"][par], in_=dr["W1T"][e]).then_inc(stg.h, 16)
                        E.dma_start(out=dr["stgW2"][par], in_=dr["W2T"][e]).then_inc(stg.h, 16)
                        E.dma_start(out=s("b1c")[:, par, :], in_=dr["L1_ffn_b1"][e, :].rearrange("(cc p) -> p cc", p=128),
                                    allow_slow_non_contiguous=True).then_inc(bch.h, 16)
                        E.dma_start(out=s("b2c")[:, par, :], in_=dr["L1_ffn_b2"][e, :].rearrange("(cc p) -> p cc", p=128),
                                    allow_slow_non_contiguous=True).then_inc(bch.h, 16)
            self.P.emit("sync", build, waits=[w for w in waits if w])
            return (stg, v), (bch, bv)
        ev0, bv0 = emit_stage(0, [evecu, self._stg_guard[0]])
        stg_ev[0], bias_ev[0] = ev0, bv0
        for c in range(NCHUNK):
            par = c % 2
            if c + 1 < NCHUNK:
                stg_ev[1 - par], bias_ev[1 - par] = emit_stage(
                    c + 1, [evecu, stg_free[1 - par], self._stg_guard[1 - par] if c == 0 else None,
                            self._bias_guard[1 - par]])
            gtev = []
            for sg in range(4):
                b2 = sg % 2
                evg = self.dma("gpsimd", s("gat")[:, b2, :], dr["tgt_rows"][:, :],
                               waits=[evgl, self.zrow_ev, self._gat_guard[b2]] +
                                     ([gstore[0], gstore[1]] if c == 0 and sg == 0 else []),
                               ch=f"gath{b2}", gather_idx=s("glist")[:, 4 * c + sg:4 * c + sg + 1])
                evrt = self.op("vector", lambda E, sg=sg, b2=b2: E.tensor_copy(s("rt")[:, sg, 0:4], s("gat")[:, b2, D:D + 4]),
                               waits=[evg, self._rt_guard[sg]])
                self._rt_guard[sg] = evrt
                for cc in range(NCH):
                    bank = pb[4 + (cc % 2)]
                    tp = self.op("tensor",
                                 lambda E, cc=cc, b2=b2, bank=bank:
                                 E.transpose(bank[:, 0:128], s("gat")[:, b2, cc * 128:(cc + 1) * 128], s("id128")[:]),
                                 waits=[evg, self._tp_guard[cc % 2], self._xe_guard if (cc == 0 and sg == 0) else None],
                                 track=True)
                    ev = self.op("vector",
                                 lambda E, cc=cc, sg=sg, bank=bank: E.tensor_copy(xe[:, cc, sg * 128:(sg + 1) * 128], bank[:, 0:128]),
                                 waits=[tp])
                    self._tp_guard[cc % 2] = ev
                    gtev.append(ev)
                self._gat_guard[b2] = gtev[-1]
            mmev = []
            for g in range(8):
                for k in range(NCH):
                    wb = self._wb = (self._wb + 1) % 2
                    evw = self.dma("sync", s("wsl")[:, wb, :, :],
                                   dr["stgW1"][par][k * 128:(k + 1) * 128, g * 512:(g + 1) * 512].rearrange("p (i q) -> p i q", q=128),
                                   waits=[self._wslab_free[wb], stg_ev[par] if (g == 0 and k == 0) else None], ch=f"wsl{wb}")
                    for i in range(4):
                        bw = [evw] + ([gtev[-1]] if k == 0 else []) + ([self._bank_free[i]] if k == 0 else [])
                        mmev.append(self.op("tensor",
                                            lambda E, i=i, k=k, wb=wb: E.matmul(pb[i][:, :], s("wsl")[:, wb, i, :], xe[:, k, :],
                                                                                start=(k == 0), stop=(k == NCH - 1)),
                                            waits=bw, track=(i == 3)))
                    self._wslab_free[wb] = mmev[-1]
                for i in range(4):
                    cc = g * 4 + i
                    self._bank_free[i] = self.op(
                        "scalar", lambda E, i=i, cc=cc, par=par: E.activation(h32[:, cc, :], pb[i][:, :], AF.Relu,
                                                                              bias=s("b1c")[:, par, cc:cc + 1]),
                        waits=[mmev[(g * NCH + NCH - 1) * 4 + 3], bias_ev[par], self._h_guard if cc == 0 else None])
            self._xe_guard = mmev[-1]
            mmev2 = []
            for g in range(2):
                for k in range(32):
                    wb = self._wb = (self._wb + 1) % 2
                    evw = self.dma("sync", s("wsl")[:, wb, :, :],
                                   dr["stgW2"][par][k * 128:(k + 1) * 128, g * 512:(g + 1) * 512].rearrange("p (i q) -> p i q", q=128),
                                   waits=[self._wslab_free[wb]], ch=f"wsl{wb}")
                    for i in range(4):
                        bw = [evw, self._bank_free[i] if k == 0 else None,
                              (self.an() if k == 0 else None)]
                        mmev2.append(self.op("tensor",
                                             lambda E, i=i, k=k, wb=wb: E.matmul(pb[i][:, :], s("wsl")[:, wb, i, :], h32[:, k, :],
                                                                                 start=(k == 0), stop=(k == 31)),
                                             waits=bw, track=(i == 3)))
                    self._wslab_free[wb] = mmev2[-1]
                for i in range(4):
                    cc = g * 4 + i
                    self._bank_free[i] = self.op("vector",
                                                 lambda E, i=i, cc=cc, par=par: E.tensor_scalar(s("y16")[:, cc, :], pb[i][:, :],
                                                                                                s("b2c")[:, par, cc:cc + 1], None, op0=ALU.add),
                                                 waits=[mmev2[(g * 32 + 31) * 4 + 3], bias_ev[par],
                                                        self._y16_guard if (g == 0 and i == 0) else None])
            self._h_guard = mmev2[-1]
            stg_free[par] = mmev2[-1]
            self._bias_guard[par] = self._bank_free[3]
            evebc = self.op("tensor", lambda E, c=c: E.matmul(pb[6][:, 0:1], s("ones_row")[:], s("ech")[0:1, c:c + 1], start=True, stop=True),
                            waits=[evecu, self._eb_guard], track=True)
            ylast_tp = None
            for sg in range(4):
                b2 = sg % 2
                for cc in range(NCH):
                    bank = pb[4 + (cc % 2)]
                    tp = self.op("tensor",
                                 lambda E, cc=cc, sg=sg, bank=bank: E.transpose(bank[:, 0:128], s("y16")[:, cc, sg * 128:(sg + 1) * 128], s("id128")[:]),
                                 waits=[self.vn() if cc == 0 else None, self._tp_guard[cc % 2]], track=True)
                    ev = self.op("vector",
                                 lambda E, cc=cc, b2=b2, bank=bank:
                                 E.tensor_copy(s("gat")[:, b2, cc * 128:(cc + 1) * 128], bank[:, 0:128]),
                                 waits=[tp, self._gat_guard[b2] if cc == 0 else None, ysc[b2] if cc == 0 else None])
                    self._tp_guard[cc % 2] = ev
                    ylast_tp = tp
                rb = s("rt")[:, sg, :]
                self.op("vector", lambda E, rb=rb: E.tensor_tensor(s("s81")[:, 0:1], rb[:, 2:3], pb[6][:, 0:1], op=ALU.is_equal),
                        waits=[evebc])
                self.op("vector", lambda E, rb=rb: E.tensor_tensor(s("s81")[:, 1:2], rb[:, 0:1], rb[:, 1:2], op=ALU.subtract))
                self.op("vector", lambda E: E.tensor_tensor(s("s81")[:, 2:3], s("s81")[:, 0:1], s("s81")[:, 1:2], op=ALU.mult))
                self.op("vector", lambda E, rb=rb: E.tensor_tensor(s("s81")[:, 2:3], s("s81")[:, 2:3], rb[:, 1:2], op=ALU.add))
                evgt = self.op("vector", lambda E, b2=b2: E.tensor_scalar(s("gat")[:, b2, 0:D], s("gat")[:, b2, 0:D],
                                                                          s("s81")[:, 2:3], None, op0=ALU.mult))
                self.op("vector", lambda E: E.tensor_scalar(s("s81")[:, 3:4], s("s81")[:, 0:1], -1.0, 1.0, op0=ALU.mult, op1=ALU.add))
                self.op("vector", lambda E, rb=rb: E.tensor_scalar(s("s81")[:, 4:5], rb[:, 3:4], 2.0, s("s81")[:, 3:4], op0=ALU.mult, op1=ALU.add))
                self.op("vector", lambda E, rb=rb: E.tensor_scalar(s("s81")[:, 5:6], rb[:, 3:4], 8191.5, None, op0=ALU.is_gt))
                self.op("vector", lambda E: E.tensor_scalar(s("s81")[:, 6:7], s("pid")[:], float(2 * T), None, op0=ALU.add))
                self.op("vector", lambda E: E.tensor_tensor(s("s81")[:, 6:7], s("s81")[:, 6:7], s("s81")[:, 4:5], op=ALU.subtract))
                self.op("vector", lambda E: E.scalar_tensor_tensor(s("s81")[:, 4:5], s("s81")[:, 5:6], None, s("s81")[:, 4:5], op0=ALU.mult, op1=ALU.add)) if False else None
                self.op("vector", lambda E: E.tensor_tensor(s("s81")[:, 6:7], s("s81")[:, 6:7], s("s81")[:, 5:6], op=ALU.mult))
                self.op("vector", lambda E: E.tensor_tensor(s("s81")[:, 4:5], s("s81")[:, 4:5], s("s81")[:, 6:7], op=ALU.add))
                evsx = self.op("vector", lambda E, b2=b2: E.tensor_copy(s("sidxu")[:, b2:b2 + 1], s("s81")[:, 4:5]),
                               waits=[ysc[b2]])
                prev_sc = ysc[0] if (ysc[0] or (0,))[-1] >= (ysc[1] or (0,))[-1] else ysc[1]
                ysc[b2] = self.dma("gpsimd", dr["buf"][:, :], s("gat")[:, b2, 0:D],
                                   waits=[evgt, evsx, prev_sc], ch="ysc",
                                   scatter_idx=s("sidxu")[:, b2:b2 + 1])
                self._gat_guard[b2] = ysc[b2]
                self._rt_guard[sg] = self.vn()
            self._eb_guard = self.vn()
            self._y16_guard = ylast_tp
        BU = dr["buf"][:, :].rearrange("(t two) d -> t two d", two=2)
        xst = [None] * NT
        for j in range(NT):
            buf = j % 2
            evx = self.load_x(j, buf, extra_waits=[self.vn(), xst[j - 2] if j >= 2 else None])
            for sg in range(4):
                t0 = j * TILE + sg * 128
                first = (j == 0 and sg == 0)
                evb0 = self.dma("sync", s("cb0")[:], BU[t0:t0 + 128, 0, :],
                                waits=[self.vn(), ysc[0] if first else None, ysc[1] if first else None,
                                       self._h_guard if first else None], ch="cb0")
                evb1_ = self.dma("sync", s("cb1")[:], BU[t0:t0 + 128, 1, :],
                                 waits=[self.vn(), ysc[0] if first else None, ysc[1] if first else None,
                                        self._h_guard if first else None], ch="cb1")
                evtg = self.dma("sync", s("cbt")[:], TR[t0:t0 + 128, 0:D], waits=[self.vn()], ch="cbt")
                self.op("vector", lambda E: E.tensor_tensor(s("cb0")[:], s("cb0")[:], s("cb1")[:], op=ALU.add),
                        waits=[evb0, evb1_])
                self.op("vector", lambda E: E.tensor_tensor(s("cb0")[:], s("cb0")[:], s("cbt")[:], op=ALU.add), waits=[evtg])
                self.op("vector", lambda E: E.reduce_sum(s("s81")[:, 0:1], s("cb0")[:], axis=AX.X))
                self.op("vector", lambda E: E.tensor_scalar(s("s81")[:, 0:1], s("s81")[:, 0:1], 1.0 / D, None, op0=ALU.mult))
                self.op("vector", lambda E: E.tensor_scalar(s("cb0")[:], s("cb0")[:], s("s81")[:, 0:1], None, op0=ALU.subtract))
                self.op("scalar", lambda E: E.activation(s("cb1")[:], s("cb0")[:], AF.Square, accum_out=s("s81")[:, 1:2]),
                        waits=[self.vn()])
                self.op("vector", lambda E: E.tensor_scalar(s("s81")[:, 1:2], s("s81")[:, 1:2], 1.0 / D, 1e-5, op0=ALU.mult, op1=ALU.add),
                        waits=[self.an()])
                self.op("scalar", lambda E: E.activation(s("s81")[:, 1:2], s("s81")[:, 1:2], AF.Sqrt), waits=[self.vn()])
                self.op("vector", lambda E: E.reciprocal(s("s81")[:, 2:3], s("s81")[:, 1:2]), waits=[self.an()])
                evz2 = self.op("vector", lambda E: E.tensor_scalar(s("cb0")[:], s("cb0")[:], s("s81")[:, 2:3], None, op0=ALU.mult))
                for cc in range(NCH):
                    bank = pb[4 + (cc % 2)]
                    tp = self.op("tensor", lambda E, cc=cc, bank=bank: E.transpose(bank[:, 0:128], s("cb0")[:, cc * 128:(cc + 1) * 128], s("id128")[:]),
                                 waits=[evz2, self._tp_guard[cc % 2]], track=True)
                    ev = self.op("vector", lambda E, cc=cc, bank=bank: E.tensor_scalar(s("sB")[:, 0:128], bank[:, 0:128],
                                                                                       s("gam_h")[:, cc:cc + 1], s("bet_h")[:, cc:cc + 1],
                                                                                       op0=ALU.mult, op1=ALU.add),
                                 waits=[tp, evx if cc == 0 else None])
                    self._tp_guard[cc % 2] = ev
                    self.op("vector", lambda E, cc=cc, sg=sg, buf=buf: E.tensor_tensor(s("xt")[:, buf, cc, sg * 128:(sg + 1) * 128],
                                                                                       s("xt")[:, buf, cc, sg * 128:(sg + 1) * 128],
                                                                                       s("sB")[:, 0:128], op=ALU.add))
            xst[j] = self.store_x(j, buf, waits=[self.vn()])
        self.xstore_ev = xst
        self._sq_guard = xst[-1]
        self._gl_guard = None

    def emit_l1attn(self):
        dr, s = self.dram, self.s
        pb = self.pb
        prev_store = self.xstore_ev
        xst = [None] * NT
        TPS = SEQ // TILE
        qkv = s("big").rearrange("p (c t) -> p c t", t=TILE)
        ktm = s("gat")[0:64, 0, 0:1024]
        vtm = s("gat")[0:64, 1, 0:1024]
        atm = s("attn_tm")
        ctxs = s("ctxs")
        BS = 64
        for q in range(NSEQ):
            evz = self.op("vector", lambda E: E.memset(s("kcum")[:, 0:8], 0.0))
            evz2 = self.op("vector", lambda E: E.memset(ctxs[:, :, :], 0.0), waits=[self._ctx_guard])
            for tj in range(TPS):
                j = q * TPS + tj
                buf = j % 2
                w = self.xprev(prev_store, j) + [self.vn(), self.an(), xst[j - 2] if j >= 2 else None]
                evx = self.load_x(j, buf, extra_waits=w)
                self._ln_pe_guard = self._att_guard
                ln_ev = self.emit_ln(s("xt")[:, buf, :, :], evx)
                self._ln_pe_guard = None
                mmev = []
                for g in range(12):
                    for k in range(NCH):
                        wb = self._wb = (self._wb + 1) % 2
                        evw = self.dma("sync", s("wsl")[:, wb, 0:2, :],
                                       dr["qkvT"][k * 128:(k + 1) * 128, g * 256:(g + 1) * 256].rearrange("p (i q) -> p i q", q=128),
                                       waits=[self._wslab_free[wb], self.wT_ev if (q == 0 and tj == 0 and g == 0 and k == 0) else None],
                                       ch=f"wsl{wb}")
                        for i in range(2):
                            bw = [evw] + (ln_ev if k == 0 else []) + ([self._bank_free[i]] if k == 0 else [])
                            mmev.append(self.op("tensor",
                                                lambda E, i=i, k=k, wb=wb, g=g:
                                                E.matmul(pb[g % 2][:, i * 256:(i + 1) * 256][:, 0:512] if False else pb[i][:, :],
                                                         s("wsl")[:, wb, i, :], s("tgt")[:, k, :],
                                                         start=(k == 0), stop=(k == NCH - 1)),
                                                waits=bw, track=(i == 1)))
                        self._wslab_free[wb] = mmev[-1]
                    self._tgt_guard = mmev[-1]
                    for i in range(2):
                        cc = g * 2 + i
                        fn = AF.Exp if 8 <= cc < 16 else AF.Copy
                        self._bank_free[i] = self.op(
                            "scalar", lambda E, i=i, cc=cc, fn=fn: E.activation(qkv[:, cc, :], pb[i][:, :], fn),
                            waits=[mmev[(g * NCH + NCH - 1) * 2 + 1], self._big_guard if cc == 0 else None])
                for cc in range(NCH):
                    for sb in range(4):
                        bank = pb[4 + (sb % 2)]
                        tp = self.op("tensor",
                                     lambda E, cc=cc, sb=sb, bank=bank:
                                     E.transpose(bank[:, 0:128], qkv[:, cc, sb * 128:(sb + 1) * 128], s("id128")[:]),
                                     waits=[self.vn() if (cc == 0 and sb == 0) else None, self._tp_guard[sb % 2],
                                            (self.an() if (cc == 0 and sb == 0) else None)], track=True)
                        self.op("vector", lambda E, bank=bank: E.tensor_copy(s("sB")[:, 0:128], bank[:, 0:128]), waits=[tp])
                        for hh in range(2):
                            sl = s("sB")[:, hh * 64:(hh + 1) * 64]
                            self.op("vector", lambda E, sl=sl: E.reduce_max(s("s81")[:, 0:1], sl, axis=AX.X))
                            self.op("vector", lambda E: E.tensor_scalar(s("s81")[:, 1:2], s("s81")[:, 0:1], -1.0, None, op0=ALU.mult))
                            self.op("scalar", lambda E, sl=sl: E.activation(sl, sl, AF.Exp, bias=s("s81")[:, 1:2]), waits=[self.vn()])
                            self.op("vector", lambda E, sl=sl: E.reduce_sum(s("s81")[:, 2:3], sl, axis=AX.X), waits=[self.an()])
                            self.op("vector", lambda E: E.reciprocal(s("s81")[:, 3:4], s("s81")[:, 2:3]))
                            self.op("vector", lambda E: E.tensor_scalar(s("s81")[:, 3:4], s("s81")[:, 3:4], 0.125, None, op0=ALU.mult))
                            self.op("vector", lambda E, sl=sl: E.tensor_scalar(sl, sl, s("s81")[:, 3:4], None, op0=ALU.mult))
                        tp2 = self.op("tensor",
                                      lambda E, bank=bank: E.transpose(bank[:, 0:128], s("sB")[:, 0:128], s("id128")[:]),
                                      waits=[self.vn()], track=True)
                        ev = self.op("vector", lambda E, cc=cc, sb=sb, bank=bank:
                                     E.tensor_copy(qkv[:, cc, sb * 128:(sb + 1) * 128], bank[:, 0:128]), waits=[tp2])
                        self._tp_guard[sb % 2] = ev
                evbs = None
                for cc in range(NCH):
                    evbs = self.op("vector", lambda E, cc=cc: E.tensor_reduce(
                        s("bs")[:, cc * 8:(cc + 1) * 8],
                        qkv[:, 8 + cc, :].rearrange("p (b u) -> p b u", u=BS), axis=AX.X, op=ALU.add))
                for j0 in range(TILE // BS):
                    glob_first = (tj == 0 and j0 == 0)
                    cpev = []
                    if not glob_first:
                        for cc in range(NCH):
                            bko = pb[2 + cc // 4]
                            co = (cc % 4) * 128
                            cpev.append(self.op("vector", lambda E, cc=cc, bko=bko, co=co:
                                                E.tensor_copy(ctxs[0:64, cc, 0:64], bko[0:64, co:co + 64]),
                                                waits=[self._sj_guard if cc == 0 else None]))
                            cpev.append(self.op("vector", lambda E, cc=cc, bko=bko, co=co:
                                                E.tensor_copy(ctxs[64:128, cc, 0:64], bko[64:128, co + 64:co + 128])))
                    for cc in range(NCH):
                        cpev.append(self.op("vector", lambda E, cc=cc: E.tensor_copy(ctxs[:, cc, 64:65], s("kcum")[:, cc:cc + 1]),
                                            waits=[evz, evz2] if glob_first else []))
                    ev_ctx = cpev[-1]
                    ktv = []
                    for cc in range(NCH):
                        bank = pb[4 + (cc % 2)]
                        tpk = self.op("tensor", lambda E, cc=cc, j0=j0, bank=bank:
                                      E.transpose(bank[0:64, 0:128], qkv[:, 8 + cc, j0 * BS:(j0 + 1) * BS], s("id128")[:]),
                                      waits=[self._tp_guard[cc % 2], evbs if cc == 0 else None], track=True)
                        ev = self.op("vector", lambda E, cc=cc, bank=bank: E.tensor_copy(ktm[:, cc * 128:(cc + 1) * 128], bank[0:64, 0:128]),
                                     waits=[tpk, self._ktm_guard if cc == 0 else None])
                        self._tp_guard[cc % 2] = ev
                        tpv = self.op("tensor", lambda E, cc=cc, j0=j0, bank=bank:
                                      E.transpose(bank[0:64, 0:128], qkv[:, 16 + cc, j0 * BS:(j0 + 1) * BS], s("id128")[:]),
                                      waits=[self._tp_guard[cc % 2]], track=True)
                        ev = self.op("vector", lambda E, cc=cc, bank=bank: E.tensor_copy(vtm[:, cc * 128:(cc + 1) * 128], bank[0:64, 0:128]),
                                     waits=[tpv])
                        self._tp_guard[cc % 2] = ev
                        ktv.append(ev)
                    sjev = []
                    for cc in range(NCH):
                        bko = pb[2 + cc // 4]
                        co = (cc % 4) * 128
                        sjev.append(self.op("tensor",
                                            lambda E, cc=cc, bko=bko, co=co, st_=(glob_first and cc % 4 == 0),
                                            sp_=(tj == TPS - 1 and j0 == TILE // BS - 1):
                                            E.matmul(bko[:, co:co + 128], ktm[:, cc * 128:(cc + 1) * 128],
                                                     vtm[:, cc * 128:(cc + 1) * 128],
                                                     start=st_, stop=sp_,
                                                     skip_group_check=True),
                                            waits=[ktv[cc], (self.vn() if cc == 0 else None)], track=(cc >= NCH - 2)))
                    self._sj_guard = sjev[-1]
                    for cc in range(NCH):
                        for hh in range(2):
                            base = hh * 64
                            bk = pb[6 + ((cc * 2 + hh) % 2)]
                            mm = self.op("tensor",
                                         lambda E, cc=cc, hh=hh, base=base, bk=bk, j0=j0:
                                         E.matmul(bk[0:64, 0:65], qkv[base:base + 64, cc, j0 * BS:(j0 + 1) * BS],
                                                  ctxs[base:base + 64, cc, 0:65], start=True, stop=True),
                                         waits=[ev_ctx, self._att_free[(cc * 2 + hh) % 2]], track=True)
                            self.op("vector", lambda E, bk=bk: E.tensor_scalar(s("s81")[:, 4:5][0:64, :], bk[0:64, 64:65], 0.001, None, op0=ALU.max),
                                    waits=[mm])
                            self.op("vector", lambda E: E.reciprocal(s("s81")[0:64, 5:6], s("s81")[0:64, 4:5]))
                            ev = self.op("vector", lambda E, cc=cc, hh=hh, bk=bk:
                                         E.tensor_scalar(atm[:, cc * 128 + hh * 64:cc * 128 + (hh + 1) * 64], bk[0:64, 0:64],
                                                         s("s81")[0:64, 5:6], None, op0=ALU.mult))
                            self._att_free[(cc * 2 + hh) % 2] = ev
                    for cc in range(NCH):
                        bank = pb[4 + (cc % 2)]
                        tp = self.op("tensor", lambda E, cc=cc, bank=bank:
                                     E.transpose(bank[:, 0:64], atm[:, cc * 128:(cc + 1) * 128], s("id128")[0:64, 0:64]),
                                     waits=[self._tp_guard[cc % 2], (self.vn() if cc == 0 else None)], track=True)
                        ev = self.op("vector", lambda E, cc=cc, j0=j0, bank=bank:
                                     E.tensor_copy(s("y16")[:, cc, j0 * BS:(j0 + 1) * BS], bank[:, 0:64]),
                                     waits=[tp, (self._y16res if (j0 == 0 and cc == 0) else None)])
                        self._tp_guard[cc % 2] = ev
                    self._ktm_guard = sjev[-1]
                    self.op("vector", lambda E, j0=j0: E.tensor_tensor(
                        s("kcum")[:, 0:8], s("kcum")[:, 0:8],
                        s("bs")[:, :].rearrange("p (c b) -> p c b", b=8)[:, :, j0], op=ALU.add),
                        waits=[ev_ctx])
                mmev2 = []
                for g in range(4):
                    for k in range(NCH):
                        wb = self._wb = (self._wb + 1) % 2
                        evw = self.dma("sync", s("wsl")[:, wb, 0:2, :],
                                       dr["projT"][k * 128:(k + 1) * 128, g * 256:(g + 1) * 256].rearrange("p (i q) -> p i q", q=128),
                                       waits=[self._wslab_free[wb]], ch=f"wsl{wb}")
                        for i in range(2):
                            bw = [evw, (self.vn() if k == 0 else None), (self._bank_free[i] if k == 0 else None)]
                            mmev2.append(self.op("tensor",
                                                 lambda E, i=i, k=k, wb=wb:
                                                 E.matmul(pb[i][:, :], s("wsl")[:, wb, i, :], s("y16")[:, k, :],
                                                          start=(k == 0), stop=(k == NCH - 1)),
                                                 waits=bw, track=(i == 1)))
                        self._wslab_free[wb] = mmev2[-1]
                    for i in range(2):
                        cc = g * 2 + i
                        self._bank_free[i] = self.op("vector",
                                                     lambda E, i=i, cc=cc: E.tensor_scalar(s("sB")[:], pb[i][:, :],
                                                                                           s("projb_c")[:, cc:cc + 1], None, op0=ALU.add),
                                                     waits=[mmev2[(g * NCH + NCH - 1) * 2 + 1], evx if cc == 0 else None])
                        self.op("vector", lambda E, cc=cc, buf=buf: E.tensor_tensor(s("xt")[:, buf, cc, :], s("xt")[:, buf, cc, :],
                                                                                    s("sB")[:], op=ALU.add))
                self._y16res = mmev2[-1]
                self._big_guard = mmev2[-1]
                self._att_guard = mmev2[-1]
                xst[j] = self.store_x(j, buf, waits=[self.vn()])
            self._ctx_guard = self._sj_guard
        self.xstore_ev = xst
        self._sq_guard = xst[-1]

    def emit_out_transpose(self):
        dr, s = self.dram, self.s
        prev = getattr(self, "xstore_ev", [None] * NT)
        laststore = None
        self._ot_st = []
        ot_evs = [None, None]
        ot_prev_evs = [None, None]
        for j in range(T // 128):
            buf = j % 2
            p = prev[(j * 128) // TILE]
            w = (list(p) if isinstance(p, list) else [p]) if p else []
            if j < 2:
                q = prev[NT - 2 + buf]
                w += (list(q) if isinstance(q, list) else [q]) if q else []
            if ot_evs[buf] is not None:
                w.append(ot_evs[buf])
            evl = self.dma("sync",
                           s("xt")[:, buf, :, 0:128],
                           dr["x_fm"][:, :].rearrange("(c p) t -> p c t", p=128)[:, :, j * 128:(j + 1) * 128],
                           waits=w, ch=f"xld{buf}")
            evs = []
            for c in range(NCH):
                bank = self.pb[4 + (c % 2)]
                bg = [evs[-2]] if len(evs) >= 2 else list(ot_prev_evs[c:c + 1])
                if j == 0 and c < 2:
                    bg.append(self._tp_guard[c % 2])
                evt = self.op("tensor",
                              lambda E, c=c, bank=bank, buf=buf:
                              E.transpose(bank[:, 0:128], s("xt")[:, buf, c, 0:128], s("id128")[:]),
                              waits=[evl] + bg, track=True)
                gw = [evt]
                if c == 0 and j >= 2:
                    gw.append(self._ot_st[j - 2])
                eve = self.op("vector",
                              lambda E, c=c, bank=bank, buf=buf:
                              E.tensor_copy(s("gat")[:, buf, c * 128:(c + 1) * 128],
                                            bank[:, 0:128]),
                              waits=gw, track=True)
                evs.append(eve)
            st = self.dma("gpsimd", dr["y_out"][j * 128:(j + 1) * 128, :],
                          s("gat")[:, buf, 0:D],
                          waits=[evs[-1], evs[-2]], ch=f"yout{buf}")
            self._ot_st.append(st)
            ot_evs[buf] = evs[-1]
            ot_prev_evs = [evs[-2], evs[-1]]
            laststore = st
        self.final_ev = laststore

    def build(self):
        self.declare()
        with self.es:
            self.alloc()
            self.emit_prep()
            self.emit_in_transpose()
            need_T = any(st in self.stages for st in ("l0attn", "l1ffn1", "l1ffn2", "l1attn"))
            if need_T:
                self.emit_pretranspose()
            for st in self.stages:
                if st == "l0ffn1" or st == "l0ffn2":
                    self.emit_l0ffn()
                elif st == "l0attn":
                    self.emit_l0attn()
                elif st == "l1ffn1":
                    self.emit_l1ffn(0)
                elif st == "l1ffn2":
                    self.emit_l1ffn(1)
                elif st == "l1attn":
                    self.emit_l1attn()
            self.emit_out_transpose()
            self.P.emit("gpsimd", lambda E: None, waits=[self.final_ev])
            block = self.es.enter_context(self.nc.Block())
            self.P.play(block)
        return self.nc


ALL_STAGES = ("l0ffn1", "l0attn", "l0ffn2", "l1ffn1", "l1attn", "l1ffn2")


def host_consts():
    c = {}
    c["c_id128"] = np.eye(128, dtype=np.float32)
    c["c_ones_col"] = np.ones((128, 1), np.float32)
    c["c_ones_row"] = np.ones((1, 128), np.float32)
    c["c_triu"] = np.triu(np.ones((128, 128), np.float32), 1)
    c["c_pid"] = np.arange(128, dtype=np.float32).reshape(128, 1)
    c["c_erow"] = np.tile(np.arange(8, dtype=np.float32), (128, 1))
    sel = np.zeros((128, 128), np.float32)
    selT = np.zeros((16, 1024), np.float32)
    for cc in range(8):
        for p in range(128):
            h = 2 * cc + p // 64
            sel[p, cc * 16 + h] = 1.0
            selT[h, cc * 128 + p] = 1.0
    c["c_sel"] = sel
    c["c_selT"] = selT
    c["c_crow"] = np.arange(NCHUNK, dtype=np.float32).reshape(1, NCHUNK)
    c["c_tid"] = (np.arange(128)[:, None] + 128 * np.arange(64)[None, :]).astype(np.float32)
    return c


def build_and_run(inputs, stages=ALL_STAGES):
    b = B(list(stages))
    nc = b.build()
    x = np.ascontiguousarray(inputs["L0_attn_x"].astype(np.float32))
    consts = host_consts()
    wnames = [k for k in inputs if k != "L0_attn_x"]
    in_maps = []
    for core in range(8):
        m = {"x_in": np.ascontiguousarray(x[core * NSEQ:(core + 1) * NSEQ].reshape(T, D))}
        for k in wnames:
            m[k] = np.ascontiguousarray(np.asarray(inputs[k], dtype=np.float32))
        m.update(consts)
        m = {k: v for k, v in m.items() if k in ("x_in",) or k in b.dram and k not in ("y_out",)}
        in_maps.append(m)
    res = run_bass_kernel_spmd(nc, in_maps, core_ids=list(range(8)))
    out = np.concatenate([r["y_out"].reshape(NSEQ, SEQ, D) for r in res.results], axis=0)
    return out


def kernel(**inputs):
    return build_and_run(inputs, ALL_STAGES)


# version 02
# speedup vs baseline: 1.0335x; 1.0335x over previous
import numpy as np
from contextlib import ExitStack

import concourse.bass as bass
import concourse.mybir as mybir
from concourse.bass_utils import run_bass_kernel_spmd
from concourse.alu_op_type import AluOpType as ALU

F32 = mybir.dt.float32
U32 = mybir.dt.uint32
AX = mybir.AxisListType
AF = mybir.ActivationFunctionType

D = 1024
NSEQ = 4
SEQ = 2048
T = NSEQ * SEQ
TILE = 512
NT = T // TILE
NCH = D // 128
ROWW = 1028
NCHUNK = 40
CSLOT = 512
NSLOT = NCHUNK * CSLOT

ENGINES = ("sync", "tensor", "vector", "scalar", "gpsimd")


class Sem:
    __slots__ = ("h", "n", "name")
    def __init__(self, h, name):
        self.h, self.n, self.name = h, 0, name
    def inc(self, k):
        self.n += k
        return self.n


class Prog:
    def __init__(self):
        self.q = {e: [] for e in ENGINES}
    def emit(self, eng, build, waits=(), inc=None):
        for (s, v) in waits:
            assert v <= s.n, f"wait on future event {s.name}:{v}>{s.n} eng={eng}"
        self.q[eng].append((tuple(waits), build, inc))
    def play(self, block):
        for eng in ENGINES:
            ops = self.q[eng]
            def body(E, ops=ops):
                for waits, build, inc in ops:
                    seen = {}
                    for s, v in waits:
                        if v > seen.get(s, -1):
                            seen[s] = v
                    for s, v in seen.items():
                        E.wait_ge(s.h, v)
                    ins = build(E)
                    if inc is not None:
                        s, k = inc
                        if ins is None:
                            E.sem_inc(s.h, k)
                        else:
                            ins.then_inc(s.h, k)
            getattr(block, eng)(body)


class B:

    def __init__(self, stages):
        self.stages = stages
        self.nc = bass.Bass("TRN2", target_bir_lowering=False)
        self.P = Prog()
        self.es = ExitStack()
        self.dram = {}
        self.sb = {}

    def mksem(self, name):
        return Sem(self.es.enter_context(self.nc.semaphore(name)), name)

    def dma(self, eng, out, in_, waits=(), slow=False, ch="misc", gather_idx=None, scatter_idx=None, scatter_bound=None):
        if ch not in self.dch:
            self.dch[ch] = self.mksem(f"ch_{ch}")
        s = self.dch[ch]
        v = s.inc(16)
        def build(E, o=out, i=in_, gi=gather_idx, si=scatter_idx, slow=slow, sb=scatter_bound):
            if callable(o):
                o = o()
            if callable(i):
                i = i()
            if gi is not None:
                return E.indirect_dma_start(out=o, out_offset=None, in_=i,
                                            in_offset=bass.IndirectOffsetOnAxis(ap=gi, axis=0))
            if si is not None:
                if sb is None:
                    return E.indirect_dma_start(out=o, out_offset=bass.IndirectOffsetOnAxis(ap=si, axis=0),
                                                in_=i, in_offset=None)
                return E.indirect_dma_start(out=o, out_offset=bass.IndirectOffsetOnAxis(ap=si, axis=0),
                                            in_=i, in_offset=None,
                                            bounds_check=sb, oob_is_err=False)
            if slow:
                return E.dma_start(out=o, in_=i, allow_slow_non_contiguous=True)
            return E.dma_start(out=o, in_=i)
        self.P.emit(eng, build, waits=[w for w in waits if w], inc=(s, 16))
        return (s, v)

    def op(self, eng, build, waits=(), track=False, serial=True):
        waits = [w for w in waits if w]
        if eng in ("vector", "scalar", "gpsimd"):
            s = self.csem[eng]
            if serial and s.n > 0:
                waits.append((s, s.n))
            v = s.inc(1)
            self.P.emit(eng, build, waits=waits, inc=(s, 1))
            return (s, v)
        if track:
            s = self.csem[eng]
            v = s.inc(1)
            self.P.emit(eng, build, waits=waits, inc=(s, 1))
            return (s, v)
        self.P.emit(eng, build, waits=waits)
        return None

    def declare(self):
        nc = self.nc
        dr = self.dram
        def di(name, shape):
            dr[name] = nc.dram_tensor(name, list(shape), F32, kind="ExternalInput").ap()
        di("x_in", (T, D))
        di("L0_attn_Wqkv", (D, D)); di("L0_attn_temp", (16, 1))
        di("L0_attn_Wout", (D, D)); di("L0_attn_bout", (D,))
        di("L0_ffn_W1", (16, D, 2048)); di("L0_ffn_b1", (16, 2048))
        di("L0_ffn_W2", (16, 2048, D)); di("L0_ffn_b2", (16, D))
        di("L1_attn_qkv_w", (3 * D, D)); di("L1_attn_proj_w", (D, D)); di("L1_attn_proj_b", (D,))
        di("L1_ffn_Wr", (8, D)); di("L1_ffn_br", (8,))
        di("L1_ffn_W1", (8, 4096, D)); di("L1_ffn_b1", (8, 4096))
        di("L1_ffn_W2", (8, D, 4096)); di("L1_ffn_b2", (8, D))
        di("L1_ffn_gamma", (D,)); di("L1_ffn_beta", (D,))
        di("c_id128", (128, 128))
        di("c_ones_col", (128, 1))
        di("c_ones_row", (1, 128))
        di("c_triu", (128, 128))
        di("c_pid", (128, 1))
        di("c_erow", (128, 8))
        di("c_sel", (128, 128))
        di("c_selT", (16, 1024))
        di("c_crow", (1, NCHUNK))
        di("c_tid", (128, 64))
        dr["y_out"] = nc.dram_tensor("y_out", [T, D], F32, kind="ExternalOutput").ap()
        def ds(name, shape, dt=F32):
            dr[name] = nc.dram_tensor(name, list(shape), dt, kind="Internal").ap()
        ds("x_fm", (D, T))
        ds("w_fm", (D, T))
        ds("wp_fm", (D, T))
        ds("tgt_rows", (T + 1, ROWW))
        ds("buf", (2 * T + 130, D))
        ds("list_hbm", (NSLOT, 1), U32)
        ds("WqkvT", (D, D)); ds("WoutT", (D, D)); ds("qkvT", (D, 3 * D)); ds("projT", (D, D))
        ds("WrT", (D, 8))
        ds("W1T", (8, D, 4096)); ds("W2T", (8, 4096, D))
        ds("stgW1", (2, D, 4096)); ds("stgW2", (2, 4096, D))
        ds("dbgL", (16, T)); ds("dbgS", (128, 8 * TILE))

    def alloc(self):
        nc, es = self.nc, self.es
        sb = self.sb
        def t(name, shape, dt=F32):
            sb[name] = es.enter_context(nc.sbuf_tensor(name, list(shape), dt))
            return sb[name]
        t("id128", (128, 128)); t("ones_col", (128, 1)); t("ones_row", (1, 128))
        t("triu", (128, 128)); t("pid", (128, 1)); t("erow", (128, 8))
        t("l0b1", (128, 16)); t("l0hb2", (128, 8))
        t("l1b1", (128, 32 * 8))
        t("l1hb2", (128, 8 * 8))
        t("gam_h", (128, 8)); t("bet_h", (128, 8))
        t("bout_c", (128, 8)); t("projb_c", (128, 8))
        t("temp16", (16, 1)); t("br8", (1, 8))
        t("xt", (128, 2, NCH, TILE))
        t("tgt", (128, NCH, TILE))
        t("big", (128, 32 * TILE))
        t("y16", (128, NCH, TILE))
        t("wsl", (128, 3, 4, 128))
        t("gat", (128, 8, ROWW))
        t("st1", (1, 5, TILE))
        t("sA", (128, TILE)); t("sB", (128, TILE))
        t("rt", (128, 4, 16))
        t("lg16", (128, 16))
        t("acc8", (128, 16)); t("inv8", (128, 16)); t("s81", (128, 8))
        t("lgtm", (128, 16))
        t("sel", (128, 128)); t("selT", (16, 1024))
        t("glist", (128, NSLOT // 128), U32)
        t("e8", (1, 8)); t("e8b", (1, 8)); t("e8c", (1, 8))
        t("zrow", (1, ROWW))
        t("wrs", (128, 8, 8)); t("brb", (128, 8)); t("goffb", (128, 8))
        t("ri0", (128, 64)); t("ri1", (128, 64))
        t("ps0", (128, 64)); t("ps1", (128, 64))
        t("m0", (128, 64)); t("m1", (128, 64)); t("mk", (128, 64))
        t("sc", (128, 64)); t("mm0", (128, 64))
        t("ech", (1, NCHUNK)); t("echu", (1, NCHUNK), U32); t("crow", (1, NCHUNK))
        t("ctid", (128, 64)); t("tidu", (128, 64), U32)
        t("pu0", (128, 64), U32); t("pu1", (128, 64), U32); t("sidxu", (128, 2), U32)
        t("b1c", (128, 2, 32)); t("b2c", (128, 2, 8))
        t("ctxs", (128, 8, 66)); t("bs", (128, 64)); t("kcum", (128, 8))
        sb["L16"] = sb["sA"][0:16, :]
        sb["Pi"] = sb["big"][0:16, 0:SEQ]
        sb["cb0"] = sb["big"][:, 0:1024]
        sb["cb1"] = sb["big"][:, 1024:2048]
        sb["cbt"] = sb["big"][:, 2048:3072]
        self.pb = [es.enter_context(nc.psum_tensor(f"pb{i}", [128, 512], F32)) for i in range(8)]
        self.dch = {}
        self.csem = {e: self.mksem(f"c_{e}") for e in ENGINES}
        self._bank_free = [None] * 4
        self._wslab_free = [None, None, None]
        self._wb = 0
        self._tgt_guard = None
        self._sq_guard = None
        self._lg_guard = None
        self._rt_guard = [None, None, None, None]
        self._tp_guard = [None, None]
        self._tp2_guard = [None, None]
        self._gat_guard = [None] * 8
        self._xe_guard = None
        self._eb_guard = None
        self._y16_guard = None
        self._h_guard = None
        self._ps_guard = None
        self._gl_guard = None
        self._creg = None
        self._stg_guard = [None, None]
        self._bias_guard = [None, None]
        self._ctx_guard = None
        self._sj_guard = None
        self._ktm_guard = None
        self._att_free = [None, None]
        self._att_guard = None
        self._big_guard = None
        self._y16res = None

    def s(self, name):
        return self.sb[name]

    def xprev(self, prev_store, j):
        def L(p):
            return (list(p) if isinstance(p, list) else [p]) if p else []
        w = L(prev_store[j])
        if j < 2:
            w += L(prev_store[NT - 2 + (j % 2)])
        return w

    def emit_prep(self):
        dr, s = self.dram, self.s
        ev = []
        for cn, tn in [("c_id128", "id128"), ("c_ones_col", "ones_col"), ("c_ones_row", "ones_row"),
                       ("c_triu", "triu"), ("c_pid", "pid"), ("c_erow", "erow"),
                       ("L0_attn_temp", "temp16"), ("c_sel", "sel"), ("c_selT", "selT"),
                       ("c_crow", "crow"), ("c_tid", "ctid")]:
            ev.append(self.dma("sync", s(tn)[:], dr[cn][:, :] if dr[cn].shape[1:] else dr[cn][:], ch="param"))
        b1v = dr["L0_ffn_b1"][1, :].rearrange("(c p) -> p c", p=128)
        ev.append(self.dma("sync", s("l0b1")[:], b1v, slow=True, ch="param"))
        b2v = dr["L0_ffn_b2"][1, :].rearrange("(c p) -> p c", p=128)
        ev.append(self.dma("sync", s("l0hb2")[:], b2v, slow=True, ch="param"))
        b1l = dr["L1_ffn_b1"][:, :].rearrange("e (c p) -> p (e c)", p=128)
        ev.append(self.dma("sync", s("l1b1")[:], b1l, slow=True, ch="param"))
        b2l = dr["L1_ffn_b2"][:, :].rearrange("e (c p) -> p (e c)", p=128)
        ev.append(self.dma("sync", s("l1hb2")[:], b2l, slow=True, ch="param"))
        gv = dr["L1_ffn_gamma"][:].rearrange("(c p) -> p c", p=128)
        bv = dr["L1_ffn_beta"][:].rearrange("(c p) -> p c", p=128)
        ev.append(self.dma("sync", s("gam_h")[:], gv, slow=True, ch="param"))
        ev.append(self.dma("sync", s("bet_h")[:], bv, slow=True, ch="param"))
        bo = dr["L0_attn_bout"][:].rearrange("(c p) -> p c", p=128)
        ev.append(self.dma("sync", s("bout_c")[:], bo, slow=True, ch="param"))
        pb = dr["L1_attn_proj_b"][:].rearrange("(c p) -> p c", p=128)
        ev.append(self.dma("sync", s("projb_c")[:], pb, slow=True, ch="param"))
        ev.append(self.dma("sync", s("br8")[:], dr["L1_ffn_br"][:].rearrange("(o e) -> o e", o=1), ch="param"))
        self.params_ev = ev[-1]
        last = ev[-1]
        e1 = self.op("scalar", lambda E: E.mul(s("gam_h")[:], s("gam_h")[:], 0.5), waits=[last], track=True)
        self.op("scalar", lambda E: E.mul(s("bet_h")[:], s("bet_h")[:], 0.5))
        self.op("scalar", lambda E: E.mul(s("l0hb2")[:], s("l0hb2")[:], 0.5))
        self.op("vector", lambda E: E.memset(s("zrow")[:], 0.0))
        ev0 = self.op("vector", lambda E: E.memset(s("zrow")[:, D + 3:D + 4], float(T)), track=True)
        self.zrow_ev = self.dma("gpsimd", dr["tgt_rows"][T:T + 1, :], s("zrow")[:], waits=[ev0], ch="zrow")

        self.prep_ev = self.op("scalar", lambda E: E.copy(s("sA")[0:1, 0:1], s("gam_h")[0:1, 0:1]),
                               waits=[e1], track=True)

    def emit_pretranspose(self):
        dr, s = self.dram, self.s
        jobs = [(dr["L0_attn_Wqkv"], dr["WqkvT"], D, D)]
        jobs.append((dr["L0_attn_Wout"], dr["WoutT"], D, D))
        jobs.append((dr["L1_attn_qkv_w"], dr["qkvT"], 3 * D, D))
        jobs.append((dr["L1_attn_proj_w"], dr["projT"], D, D))
        if "l1ffn1" in self.stages or "l1ffn2" in self.stages:
            jobs.append((dr["L1_ffn_Wr"], dr["WrT"], 8, D))
            for e in range(8):
                jobs.append((dr["L1_ffn_W1"][e], dr["W1T"][e], 4096, D))
                jobs.append((dr["L1_ffn_W2"][e], dr["W2T"][e], D, 4096))
        big, wsl = s("big"), s("wsl")
        pev = [self.prep_ev]
        lastw = None
        for (W, WT, FOUT, DIN) in jobs:
            nko = (FOUT + 127) // 128
            for ko in range(nko):
                rows = min(128, FOUT - ko * 128)
                slab_in = big[:, 0:DIN]
                evl = self.dma("sync", slab_in[0:rows, :] if rows < 128 else slab_in,
                               W[ko * 128:ko * 128 + rows, :], waits=pev + ([lastw] if lastw else []), ch="pt_in")
                outs = []
                for kd in range(DIN // 128):
                    bank = self.pb[kd % 2]
                    evt = self.op("tensor",
                                  lambda E, kd=kd, rows=rows, bank=bank, slab_in=slab_in:
                                  E.transpose(bank[:, 0:rows], slab_in[0:rows, kd * 128:(kd + 1) * 128], s("id128")[0:rows, 0:rows]),
                                  waits=[evl] + ([outs[-2]] if len(outs) >= 2 else []), track=True)
                    eve = self.op("scalar",
                                  lambda E, kd=kd, rows=rows, bank=bank:
                                  E.copy(s("y16")[:].rearrange("p c t -> p (c t)")[:, kd * 128:kd * 128 + rows], bank[:, 0:rows]),
                                  waits=[evt] + ([lastw] if (kd == 0 and lastw) else []), track=True)
                    outs.append(eve)
                for kd in range(DIN // 128):
                    lastw = self.dma("scalar", WT[kd * 128:(kd + 1) * 128, ko * 128:ko * 128 + rows],
                                     s("y16")[:].rearrange("p c t -> p (c t)")[:, kd * 128:kd * 128 + rows],
                                     waits=[outs[kd]] + ([lastw] if lastw else []), ch="pt_out")
            pev = [lastw]
        self.wT_ev = lastw if lastw else self.prep_ev
        self._sq_guard = self.wT_ev

    def emit_in_transpose(self):
        dr, s = self.dram, self.s
        xin = dr["x_in"]
        prev_store = [None, None]
        prev_evs = [None, None]
        for j in range(T // 128):
            buf = j % 2
            tm = s("gat")[:, buf, 0:D]
            w = [self.zrow_ev] if j == 0 else []
            if prev_store[buf] is not None:
                w.append(prev_store[buf])
            evl = self.dma("sync", tm, xin[j * 128:(j + 1) * 128, :], waits=w, ch=f"gat{buf}")
            evs = []
            for c in range(NCH):
                bank = self.pb[4 + (c % 2)]
                bg = [evs[-2]] if len(evs) >= 2 else list(prev_evs[c:c + 1])
                evt = self.op("tensor",
                              lambda E, tm=tm, c=c, bank=bank:
                              E.transpose(bank[:, 0:128], tm[:, c * 128:(c + 1) * 128], s("id128")[:]),
                              waits=[evl, self.params_ev] + bg, track=True)
                eve = self.op("vector",
                              lambda E, c=c, bank=bank, buf=buf:
                              E.tensor_copy(s("xt")[:, buf, c, 0:128], bank[:, 0:128]),
                              waits=[evt] + ([prev_store[buf]] if (c == 0 and prev_store[buf]) else []),
                              track=True)
                evs.append(eve)
            st = self.dma("scalar",
                          dr["x_fm"][:, :].rearrange("(c p) t -> p c t", p=128)[:, :, j * 128:(j + 1) * 128],
                          s("xt")[:, buf, :, 0:128], waits=[evs[-1], evs[-2]], ch=f"xst{buf}")
            prev_store[buf] = st
            prev_evs = [evs[-2], evs[-1]]
        self.xstore_ev = [[prev_store[0], prev_store[1]]] * NT

    def emit_ln(self, xtile, evx, banks=(4, 5, 6, 7), out=None, sq=None, track_out=True):
        s = self.s
        out = out if out is not None else s("tgt")
        sq = sq if sq is not None else s("y16")
        sA, sB = s("sA"), s("sB")
        tgt_guard = [self._tgt_guard] if getattr(self, "_tgt_guard", None) else []
        sq_guard = [self._sq_guard] if getattr(self, "_sq_guard", None) else []
        evq = []
        for c in range(NCH):
            evq.append(self.op("scalar", lambda E, c=c: E.activation(sq[:, c, :], xtile[:, c, :], AF.Square),
                               waits=([evx] + sq_guard) if c == 0 else [], track=True))
        ev_s = self.op("vector", lambda E: E.tensor_tensor(sA[:], xtile[:, 0, :], xtile[:, 1, :], op=ALU.add),
                       waits=[evx])
        for c in range(2, NCH):
            ev_s = self.op("vector", lambda E, c=c: E.tensor_tensor(sA[:], sA[:], xtile[:, c, :], op=ALU.add),
                           track=(c == NCH - 1))
        ev_q = self.op("vector", lambda E: E.tensor_tensor(sB[:], sq[:, 0, :], sq[:, 1, :], op=ALU.add),
                       waits=[evq[1]])
        for c in range(2, NCH):
            ev_q = self.op("vector", lambda E, c=c: E.tensor_tensor(sB[:], sB[:], sq[:, c, :], op=ALU.add),
                           waits=[evq[c]], track=(c == NCH - 1))
        b_sx, b_sq, b_r, b_mr = (self.pb[b] for b in banks)
        m1 = self.op("tensor", lambda E: E.matmul(b_sx[0:1, :], s("ones_col")[:], sA[:], start=True, stop=True),
                     waits=[ev_s, getattr(self, "_ln_pe_guard", None)], track=True)
        m2 = self.op("tensor", lambda E: E.matmul(b_sq[0:1, :], s("ones_col")[:], sB[:], start=True, stop=True),
                     waits=[ev_q], track=True)
        st1 = s("st1")
        mrow = st1[:, 0, :]; vrow = st1[:, 1, :]; rrow = st1[:, 2, :]; mrrow = st1[:, 3, :]
        self.op("vector", lambda E: E.tensor_scalar(mrow, b_sx[0:1, :], 1.0 / D, None, op0=ALU.mult), waits=[m1])
        self.op("vector", lambda E: E.tensor_scalar(vrow, b_sq[0:1, :], 1.0 / D, None, op0=ALU.mult), waits=[m2])
        self.op("vector", lambda E: E.tensor_tensor(s("st1")[:, 4, :], mrow, mrow, op=ALU.mult))
        self.op("vector", lambda E: E.tensor_tensor(vrow, vrow, s("st1")[:, 4, :], op=ALU.subtract))
        ev_v = self.op("vector", lambda E: E.tensor_scalar(vrow, vrow, 1e-5, None, op0=ALU.add), track=True)
        ev_sq = self.op("scalar", lambda E: E.activation(vrow, vrow, AF.Sqrt), waits=[ev_v], track=True)
        ev_r = self.op("vector", lambda E: E.reciprocal(rrow, vrow), waits=[ev_sq], track=True)
        ev_mr = self.op("vector", lambda E: E.tensor_tensor(mrrow, mrow, rrow, op=ALU.mult), track=True)
        mb1 = self.op("tensor", lambda E: E.matmul(b_r[:, :], s("ones_row")[:], rrow, start=True, stop=True),
                      waits=[ev_r], track=True)
        mb2 = self.op("tensor", lambda E: E.matmul(b_mr[:, :], s("ones_row")[:], mrrow, start=True, stop=True),
                      waits=[ev_mr], track=True)
        evs = []
        for c in range(NCH):
            self.op("vector", lambda E, c=c: E.tensor_tensor(out[:, c, :], xtile[:, c, :], b_r[:, :], op=ALU.mult),
                    waits=([mb1] + tgt_guard) if c == 0 else [])
            evs.append(self.op("vector", lambda E, c=c: E.tensor_tensor(out[:, c, :], out[:, c, :], b_mr[:, :], op=ALU.subtract),
                               waits=[mb2] if c == 0 else [], track=(track_out and c >= NCH - 2)))
        return [e for e in evs if e]

    def load_x(self, j, buf, extra_waits=()):
        dr, s = self.dram, self.s
        src = dr["x_fm"][:, :].rearrange("(c p) t -> p c t", p=128)[:, :, j * TILE:(j + 1) * TILE]
        return self.dma("sync", s("xt")[:, buf, :, :], src, waits=list(extra_waits), ch=f"xld{buf}")

    def store_x(self, j, buf, waits):
        dr, s = self.dram, self.s
        dst = dr["x_fm"][:, :].rearrange("(c p) t -> p c t", p=128)[:, :, j * TILE:(j + 1) * TILE]
        return self.dma("scalar", dst, s("xt")[:, buf, :, :], waits=waits, ch=f"xst{buf}")

    def emit_l0ffn(self):
        dr, s = self.dram, self.s
        W1 = dr["L0_ffn_W1"][1]
        W2 = dr["L0_ffn_W2"][1]
        h32 = s("big").rearrange("p (c t) -> p c t", t=TILE)
        prev_store = getattr(self, "xstore_ev", [None] * NT)
        xst = [None] * NT
        for j in range(NT):
            buf = j % 2
            w = self.xprev(prev_store, j)
            if j >= 2:
                w.append(xst[j - 2])
            evx = self.load_x(j, buf, extra_waits=w)
            ln_ev = self.emit_ln(s("xt")[:, buf, :, :], evx)
            hevac = [None] * 16
            for g in range(4):
                mmev = []
                for k in range(NCH):
                    wb = self._wb = (getattr(self, "_wb", 0) + 1) % 3
                    evw = self.dma("sync", s("wsl")[:, wb, :, :],
                                   W1[k * 128:(k + 1) * 128, g * 512:(g + 1) * 512].rearrange("p (i q) -> p i q", q=128),
                                   waits=[self._wslab_free[wb]], ch=f"wsl{wb}")
                    for i in range(4):
                        bw = [evw] + (ln_ev if k == 0 else []) + ([self._bank_free[i]] if k == 0 else [])
                        mmev.append(self.op("tensor",
                                            lambda E, i=i, k=k, wb=wb:
                                            E.matmul(self.pb[i][:, :], s("wsl")[:, wb, i, :], s("tgt")[:, k, :],
                                                     start=(k == 0), stop=(k == NCH - 1)),
                                            waits=bw, track=(i == 3)))
                    self._wslab_free[wb] = mmev[-1]
                if g == 3:
                    self._tgt_guard = mmev[-1]
                for i in range(4):
                    c = g * 4 + i
                    hevac[c] = self._bank_free[i] = self.op(
                        "scalar",
                        lambda E, i=i, c=c: E.activation(h32[:, c, :], self.pb[i][:, :], AF.Relu,
                                                         bias=s("l0b1")[:, c:c + 1]),
                        waits=[mmev[NCH * 4 - 1]], track=True)
            yev = []
            for g in range(2):
                mmev = []
                for k in range(16):
                    wb = self._wb = (self._wb + 1) % 3
                    evw = self.dma("sync", s("wsl")[:, wb, :, :],
                                   W2[k * 128:(k + 1) * 128, g * 512:(g + 1) * 512].rearrange("p (i q) -> p i q", q=128),
                                   waits=[self._wslab_free[wb]], ch=f"wsl{wb}")
                    for i in range(4):
                        bw = [evw, self._bank_free[i] if k == 0 else None, hevac[k]]
                        mmev.append(self.op("tensor",
                                            lambda E, i=i, k=k, wb=wb:
                                            E.matmul(self.pb[i][:, :], s("wsl")[:, wb, i, :], h32[:, k, :],
                                                     start=(k == 0), stop=(k == 15)),
                                            waits=bw, track=(i == 3)))
                    self._wslab_free[wb] = mmev[-1]
                for i in range(4):
                    c = g * 4 + i
                    self._bank_free[i] = self.op(
                        "vector",
                        lambda E, i=i, c=c: E.tensor_scalar(s("y16")[:, c, :], self.pb[i][:, :],
                                                            0.5, s("l0hb2")[:, c:c + 1],
                                                            op0=ALU.mult, op1=ALU.add),
                        waits=[mmev[63]], track=True)
                    yev.append(self._bank_free[i])
            evu = []
            for c in range(NCH):
                evu.append(self.op("vector",
                                   lambda E, c=c, buf=buf: E.tensor_tensor(s("xt")[:, buf, c, :], s("xt")[:, buf, c, :],
                                                                           s("y16")[:, c, :], op=ALU.add),
                                   waits=[yev[c]], track=(c >= NCH - 2)))
            self._sq_guard = evu[-1]
            xst[j] = self.store_x(j, buf, waits=[e for e in evu if e])
        self.xstore_ev = xst

    def vn(self):
        return (self.csem["vector"], self.csem["vector"].n)

    def an(self):
        return (self.csem["scalar"], self.csem["scalar"].n)

    def emit_l0attn(self):
        dr, s = self.dram, self.s
        prev_store = self.xstore_ev
        xst = [None] * NT
        w_fm = dr["w_fm"][:, :].rearrange("(c p) t -> p c t", p=128)
        wp_fm = dr["wp_fm"][:, :].rearrange("(c p) t -> p c t", p=128)
        TPS = SEQ // TILE
        pb = self.pb
        for q in range(NSEQ):
            ev_z = self.op("vector", lambda E: E.memset(s("acc8")[:], 0.0))
            wstall = [None] * TPS
            for tj in range(TPS):
                j = q * TPS + tj
                buf = j % 2
                evx = self.load_x(j, buf, extra_waits=self.xprev(prev_store, j) + [self.vn(), self.an(),
                                                                                    xst[j - 2] if j >= 2 else None])
                ln_ev = self.emit_ln(s("xt")[:, buf, :, :], evx)
                stage = s("y16")
                mmev = []
                for g in range(2):
                    for k in range(NCH):
                        wb = self._wb = (self._wb + 1) % 3
                        evw = self.dma("sync", s("wsl")[:, wb, :, :],
                                       dr["WqkvT"][k * 128:(k + 1) * 128, g * 512:(g + 1) * 512].rearrange("p (i q) -> p i q", q=128),
                                       waits=[self._wslab_free[wb],
                                              self.wT_ev if (q == 0 and tj == 0 and k == 0) else None], ch=f"wsl{wb}")
                        for i in range(4):
                            bw = [evw] + (ln_ev if k == 0 else []) + ([self._bank_free[i]] if k == 0 else [])
                            mmev.append(self.op("tensor",
                                                lambda E, i=i, k=k, wb=wb:
                                                E.matmul(pb[i][:, :], s("wsl")[:, wb, i, :], s("tgt")[:, k, :],
                                                         start=(k == 0), stop=(k == NCH - 1)),
                                                waits=bw, track=(i == 3)))
                        self._wslab_free[wb] = mmev[-1]
                    self._tgt_guard = mmev[-1]
                    for i in range(4):
                        c = g * 4 + i
                        self._bank_free[i] = self.op("vector",
                                                     lambda E, i=i, c=c, stage=stage: E.tensor_copy(stage[:, c, :], pb[i][:, :]),
                                                     waits=[mmev[(g * NCH + NCH - 1) * 4 + 3],
                                                            self._sq_guard if c == 0 else None])
                for c in range(NCH):
                    self.op("scalar", lambda E, c=c, stage=stage: E.activation(s("sB")[:], stage[:, c, :], AF.Square,
                                                                  accum_out=s("s81")[:, 0:1]),
                            waits=[self.vn(), ev_z if c == 0 else None])
                    self.op("vector", lambda E, c=c: E.tensor_tensor(s("acc8")[:, c:c + 1], s("acc8")[:, c:c + 1],
                                                                     s("s81")[:, 0:1], op=ALU.add),
                            waits=[self.an()])
                wstall[tj] = self.dma("scalar", w_fm[:, :, j * TILE:(j + 1) * TILE], stage[:, :, :],
                                      waits=[self.vn(), self.an()], ch=f"wst{tj}")
                self._sq_guard = wstall[tj]
            self.op("scalar", lambda E: E.activation(s("acc8")[:, 0:8], s("acc8")[:, 0:8], AF.Sqrt), waits=[self.vn()])
            self.op("vector", lambda E: E.tensor_scalar(s("acc8")[:, 0:8], s("acc8")[:, 0:8], 1e-12, None, op0=ALU.max),
                    waits=[self.an()])
            self.op("vector", lambda E: E.tensor_tensor(s("acc8")[:, 0:8], s("acc8")[:, 0:8], s("acc8")[:, 0:8], op=ALU.mult))
            ev_inv = self.op("vector", lambda E: E.reciprocal(s("inv8")[:, 0:8], s("acc8")[:, 0:8]))
            wpstall = [None] * TPS
            tpg = [None, None]
            for tj in range(TPS):
                j = q * TPS + tj
                buf = j % 2
                evw = self.dma("sync", s("xt")[:, buf, :, :], w_fm[:, :, j * TILE:(j + 1) * TILE],
                               waits=[wstall[tj], self.vn(), self.an()], ch=f"xld{buf}")
                wtile = s("xt")[:, buf, :, :]
                stage = s("y16")
                selev = None
                for c in range(NCH):
                    self.op("scalar", lambda E, c=c, wtile=wtile: E.activation(s("sB")[:], wtile[:, c, :], AF.Square),
                            waits=[evw, ev_inv, self.vn(), selev])
                    self.op("vector", lambda E, c=c: E.tensor_scalar(s("sB")[:], s("sB")[:], s("inv8")[:, c:c + 1], None, op0=ALU.mult),
                            waits=[self.an()])
                    selev = self.op("tensor",
                                    lambda E, c=c: E.matmul(pb[4][0:16, :], s("sel")[:, c * 16:(c + 1) * 16], s("sB")[:],
                                                            start=(c == 0), stop=(c == NCH - 1)),
                                    waits=[self.vn(), selev], track=True)
                ev_L = self.op("vector", lambda E: E.tensor_scalar(s("L16")[:, :], pb[4][0:16, :], s("temp16")[:, 0:1], None, op0=ALU.mult),
                               waits=[selev, getattr(self, "_dbg_ev", None)])
                if getattr(self, "debug", False):
                    self._dbg_ev = self.dma("gpsimd", dr["dbgL"][:, j * TILE:(j + 1) * TILE], s("L16")[:, :],
                                            waits=[ev_L], ch="dbg")
                for gq in range(4):
                    tp1 = self.op("tensor", lambda E, gq=gq: E.transpose(pb[5][:, 0:16], s("L16")[:, gq * 128:(gq + 1) * 128], s("id128")[0:16, 0:16]),
                                  waits=[ev_L, tpg[0]], track=True)
                    self.op("vector", lambda E: E.tensor_copy(s("lgtm")[:, 0:16], pb[5][:, 0:16]), waits=[tp1])
                    tpg[0] = self.vn()
                    self.op("vector", lambda E: E.reduce_max(s("s81")[:, 0:1], s("lgtm")[:, 0:16], axis=AX.X))
                    self.op("vector", lambda E: E.tensor_scalar(s("s81")[:, 1:2], s("s81")[:, 0:1], -1.0, None, op0=ALU.mult))
                    self.op("scalar", lambda E: E.activation(s("lgtm")[:, 0:16], s("lgtm")[:, 0:16], AF.Exp, bias=s("s81")[:, 1:2]),
                            waits=[self.vn()])
                    self.op("vector", lambda E: E.reduce_sum(s("s81")[:, 2:3], s("lgtm")[:, 0:16], axis=AX.X), waits=[self.an()])
                    self.op("vector", lambda E: E.reciprocal(s("s81")[:, 3:4], s("s81")[:, 2:3]))
                    evp = self.op("vector", lambda E: E.tensor_scalar(s("lgtm")[:, 0:16], s("lgtm")[:, 0:16], s("s81")[:, 3:4], None, op0=ALU.mult))
                    tp2 = self.op("tensor", lambda E: E.transpose(pb[6][0:16, 0:128], s("lgtm")[:, 0:16], s("id128")[:]),
                                  waits=[evp, tpg[1]], track=True)
                    tpg[1] = self.op("vector", lambda E, tj=tj, gq=gq: E.tensor_copy(s("Pi")[:, tj * TILE + gq * 128:tj * TILE + (gq + 1) * 128], pb[6][0:16, 0:128]),
                                     waits=[tp2])
                exv = None
                for c in range(NCH):
                    exv = self.op("tensor", lambda E, c=c, tj=tj: E.matmul(pb[7][:, :], s("selT")[:, c * 128:(c + 1) * 128],
                                                                           s("Pi")[:, tj * TILE:(tj + 1) * TILE], start=True, stop=True),
                                  waits=[self.vn(), exv], track=True)
                    self.op("vector", lambda E, c=c, stage=stage, wtile=wtile: E.tensor_tensor(stage[:, c, :], wtile[:, c, :], pb[7][:, :], op=ALU.mult),
                            waits=[exv, self._sq_guard if c == 0 else None])
                    self.op("vector", lambda E, c=c, stage=stage, wtile=wtile: E.tensor_tensor(s("sB")[:], stage[:, c, :], wtile[:, c, :], op=ALU.mult))
                    self.op("vector", lambda E, c=c: E.tensor_reduce(s("s81")[:, 4:5], s("sB")[:], axis=AX.X, op=ALU.add))
                    self.op("vector", lambda E, c=c: E.tensor_tensor(s("acc8")[:, 8 + c:9 + c], s("acc8")[:, 8 + c:9 + c],
                                                                     s("s81")[:, 4:5], op=ALU.add))
                wpstall[tj] = self.dma("scalar", wp_fm[:, :, j * TILE:(j + 1) * TILE], stage[:, :, :],
                                       waits=[self.vn()], ch=f"wst{tj}")
                self._sq_guard = wpstall[tj]
            self.op("vector", lambda E: E.reduce_sum(s("s81")[0:16, 5:6], s("Pi")[:, :], axis=AX.X))
            self.op("vector", lambda E: E.tensor_scalar(s("s81")[0:16, 5:6], s("s81")[0:16, 5:6], 1e-8, None, op0=ALU.add))
            ev_ps = self.op("vector", lambda E: E.reciprocal(s("s81")[0:16, 6:7], s("s81")[0:16, 5:6]))
            exv = None
            for c in range(NCH):
                exv = self.op("tensor", lambda E, c=c: E.matmul(pb[7][:, 0:1], s("selT")[:, c * 128:(c + 1) * 128],
                                                                s("s81")[0:16, 6:7], start=True, stop=True),
                              waits=[ev_ps, self.vn(), exv], track=True)
                self.op("vector", lambda E, c=c: E.tensor_tensor(s("inv8")[:, 8 + c:9 + c], s("acc8")[:, 8 + c:9 + c],
                                                                 pb[7][:, 0:1], op=ALU.mult), waits=[exv])
                self.op("vector", lambda E, c=c: E.tensor_scalar(s("inv8")[:, 8 + c:9 + c], s("inv8")[:, 8 + c:9 + c], 1.0, None, op0=ALU.add))
                self.op("vector", lambda E, c=c: E.reciprocal(s("inv8")[:, 8 + c:9 + c], s("inv8")[:, 8 + c:9 + c]))
            ev_att = self.op("vector", lambda E: E.tensor_scalar(s("inv8")[:, 8:16], s("inv8")[:, 8:16], -1.0, None, op0=ALU.mult))
            for tj in range(TPS):
                j = q * TPS + tj
                buf = j % 2
                evwp = self.dma("sync", s("y16")[:, :, :], wp_fm[:, :, j * TILE:(j + 1) * TILE],
                                waits=[wpstall[tj], self.vn()] + ([wpstall[2], wpstall[3]] if tj == 0 else []),
                                ch=f"wld0")
                evx = self.load_x(j, buf, extra_waits=self.xprev(prev_store, j) + [self.vn(), xst[j - 2] if j >= 2 else None])
                oev = None
                for c in range(NCH):
                    oev = self.op("vector", lambda E, c=c: E.tensor_scalar(s("tgt")[:, c, :], s("y16")[:, c, :],
                                                                           s("inv8")[:, 8 + c:9 + c], None, op0=ALU.mult),
                                  waits=[evwp, ev_att, self._tgt_guard] if c == 0 else [])
                mmev = []
                for g in range(2):
                    for k in range(NCH):
                        wb = self._wb = (self._wb + 1) % 3
                        evw = self.dma("sync", s("wsl")[:, wb, :, :],
                                       dr["WoutT"][k * 128:(k + 1) * 128, g * 512:(g + 1) * 512].rearrange("p (i q) -> p i q", q=128),
                                       waits=[self._wslab_free[wb]], ch=f"wsl{wb}")
                        for i in range(4):
                            bw = [evw] + ([oev] if k == 0 else []) + ([self._bank_free[i]] if k == 0 else [])
                            mmev.append(self.op("tensor",
                                                lambda E, i=i, k=k, wb=wb:
                                                E.matmul(pb[i][:, :], s("wsl")[:, wb, i, :], s("tgt")[:, k, :],
                                                         start=(k == 0), stop=(k == NCH - 1)),
                                                waits=bw, track=(i == 3)))
                        self._wslab_free[wb] = mmev[-1]
                    self._tgt_guard = mmev[-1]
                    for i in range(4):
                        c = g * 4 + i
                        self._bank_free[i] = self.op("vector",
                                                     lambda E, i=i, c=c: E.tensor_scalar(s("sB")[:], pb[i][:, :], s("bout_c")[:, c:c + 1], None, op0=ALU.add),
                                                     waits=[mmev[(g * NCH + NCH - 1) * 4 + 3], evx if c == 0 else None])
                        self.op("vector", lambda E, c=c, buf=buf: E.tensor_tensor(s("xt")[:, buf, c, :], s("xt")[:, buf, c, :],
                                                                                  s("sB")[:], op=ALU.add))
                xst[j] = self.store_x(j, buf, waits=[self.vn()])
        self.xstore_ev = xst
        self._sq_guard = xst[-1]

    def emit_l1_gathers(self, c, evgl, gstore):
        dr, s = self.dram, self.s
        if not hasattr(self, "_gath_ev"):
            self._gath_ev = [[None] * 4, [None] * 4]
        for sg in range(4):
            gb = 4 * (c % 2) + sg
            self._gath_ev[c % 2][sg] = self.dma(
                "gpsimd", s("gat")[:, gb, :], dr["tgt_rows"][:, :],
                waits=[evgl, self.zrow_ev, self._gat_guard[gb]] +
                      ([gstore[0], gstore[1]] if c == 0 else []),
                ch=f"gath{gb}", gather_idx=s("glist")[:, 4 * c + sg:4 * c + sg + 1])

    def emit_l1ffn(self, cidx):
        dr, s = self.dram, self.s
        pb = self.pb
        prev_store = self.xstore_ev
        TR = dr["tgt_rows"]
        tag = f"f{cidx}"
        evwr = self.dma("sync", s("wrs")[:], dr["WrT"][:, :].rearrange("(k p) e -> p k e", p=128),
                        waits=[self.wT_ev], ch="wrs", slow=True)
        evbr = self.op("tensor", lambda E: E.matmul(pb[3][:, 0:8], s("ones_row")[:], s("br8")[:, :], start=True, stop=True),
                       waits=[self.params_ev, self._bank_free[3], self.vn()], track=True)
        evbrb = self.op("vector", lambda E: E.tensor_copy(s("brb")[:], pb[3][:, 0:8]), waits=[evbr])
        self._bank_free[3] = evbrb
        gstore = [None, None]
        for j in range(NT):
            buf = j % 2
            w = self.xprev(prev_store, j) + [self.vn(), self.an()]
            evx = self.load_x(j, buf, extra_waits=w)
            ln_ev = self.emit_ln(s("xt")[:, buf, :, :], evx)
            lgev = None
            for sg in range(4):
                for k in range(NCH):
                    lgev = self.op("tensor",
                                   lambda E, sg=sg, k=k: E.matmul(pb[0][:, sg * 8:(sg + 1) * 8],
                                                                  s("tgt")[:, k, sg * 128:(sg + 1) * 128],
                                                                  s("wrs")[:, k, :],
                                                                  start=(sg == 0 and k == 0), stop=(sg == 3 and k == NCH - 1)),
                                   waits=[evwr, self._lg_guard if (sg == 0 and k == 0) else None,
                                          self._bank_free[0] if (sg == 0 and k == 0) else None] +
                                         (ln_ev if (sg == 0 and k == 0) else []),
                                   track=(k == NCH - 1))
            for sg in range(4):
                rb = s("rt")[:, sg, :]
                self.op("vector", lambda E, sg=sg: E.tensor_tensor(s("lg16")[:, 0:8], pb[0][:, sg * 8:(sg + 1) * 8], s("brb")[:], op=ALU.add),
                        waits=[lgev, self._rt_guard[sg]])
                self.op("vector", lambda E: E.reduce_max(s("s81")[:, 0:1], s("lg16")[:, 0:8], axis=AX.X))
                self.op("vector", lambda E: E.tensor_scalar(s("lg16")[:, 8:16], s("lg16")[:, 0:8], s("s81")[:, 0:1], None, op0=ALU.is_equal))
                self.op("vector", lambda E: E.tensor_tensor(s("sB")[:, 0:8], s("lg16")[:, 8:16], s("erow")[:], op=ALU.mult))
                self.op("vector", lambda E, rb=rb: E.reduce_sum(rb[:, 2:3], s("sB")[:, 0:8], axis=AX.X))
                self.op("vector", lambda E: E.scalar_tensor_tensor(s("lg16")[:, 8:16], s("lg16")[:, 8:16], -1e30, s("lg16")[:, 0:8], op0=ALU.mult, op1=ALU.add))
                self.op("vector", lambda E: E.reduce_max(s("s81")[:, 1:2], s("lg16")[:, 8:16], axis=AX.X))
                self.op("vector", lambda E: E.tensor_scalar(s("lg16")[:, 8:16], s("lg16")[:, 8:16], s("s81")[:, 1:2], None, op0=ALU.is_equal))
                self.op("vector", lambda E: E.tensor_tensor(s("sB")[:, 0:8], s("lg16")[:, 8:16], s("erow")[:], op=ALU.mult))
                self.op("vector", lambda E, j=j, sg=sg: E.reduce_sum(s("ri1")[:, 4 * j + sg:4 * j + sg + 1], s("sB")[:, 0:8], axis=AX.X))
                self.op("vector", lambda E: E.tensor_tensor(s("s81")[:, 2:3], s("s81")[:, 0:1], s("s81")[:, 1:2], op=ALU.subtract))
                self.op("scalar", lambda E, rb=rb: E.activation(rb[:, 0:1], s("s81")[:, 2:3], AF.Sigmoid), waits=[self.vn()])
                self.op("vector", lambda E: E.tensor_scalar(s("s81")[:, 3:4], s("s81")[:, 2:3], -1.0, None, op0=ALU.mult), waits=[self.an()])
                self.op("scalar", lambda E, rb=rb: E.activation(rb[:, 1:2], s("s81")[:, 3:4], AF.Sigmoid), waits=[self.vn()])
                self.op("vector", lambda E, rb=rb, j=j, sg=sg: E.tensor_scalar(rb[:, 3:4], s("pid")[:], float(128 * (4 * j + sg)), None, op0=ALU.add),
                        waits=[self.an()])
                self.op("vector", lambda E, rb=rb, j=j, sg=sg: E.tensor_copy(s("ri0")[:, 4 * j + sg:4 * j + sg + 1], rb[:, 2:3]))
                buf2 = sg % 2
                tev = []
                for c in range(NCH):
                    bank = pb[1 + (c % 2)]
                    tp = self.op("tensor",
                                 lambda E, c=c, sg=sg, bank=bank: E.transpose(bank[:, 0:128], s("tgt")[:, c, sg * 128:(sg + 1) * 128], s("id128")[:]),
                                 waits=[self._tp2_guard[c % 2], lgev,
                                        self._bank_free[1 + (c % 2)] if (j == 0 and sg == 0 and c < 2) else None],
                                 track=True)
                    ev = self.op("vector",
                                 lambda E, c=c, buf2=buf2, bank=bank:
                                 E.tensor_copy(s("gat")[:, buf2, c * 128:(c + 1) * 128], bank[:, 0:128]),
                                 waits=[tp, gstore[buf2] if c == 0 else None])
                    self._tp2_guard[c % 2] = ev
                    tev.append(ev)
                t0 = j * TILE + sg * 128
                self.dma("scalar", TR[t0:t0 + 128, 0:D], s("gat")[:, sg % 2, 0:D],
                         waits=[self.vn()], ch=f"trow{sg % 2}")
                g2 = self.dma("scalar", TR[t0:t0 + 128, D:D + 4], s("rt")[:, sg, 0:4],
                              waits=[self.vn()], ch=f"trow{sg % 2}")
                gstore[sg % 2] = g2
                self._rt_guard[sg] = g2
            self._tgt_guard = self._tp2_guard[1]
            self._lg_guard = self.vn()
        ev_pre0 = self.op("vector", lambda E: E.memset(s("sB")[:, 0:160], 8192.0), waits=[self._ps_guard])
        ev_pre1 = self.op("vector", lambda E: E.tensor_copy(s("glist")[:], s("sB")[:, 0:160]))
        ev_pre = self.dma("gpsimd", dr["list_hbm"][0:NSLOT // 2, :].rearrange("(g p) o -> p (g o)", p=128),
                          s("glist")[:, 0:80], waits=[ev_pre1], ch="lpre", slow=True)
        ev_pre = self.dma("gpsimd", dr["list_hbm"][NSLOT // 2:, :].rearrange("(g p) o -> p (g o)", p=128),
                          s("glist")[:, 80:160], waits=[ev_pre1, ev_pre], ch="lpre", slow=True)
        self.op("vector", lambda E: E.memset(s("ps0")[:], 0.0))
        self.op("vector", lambda E: E.memset(s("ps1")[:], 0.0))
        evz = self.op("vector", lambda E: E.memset(s("e8")[:, :], 0.0))
        for e in range(8):
            self.op("vector", lambda E, e=e: E.tensor_scalar(s("m0")[:], s("ri0")[:], float(e), None, op0=ALU.is_equal), waits=[evz])
            self.op("vector", lambda E, e=e: E.tensor_scalar(s("m1")[:], s("ri1")[:], float(e), None, op0=ALU.is_equal))
            self.op("vector", lambda E: E.tensor_tensor(s("mk")[:], s("m0")[:], s("m1")[:], op=ALU.add))
            self.op("vector", lambda E: E.tensor_tensor_scan(s("sc")[:], s("mk")[:], s("mk")[:], 0.0, op0=ALU.add, op1=ALU.bypass))
            evtot = self.op("vector", lambda E: E.reduce_sum(s("s81")[:, 0:1], s("mk")[:], axis=AX.X))
            evpf = self.op("tensor", lambda E: E.matmul(pb[3][:, 0:1], s("triu")[:], s("s81")[:, 0:1], start=True, stop=True),
                           waits=[evtot, self._bank_free[3]], track=True)
            evne = self.op("tensor", lambda E: E.matmul(pb[3][0:1, 1:2], s("ones_col")[:], s("s81")[:, 0:1], start=False, stop=True,
                                                        skip_group_check=True),
                           waits=[evtot], track=True)
            self.op("vector", lambda E: E.tensor_tensor(s("sc")[:], s("sc")[:], s("mk")[:], op=ALU.subtract), waits=[evpf])
            self.op("vector", lambda E: E.tensor_scalar(s("sc")[:], s("sc")[:], pb[3][:, 0:1], None, op0=ALU.add))
            self.op("vector", lambda E: E.tensor_tensor(s("mm0")[:], s("sc")[:], s("m0")[:], op=ALU.mult))
            self.op("vector", lambda E: E.tensor_tensor(s("ps0")[:], s("ps0")[:], s("mm0")[:], op=ALU.add))
            self.op("vector", lambda E: E.tensor_tensor(s("mm0")[:], s("sc")[:], s("m1")[:], op=ALU.mult))
            self.op("vector", lambda E: E.tensor_tensor(s("ps1")[:], s("ps1")[:], s("mm0")[:], op=ALU.add))
            evcp = self.op("vector", lambda E, e=e: E.tensor_copy(s("e8")[0:1, e:e + 1], pb[3][0:1, 1:2]), waits=[evne])
            self._bank_free[3] = evcp
        self.op("vector", lambda E: E.memset(s("e8b")[:, :], 0.0))
        for kk in range(1, 17):
            self.op("vector", lambda E, kk=kk: E.scalar_tensor_tensor(s("e8b")[:, :], s("e8")[:, :], float(512 * (kk - 1)),
                                                                      s("e8b")[:, :], op0=ALU.is_gt, op1=ALU.add))
        self.op("vector", lambda E: E.tensor_tensor_scan(s("e8c")[:, :], s("e8b")[:, :], s("e8b")[:, :], 0.0, op0=ALU.add, op1=ALU.bypass))
        self.op("vector", lambda E: E.tensor_tensor(s("e8")[:, :], s("e8c")[:, :], s("e8b")[:, :], op=ALU.subtract))
        evgo = self.op("vector", lambda E: E.tensor_scalar(s("e8")[:, :], s("e8")[:, :], 512.0, None, op0=ALU.mult))
        self.op("vector", lambda E: E.memset(s("ech")[:, :], 0.0))
        for e in range(8):
            self.op("vector", lambda E, e=e: E.scalar_tensor_tensor(s("ech")[:, :], s("crow")[:, :], s("e8c")[0:1, e:e + 1],
                                                                    s("ech")[:, :], op0=ALU.is_ge, op1=ALU.add))
        self.op("vector", lambda E: E.tensor_scalar(s("ech")[:, :], s("ech")[:, :], 7.0, None, op0=ALU.min))
        evecu = self.op("vector", lambda E: E.tensor_copy(s("echu")[:, :], s("ech")[:, :]))
        evgb = self.op("tensor", lambda E: E.matmul(pb[3][:, 2:10], s("ones_row")[:], s("e8")[0:1, :], start=False, stop=True,
                                                    skip_group_check=True),
                       waits=[evgo], track=True)
        evgbc = self.op("vector", lambda E: E.tensor_copy(s("goffb")[:], pb[3][:, 2:10]), waits=[evgb])
        self._bank_free[3] = evgbc
        for e in range(8):
            self.op("vector", lambda E, e=e: E.tensor_scalar(s("m0")[:], s("ri0")[:], float(e), None, op0=ALU.is_equal))
            self.op("vector", lambda E, e=e: E.scalar_tensor_tensor(s("ps0")[:], s("m0")[:], s("goffb")[:, e:e + 1], s("ps0")[:],
                                                                    op0=ALU.mult, op1=ALU.add))
            self.op("vector", lambda E, e=e: E.tensor_scalar(s("m1")[:], s("ri1")[:], float(e), None, op0=ALU.is_equal))
            self.op("vector", lambda E, e=e: E.scalar_tensor_tensor(s("ps1")[:], s("m1")[:], s("goffb")[:, e:e + 1], s("ps1")[:],
                                                                    op0=ALU.mult, op1=ALU.add))
        evq0 = self.op("vector", lambda E: E.tensor_copy(s("pu0")[:], s("ps0")[:]))
        evq1 = self.op("vector", lambda E: E.tensor_copy(s("pu1")[:], s("ps1")[:]))
        evtd = self.op("vector", lambda E: E.tensor_copy(s("tidu")[:], s("ctid")[:]), waits=[self.params_ev])
        scev = []
        for g in range(64):
            scev.append(self.dma("gpsimd", dr["list_hbm"][:, :], s("tidu")[:, g:g + 1],
                                 waits=[evq0, evtd, ev_pre], ch=f"lsc{g % 4}",
                                 scatter_idx=s("pu0")[:, g:g + 1]))
        for g in range(64):
            scev.append(self.dma("gpsimd", dr["list_hbm"][:, :], s("tidu")[:, g:g + 1],
                                 waits=[evq1, evtd, ev_pre], ch=f"lsc{g % 4}",
                                 scatter_idx=s("pu1")[:, g:g + 1]))
        evgl = self.dma("sync", s("glist")[:, 0:80],
                        dr["list_hbm"][0:NSLOT // 2, :].rearrange("(g p) o -> p (g o)", p=128),
                        waits=scev[-4:] + [ev_pre1], ch="glud", slow=True)
        evgl = self.dma("sync", s("glist")[:, 80:160],
                        dr["list_hbm"][NSLOT // 2:, :].rearrange("(g p) o -> p (g o)", p=128),
                        waits=scev[-4:] + [evgl], ch="glud", slow=True)
        self._ps_guard = evgl
        ysc = [None, None]
        h32 = s("big").rearrange("p (cc t) -> p cc t", t=TILE)
        xe = s("tgt")
        for pp in range(2):
            if f"stg{pp}" not in self.dch:
                self.dch[f"stg{pp}"] = self.mksem(f"ch_stg{pp}")
        stg_ev = [None, None]
        stg_free = [None, None]
        bias_ev = [None, None]
        def emit_stage(c, waits):
            par = c % 2
            stg = self.dch[f"stg{par}"]
            v = stg.inc(16 * 2)
            if f"bia{par}" not in self.dch:
                self.dch[f"bia{par}"] = self.mksem(f"ch_bia{par}")
            bch = self.dch[f"bia{par}"]
            bv = bch.inc(16 * 2)
            def build(E, c=c, par=par):
                if self._creg is None:
                    self._creg = E.alloc_register()
                E.reg_load(self._creg, s("echu")[0:1, c:c + 1])
                for e in range(8):
                    with E.If_eq(self._creg, e):
                        E.dma_start(out=dr["stgW1"][par], in_=dr["W1T"][e]).then_inc(stg.h, 16)
                        E.dma_start(out=dr["stgW2"][par], in_=dr["W2T"][e]).then_inc(stg.h, 16)
                        E.dma_start(out=s("b1c")[:, par, :], in_=dr["L1_ffn_b1"][e, :].rearrange("(cc p) -> p cc", p=128),
                                    allow_slow_non_contiguous=True).then_inc(bch.h, 16)
                        E.dma_start(out=s("b2c")[:, par, :], in_=dr["L1_ffn_b2"][e, :].rearrange("(cc p) -> p cc", p=128),
                                    allow_slow_non_contiguous=True).then_inc(bch.h, 16)
            self.P.emit("sync", build, waits=[w for w in waits if w])
            return (stg, v), (bch, bv)
        ev0, bv0 = emit_stage(0, [evecu, self._stg_guard[0]])
        stg_ev[0], bias_ev[0] = ev0, bv0
        for c in range(NCHUNK):
            par = c % 2
            if c + 1 < NCHUNK:
                stg_ev[1 - par], bias_ev[1 - par] = emit_stage(
                    c + 1, [evecu, stg_free[1 - par], self._stg_guard[1 - par] if c == 0 else None,
                            self._bias_guard[1 - par]])
            if c == 0:
                self.emit_l1_gathers(0, evgl, gstore)
            gtev = []
            for sg in range(4):
                gb = 4 * (c % 2) + sg
                evg = self._gath_ev[c % 2][sg]
                evrt = self.op("vector", lambda E, sg=sg, gb=gb: E.tensor_copy(s("rt")[:, sg, 0:4], s("gat")[:, gb, D:D + 4]),
                               waits=[evg, self._rt_guard[sg]])
                self._rt_guard[sg] = evrt
                for cc in range(NCH):
                    bank = pb[4 + (cc % 2)]
                    tp = self.op("tensor",
                                 lambda E, cc=cc, gb=gb, bank=bank:
                                 E.transpose(bank[:, 0:128], s("gat")[:, gb, cc * 128:(cc + 1) * 128], s("id128")[:]),
                                 waits=[evg, self._tp_guard[cc % 2], self._xe_guard if (cc == 0 and sg == 0) else None],
                                 track=True)
                    ev = self.op("vector",
                                 lambda E, cc=cc, sg=sg, bank=bank: E.tensor_copy(xe[:, cc, sg * 128:(sg + 1) * 128], bank[:, 0:128]),
                                 waits=[tp])
                    self._tp_guard[cc % 2] = ev
                    gtev.append(ev)
                self._gat_guard[gb] = gtev[-1]
            mmev = []
            for g in range(8):
                for k in range(NCH):
                    wb = self._wb = (self._wb + 1) % 3
                    evw = self.dma("sync", s("wsl")[:, wb, :, :],
                                   dr["stgW1"][par][k * 128:(k + 1) * 128, g * 512:(g + 1) * 512].rearrange("p (i q) -> p i q", q=128),
                                   waits=[self._wslab_free[wb], stg_ev[par] if (g == 0 and k == 0) else None], ch=f"wsl{wb}")
                    for i in range(4):
                        bw = [evw] + ([gtev[-1]] if k == 0 else []) + ([self._bank_free[i]] if k == 0 else [])
                        mmev.append(self.op("tensor",
                                            lambda E, i=i, k=k, wb=wb: E.matmul(pb[i][:, :], s("wsl")[:, wb, i, :], xe[:, k, :],
                                                                                start=(k == 0), stop=(k == NCH - 1)),
                                            waits=bw, track=(i == 3)))
                    self._wslab_free[wb] = mmev[-1]
                for i in range(4):
                    cc = g * 4 + i
                    self._bank_free[i] = self.op(
                        "scalar", lambda E, i=i, cc=cc, par=par: E.activation(h32[:, cc, :], pb[i][:, :], AF.Relu,
                                                                              bias=s("b1c")[:, par, cc:cc + 1]),
                        waits=[mmev[(g * NCH + NCH - 1) * 4 + 3], bias_ev[par], self._h_guard if cc == 0 else None,
                               ysc[0] if cc < 6 else None, ysc[1] if cc < 6 else None])
            self._xe_guard = mmev[-1]
            if c + 1 < NCHUNK:
                self.emit_l1_gathers(c + 1, evgl, gstore)
            mmev2 = []
            for g in range(2):
                for k in range(32):
                    wb = self._wb = (self._wb + 1) % 3
                    evw = self.dma("sync", s("wsl")[:, wb, :, :],
                                   dr["stgW2"][par][k * 128:(k + 1) * 128, g * 512:(g + 1) * 512].rearrange("p (i q) -> p i q", q=128),
                                   waits=[self._wslab_free[wb]], ch=f"wsl{wb}")
                    for i in range(4):
                        bw = [evw, self._bank_free[i] if k == 0 else None,
                              (self.an() if k == 0 else None)]
                        mmev2.append(self.op("tensor",
                                             lambda E, i=i, k=k, wb=wb: E.matmul(pb[i][:, :], s("wsl")[:, wb, i, :], h32[:, k, :],
                                                                                 start=(k == 0), stop=(k == 31)),
                                             waits=bw, track=(i == 3)))
                    self._wslab_free[wb] = mmev2[-1]
                for i in range(4):
                    cc = g * 4 + i
                    self._bank_free[i] = self.op("vector",
                                                 lambda E, i=i, cc=cc, par=par: E.tensor_scalar(s("y16")[:, cc, :], pb[i][:, :],
                                                                                                s("b2c")[:, par, cc:cc + 1], None, op0=ALU.add),
                                                 waits=[mmev2[(g * 32 + 31) * 4 + 3], bias_ev[par],
                                                        self._y16_guard if (g == 0 and i == 0) else None])
            self._h_guard = mmev2[-1]
            stg_free[par] = mmev2[-1]
            self._bias_guard[par] = self._bank_free[3]
            evebc = self.op("tensor", lambda E, c=c: E.matmul(pb[6][:, 0:1], s("ones_row")[:], s("ech")[0:1, c:c + 1], start=True, stop=True),
                            waits=[evecu, self._eb_guard], track=True)
            ylast_tp = None
            ybig = s("big")
            for sg in range(4):
                b2 = sg % 2
                yst = ybig[:, b2 * 1088:b2 * 1088 + 1088]
                for cc in range(NCH):
                    bank = pb[4 + (cc % 2)]
                    tp = self.op("tensor",
                                 lambda E, cc=cc, sg=sg, bank=bank: E.transpose(bank[:, 0:128], s("y16")[:, cc, sg * 128:(sg + 1) * 128], s("id128")[:]),
                                 waits=[self.vn() if cc == 0 else None, self._tp_guard[cc % 2],
                                        (self._h_guard if (sg == 0 and cc == 0) else None)], track=True)
                    ev = self.op("vector",
                                 lambda E, cc=cc, yst=yst, bank=bank:
                                 E.tensor_copy(yst[:, cc * 128:(cc + 1) * 128], bank[:, 0:128]),
                                 waits=[tp, ysc[b2] if cc == 0 else None])
                    self._tp_guard[cc % 2] = ev
                    ylast_tp = tp
                rb = s("rt")[:, sg, :]
                self.op("vector", lambda E, rb=rb: E.tensor_tensor(s("s81")[:, 0:1], rb[:, 2:3], pb[6][:, 0:1], op=ALU.is_equal),
                        waits=[evebc])
                self.op("vector", lambda E, rb=rb: E.tensor_tensor(s("s81")[:, 1:2], rb[:, 0:1], rb[:, 1:2], op=ALU.subtract))
                self.op("vector", lambda E: E.tensor_tensor(s("s81")[:, 2:3], s("s81")[:, 0:1], s("s81")[:, 1:2], op=ALU.mult))
                self.op("vector", lambda E, rb=rb: E.tensor_tensor(s("s81")[:, 2:3], s("s81")[:, 2:3], rb[:, 1:2], op=ALU.add))
                evgt = self.op("vector", lambda E, yst=yst: E.tensor_scalar(yst[:, 0:D], yst[:, 0:D],
                                                                            s("s81")[:, 2:3], None, op0=ALU.mult))
                self.op("vector", lambda E: E.tensor_scalar(s("s81")[:, 3:4], s("s81")[:, 0:1], -1.0, 1.0, op0=ALU.mult, op1=ALU.add))
                self.op("vector", lambda E, rb=rb: E.tensor_scalar(s("s81")[:, 4:5], rb[:, 3:4], 2.0, s("s81")[:, 3:4], op0=ALU.mult, op1=ALU.add))
                self.op("vector", lambda E, rb=rb: E.tensor_scalar(s("s81")[:, 5:6], rb[:, 3:4], 8191.5, None, op0=ALU.is_gt))
                self.op("vector", lambda E: E.tensor_scalar(s("s81")[:, 6:7], s("pid")[:], float(2 * T), None, op0=ALU.add))
                self.op("vector", lambda E: E.tensor_tensor(s("s81")[:, 6:7], s("s81")[:, 6:7], s("s81")[:, 4:5], op=ALU.subtract))
                self.op("vector", lambda E: E.tensor_tensor(s("s81")[:, 6:7], s("s81")[:, 6:7], s("s81")[:, 5:6], op=ALU.mult))
                self.op("vector", lambda E: E.tensor_tensor(s("s81")[:, 4:5], s("s81")[:, 4:5], s("s81")[:, 6:7], op=ALU.add))
                evsx = self.op("vector", lambda E, b2=b2: E.tensor_copy(s("sidxu")[:, b2:b2 + 1], s("s81")[:, 4:5]),
                               waits=[ysc[b2]])
                prev_sc = ysc[0] if (ysc[0] or (0,))[-1] >= (ysc[1] or (0,))[-1] else ysc[1]
                ysc[b2] = self.dma("gpsimd", dr["buf"][:, :], yst[:, 0:D],
                                   waits=[evgt, evsx, prev_sc], ch="ysc",
                                   scatter_idx=s("sidxu")[:, b2:b2 + 1])
                self._rt_guard[sg] = self.vn()
            self._eb_guard = self.vn()
            self._y16_guard = ylast_tp
        BU = dr["buf"][:, :].rearrange("(t two) d -> t two d", two=2)
        xst = [None] * NT
        for j in range(NT):
            buf = j % 2
            evx = self.load_x(j, buf, extra_waits=[self.vn(), xst[j - 2] if j >= 2 else None])
            for sg in range(4):
                t0 = j * TILE + sg * 128
                first = (j == 0 and sg == 0)
                evb0 = self.dma("sync", s("cb0")[:], BU[t0:t0 + 128, 0, :],
                                waits=[self.vn(), ysc[0] if first else None, ysc[1] if first else None,
                                       self._h_guard if first else None], ch="cb0")
                evb1_ = self.dma("sync", s("cb1")[:], BU[t0:t0 + 128, 1, :],
                                 waits=[self.vn(), ysc[0] if first else None, ysc[1] if first else None,
                                        self._h_guard if first else None], ch="cb1")
                evtg = self.dma("sync", s("cbt")[:], TR[t0:t0 + 128, 0:D], waits=[self.vn()], ch="cbt")
                self.op("vector", lambda E: E.tensor_tensor(s("cb0")[:], s("cb0")[:], s("cb1")[:], op=ALU.add),
                        waits=[evb0, evb1_])
                self.op("vector", lambda E: E.tensor_tensor(s("cb0")[:], s("cb0")[:], s("cbt")[:], op=ALU.add), waits=[evtg])
                self.op("vector", lambda E: E.reduce_sum(s("s81")[:, 0:1], s("cb0")[:], axis=AX.X))
                self.op("vector", lambda E: E.tensor_scalar(s("s81")[:, 0:1], s("s81")[:, 0:1], 1.0 / D, None, op0=ALU.mult))
                self.op("vector", lambda E: E.tensor_scalar(s("cb0")[:], s("cb0")[:], s("s81")[:, 0:1], None, op0=ALU.subtract))
                self.op("scalar", lambda E: E.activation(s("cb1")[:], s("cb0")[:], AF.Square, accum_out=s("s81")[:, 1:2]),
                        waits=[self.vn()])
                self.op("vector", lambda E: E.tensor_scalar(s("s81")[:, 1:2], s("s81")[:, 1:2], 1.0 / D, 1e-5, op0=ALU.mult, op1=ALU.add),
                        waits=[self.an()])
                self.op("scalar", lambda E: E.activation(s("s81")[:, 1:2], s("s81")[:, 1:2], AF.Sqrt), waits=[self.vn()])
                self.op("vector", lambda E: E.reciprocal(s("s81")[:, 2:3], s("s81")[:, 1:2]), waits=[self.an()])
                evz2 = self.op("vector", lambda E: E.tensor_scalar(s("cb0")[:], s("cb0")[:], s("s81")[:, 2:3], None, op0=ALU.mult))
                for cc in range(NCH):
                    bank = pb[4 + (cc % 2)]
                    tp = self.op("tensor", lambda E, cc=cc, bank=bank: E.transpose(bank[:, 0:128], s("cb0")[:, cc * 128:(cc + 1) * 128], s("id128")[:]),
                                 waits=[evz2, self._tp_guard[cc % 2]], track=True)
                    ev = self.op("vector", lambda E, cc=cc, bank=bank: E.tensor_scalar(s("sB")[:, 0:128], bank[:, 0:128],
                                                                                       s("gam_h")[:, cc:cc + 1], s("bet_h")[:, cc:cc + 1],
                                                                                       op0=ALU.mult, op1=ALU.add),
                                 waits=[tp, evx if cc == 0 else None])
                    self._tp_guard[cc % 2] = ev
                    self.op("vector", lambda E, cc=cc, sg=sg, buf=buf: E.tensor_tensor(s("xt")[:, buf, cc, sg * 128:(sg + 1) * 128],
                                                                                       s("xt")[:, buf, cc, sg * 128:(sg + 1) * 128],
                                                                                       s("sB")[:, 0:128], op=ALU.add))
            xst[j] = self.store_x(j, buf, waits=[self.vn()])
        self.xstore_ev = xst
        self._sq_guard = xst[-1]
        self._gl_guard = None

    def emit_l1attn(self):
        dr, s = self.dram, self.s
        pb = self.pb
        prev_store = self.xstore_ev
        xst = [None] * NT
        TPS = SEQ // TILE
        qkv = s("big").rearrange("p (c t) -> p c t", t=TILE)
        self.sb["attn_tm"] = self.sb["tgt"][0:64, 0:2, :].rearrange("p a b -> p (a b)")
        ktm = s("gat")[0:64, 0, 0:1024]
        vtm = s("gat")[0:64, 1, 0:1024]
        atm = s("attn_tm")
        ctxs = s("ctxs")
        BS = 64
        for q in range(NSEQ):
            evz = self.op("vector", lambda E: E.memset(s("kcum")[:, 0:8], 0.0))
            evz2 = self.op("vector", lambda E: E.memset(ctxs[:, :, :], 0.0), waits=[self._ctx_guard])
            for tj in range(TPS):
                j = q * TPS + tj
                buf = j % 2
                w = self.xprev(prev_store, j) + [self.vn(), self.an(), xst[j - 2] if j >= 2 else None]
                evx = self.load_x(j, buf, extra_waits=w)
                self._ln_pe_guard = self._att_guard
                ln_ev = self.emit_ln(s("xt")[:, buf, :, :], evx)
                self._ln_pe_guard = None
                mmev = []
                for g in range(12):
                    for k in range(NCH):
                        wb = self._wb = (self._wb + 1) % 3
                        evw = self.dma("sync", s("wsl")[:, wb, 0:2, :],
                                       dr["qkvT"][k * 128:(k + 1) * 128, g * 256:(g + 1) * 256].rearrange("p (i q) -> p i q", q=128),
                                       waits=[self._wslab_free[wb], self.wT_ev if (q == 0 and tj == 0 and g == 0 and k == 0) else None],
                                       ch=f"wsl{wb}")
                        for i in range(2):
                            bw = [evw] + (ln_ev if k == 0 else []) + ([self._bank_free[i]] if k == 0 else [])
                            mmev.append(self.op("tensor",
                                                lambda E, i=i, k=k, wb=wb, g=g:
                                                E.matmul(pb[g % 2][:, i * 256:(i + 1) * 256][:, 0:512] if False else pb[i][:, :],
                                                         s("wsl")[:, wb, i, :], s("tgt")[:, k, :],
                                                         start=(k == 0), stop=(k == NCH - 1)),
                                                waits=bw, track=(i == 1)))
                        self._wslab_free[wb] = mmev[-1]
                    self._tgt_guard = mmev[-1]
                    for i in range(2):
                        cc = g * 2 + i
                        fn = AF.Exp if 8 <= cc < 16 else AF.Copy
                        self._bank_free[i] = self.op(
                            "scalar", lambda E, i=i, cc=cc, fn=fn: E.activation(qkv[:, cc, :], pb[i][:, :], fn),
                            waits=[mmev[(g * NCH + NCH - 1) * 2 + 1], self._big_guard if cc == 0 else None])
                for cc in range(NCH):
                    for sb in range(4):
                        bank = pb[4 + (sb % 2)]
                        tp = self.op("tensor",
                                     lambda E, cc=cc, sb=sb, bank=bank:
                                     E.transpose(bank[:, 0:128], qkv[:, cc, sb * 128:(sb + 1) * 128], s("id128")[:]),
                                     waits=[self.vn() if (cc == 0 and sb == 0) else None, self._tp_guard[sb % 2],
                                            (self.an() if (cc == 0 and sb == 0) else None)], track=True)
                        self.op("vector", lambda E, bank=bank: E.tensor_copy(s("sB")[:, 0:128], bank[:, 0:128]), waits=[tp])
                        for hh in range(2):
                            sl = s("sB")[:, hh * 64:(hh + 1) * 64]
                            self.op("vector", lambda E, sl=sl: E.reduce_max(s("s81")[:, 0:1], sl, axis=AX.X))
                            self.op("vector", lambda E: E.tensor_scalar(s("s81")[:, 1:2], s("s81")[:, 0:1], -1.0, None, op0=ALU.mult))
                            self.op("scalar", lambda E, sl=sl: E.activation(sl, sl, AF.Exp, bias=s("s81")[:, 1:2]), waits=[self.vn()])
                            self.op("vector", lambda E, sl=sl: E.reduce_sum(s("s81")[:, 2:3], sl, axis=AX.X), waits=[self.an()])
                            self.op("vector", lambda E: E.reciprocal(s("s81")[:, 3:4], s("s81")[:, 2:3]))
                            self.op("vector", lambda E: E.tensor_scalar(s("s81")[:, 3:4], s("s81")[:, 3:4], 0.125, None, op0=ALU.mult))
                            self.op("vector", lambda E, sl=sl: E.tensor_scalar(sl, sl, s("s81")[:, 3:4], None, op0=ALU.mult))
                        tp2 = self.op("tensor",
                                      lambda E, bank=bank: E.transpose(bank[:, 0:128], s("sB")[:, 0:128], s("id128")[:]),
                                      waits=[self.vn()], track=True)
                        ev = self.op("vector", lambda E, cc=cc, sb=sb, bank=bank:
                                     E.tensor_copy(qkv[:, cc, sb * 128:(sb + 1) * 128], bank[:, 0:128]), waits=[tp2])
                        self._tp_guard[sb % 2] = ev
                evbs = None
                for cc in range(NCH):
                    evbs = self.op("vector", lambda E, cc=cc: E.tensor_reduce(
                        s("bs")[:, cc * 8:(cc + 1) * 8],
                        qkv[:, 8 + cc, :].rearrange("p (b u) -> p b u", u=BS), axis=AX.X, op=ALU.add))
                for j0 in range(TILE // BS):
                    glob_first = (tj == 0 and j0 == 0)
                    cpev = []
                    if not glob_first:
                        for cc in range(NCH):
                            bko = pb[2 + cc // 4]
                            co = (cc % 4) * 128
                            cpev.append(self.op("vector", lambda E, cc=cc, bko=bko, co=co:
                                                E.tensor_copy(ctxs[0:64, cc, 0:64], bko[0:64, co:co + 64]),
                                                waits=[self._sj_guard if cc == 0 else None]))
                            cpev.append(self.op("vector", lambda E, cc=cc, bko=bko, co=co:
                                                E.tensor_copy(ctxs[64:128, cc, 0:64], bko[64:128, co + 64:co + 128])))
                    for cc in range(NCH):
                        cpev.append(self.op("vector", lambda E, cc=cc: E.tensor_copy(ctxs[:, cc, 64:65], s("kcum")[:, cc:cc + 1]),
                                            waits=[evz, evz2] if glob_first else []))
                    ev_ctx = cpev[-1]
                    ktv = []
                    for cc in range(NCH):
                        bank = pb[4 + (cc % 2)]
                        tpk = self.op("tensor", lambda E, cc=cc, j0=j0, bank=bank:
                                      E.transpose(bank[0:64, 0:128], qkv[:, 8 + cc, j0 * BS:(j0 + 1) * BS], s("id128")[:]),
                                      waits=[self._tp_guard[cc % 2], evbs if cc == 0 else None], track=True)
                        ev = self.op("vector", lambda E, cc=cc, bank=bank: E.tensor_copy(ktm[:, cc * 128:(cc + 1) * 128], bank[0:64, 0:128]),
                                     waits=[tpk, self._ktm_guard if cc == 0 else None])
                        self._tp_guard[cc % 2] = ev
                        tpv = self.op("tensor", lambda E, cc=cc, j0=j0, bank=bank:
                                      E.transpose(bank[0:64, 0:128], qkv[:, 16 + cc, j0 * BS:(j0 + 1) * BS], s("id128")[:]),
                                      waits=[self._tp_guard[cc % 2]], track=True)
                        ev = self.op("vector", lambda E, cc=cc, bank=bank: E.tensor_copy(vtm[:, cc * 128:(cc + 1) * 128], bank[0:64, 0:128]),
                                     waits=[tpv])
                        self._tp_guard[cc % 2] = ev
                        ktv.append(ev)
                    sjev = []
                    for cc in range(NCH):
                        bko = pb[2 + cc // 4]
                        co = (cc % 4) * 128
                        sjev.append(self.op("tensor",
                                            lambda E, cc=cc, bko=bko, co=co, st_=(glob_first and cc % 4 == 0),
                                            sp_=(tj == TPS - 1 and j0 == TILE // BS - 1):
                                            E.matmul(bko[:, co:co + 128], ktm[:, cc * 128:(cc + 1) * 128],
                                                     vtm[:, cc * 128:(cc + 1) * 128],
                                                     start=st_, stop=sp_,
                                                     skip_group_check=True),
                                            waits=[ktv[cc], (self.vn() if cc == 0 else None)], track=(cc >= NCH - 2)))
                    self._sj_guard = sjev[-1]
                    for cc in range(NCH):
                        for hh in range(2):
                            base = hh * 64
                            bk = pb[6 + ((cc * 2 + hh) % 2)]
                            mm = self.op("tensor",
                                         lambda E, cc=cc, hh=hh, base=base, bk=bk, j0=j0:
                                         E.matmul(bk[0:64, 0:65], qkv[base:base + 64, cc, j0 * BS:(j0 + 1) * BS],
                                                  ctxs[base:base + 64, cc, 0:65], start=True, stop=True),
                                         waits=[ev_ctx, self._att_free[(cc * 2 + hh) % 2]], track=True)
                            self.op("vector", lambda E, bk=bk: E.tensor_scalar(s("s81")[:, 4:5][0:64, :], bk[0:64, 64:65], 0.001, None, op0=ALU.max),
                                    waits=[mm])
                            self.op("vector", lambda E: E.reciprocal(s("s81")[0:64, 5:6], s("s81")[0:64, 4:5]))
                            ev = self.op("vector", lambda E, cc=cc, hh=hh, bk=bk:
                                         E.tensor_scalar(atm[:, cc * 128 + hh * 64:cc * 128 + (hh + 1) * 64], bk[0:64, 0:64],
                                                         s("s81")[0:64, 5:6], None, op0=ALU.mult),
                                         waits=[self._tgt_guard if (j0 == 0 and cc == 0 and hh == 0) else None])
                            self._att_free[(cc * 2 + hh) % 2] = ev
                    for cc in range(NCH):
                        bank = pb[4 + (cc % 2)]
                        tp = self.op("tensor", lambda E, cc=cc, bank=bank:
                                     E.transpose(bank[:, 0:64], atm[:, cc * 128:(cc + 1) * 128], s("id128")[0:64, 0:64]),
                                     waits=[self._tp_guard[cc % 2], (self.vn() if cc == 0 else None)], track=True)
                        self._last_atp = tp
                        ev = self.op("vector", lambda E, cc=cc, j0=j0, bank=bank:
                                     E.tensor_copy(s("y16")[:, cc, j0 * BS:(j0 + 1) * BS], bank[:, 0:64]),
                                     waits=[tp, (self._y16res if (j0 == 0 and cc == 0) else None)])
                        self._tp_guard[cc % 2] = ev
                    self._ktm_guard = sjev[-1]
                    self.op("vector", lambda E, j0=j0: E.tensor_tensor(
                        s("kcum")[:, 0:8], s("kcum")[:, 0:8],
                        s("bs")[:, :].rearrange("p (c b) -> p c b", b=8)[:, :, j0], op=ALU.add),
                        waits=[ev_ctx])
                mmev2 = []
                for g in range(4):
                    for k in range(NCH):
                        wb = self._wb = (self._wb + 1) % 3
                        evw = self.dma("sync", s("wsl")[:, wb, 0:2, :],
                                       dr["projT"][k * 128:(k + 1) * 128, g * 256:(g + 1) * 256].rearrange("p (i q) -> p i q", q=128),
                                       waits=[self._wslab_free[wb]], ch=f"wsl{wb}")
                        for i in range(2):
                            bw = [evw, (self.vn() if k == 0 else None), (self._bank_free[i] if k == 0 else None)]
                            mmev2.append(self.op("tensor",
                                                 lambda E, i=i, k=k, wb=wb:
                                                 E.matmul(pb[i][:, :], s("wsl")[:, wb, i, :], s("y16")[:, k, :],
                                                          start=(k == 0), stop=(k == NCH - 1)),
                                                 waits=bw, track=(i == 1)))
                        self._wslab_free[wb] = mmev2[-1]
                    for i in range(2):
                        cc = g * 2 + i
                        self._bank_free[i] = self.op("vector",
                                                     lambda E, i=i, cc=cc: E.tensor_scalar(s("sB")[:], pb[i][:, :],
                                                                                           s("projb_c")[:, cc:cc + 1], None, op0=ALU.add),
                                                     waits=[mmev2[(g * NCH + NCH - 1) * 2 + 1], evx if cc == 0 else None])
                        self.op("vector", lambda E, cc=cc, buf=buf: E.tensor_tensor(s("xt")[:, buf, cc, :], s("xt")[:, buf, cc, :],
                                                                                    s("sB")[:], op=ALU.add))
                self._y16res = mmev2[-1]
                self._big_guard = mmev2[-1]
                self._att_guard = mmev2[-1]
                self._tgt_guard = self._last_atp
                xst[j] = self.store_x(j, buf, waits=[self.vn()])
            self._ctx_guard = self._sj_guard
        self.xstore_ev = xst
        self._sq_guard = xst[-1]

    def emit_out_transpose(self):
        dr, s = self.dram, self.s
        prev = getattr(self, "xstore_ev", [None] * NT)
        laststore = None
        self._ot_st = []
        ot_evs = [None, None]
        ot_prev_evs = [None, None]
        for j in range(T // 128):
            buf = j % 2
            p = prev[(j * 128) // TILE]
            w = (list(p) if isinstance(p, list) else [p]) if p else []
            if j < 2:
                q = prev[NT - 2 + buf]
                w += (list(q) if isinstance(q, list) else [q]) if q else []
            if ot_evs[buf] is not None:
                w.append(ot_evs[buf])
            evl = self.dma("sync",
                           s("xt")[:, buf, :, 0:128],
                           dr["x_fm"][:, :].rearrange("(c p) t -> p c t", p=128)[:, :, j * 128:(j + 1) * 128],
                           waits=w, ch=f"xld{buf}")
            evs = []
            for c in range(NCH):
                bank = self.pb[4 + (c % 2)]
                bg = [evs[-2]] if len(evs) >= 2 else list(ot_prev_evs[c:c + 1])
                if j == 0 and c < 2:
                    bg.append(self._tp_guard[c % 2])
                evt = self.op("tensor",
                              lambda E, c=c, bank=bank, buf=buf:
                              E.transpose(bank[:, 0:128], s("xt")[:, buf, c, 0:128], s("id128")[:]),
                              waits=[evl] + bg, track=True)
                gw = [evt]
                if c == 0 and j >= 2:
                    gw.append(self._ot_st[j - 2])
                eve = self.op("vector",
                              lambda E, c=c, bank=bank, buf=buf:
                              E.tensor_copy(s("gat")[:, buf, c * 128:(c + 1) * 128],
                                            bank[:, 0:128]),
                              waits=gw, track=True)
                evs.append(eve)
            st = self.dma("gpsimd", dr["y_out"][j * 128:(j + 1) * 128, :],
                          s("gat")[:, buf, 0:D],
                          waits=[evs[-1], evs[-2]], ch=f"yout{buf}")
            self._ot_st.append(st)
            ot_evs[buf] = evs[-1]
            ot_prev_evs = [evs[-2], evs[-1]]
            laststore = st
        self.final_ev = laststore

    def build(self):
        self.declare()
        with self.es:
            self.alloc()
            self.emit_prep()
            self.emit_in_transpose()
            need_T = any(st in self.stages for st in ("l0attn", "l1ffn1", "l1ffn2", "l1attn"))
            if need_T:
                self.emit_pretranspose()
            for st in self.stages:
                if st == "l0ffn1" or st == "l0ffn2":
                    self.emit_l0ffn()
                elif st == "l0attn":
                    self.emit_l0attn()
                elif st == "l1ffn1":
                    self.emit_l1ffn(0)
                elif st == "l1ffn2":
                    self.emit_l1ffn(1)
                elif st == "l1attn":
                    self.emit_l1attn()
            self.emit_out_transpose()
            self.P.emit("gpsimd", lambda E: None, waits=[self.final_ev])
            block = self.es.enter_context(self.nc.Block())
            self.P.play(block)
        return self.nc


ALL_STAGES = ("l0ffn1", "l0attn", "l0ffn2", "l1ffn1", "l1attn", "l1ffn2")


def host_consts():
    c = {}
    c["c_id128"] = np.eye(128, dtype=np.float32)
    c["c_ones_col"] = np.ones((128, 1), np.float32)
    c["c_ones_row"] = np.ones((1, 128), np.float32)
    c["c_triu"] = np.triu(np.ones((128, 128), np.float32), 1)
    c["c_pid"] = np.arange(128, dtype=np.float32).reshape(128, 1)
    c["c_erow"] = np.tile(np.arange(8, dtype=np.float32), (128, 1))
    sel = np.zeros((128, 128), np.float32)
    selT = np.zeros((16, 1024), np.float32)
    for cc in range(8):
        for p in range(128):
            h = 2 * cc + p // 64
            sel[p, cc * 16 + h] = 1.0
            selT[h, cc * 128 + p] = 1.0
    c["c_sel"] = sel
    c["c_selT"] = selT
    c["c_crow"] = np.arange(NCHUNK, dtype=np.float32).reshape(1, NCHUNK)
    c["c_tid"] = (np.arange(128)[:, None] + 128 * np.arange(64)[None, :]).astype(np.float32)
    return c


def build_and_run(inputs, stages=ALL_STAGES):
    b = B(list(stages))
    nc = b.build()
    x = np.ascontiguousarray(inputs["L0_attn_x"].astype(np.float32))
    consts = host_consts()
    wnames = [k for k in inputs if k != "L0_attn_x"]
    in_maps = []
    for core in range(8):
        m = {"x_in": np.ascontiguousarray(x[core * NSEQ:(core + 1) * NSEQ].reshape(T, D))}
        for k in wnames:
            m[k] = np.ascontiguousarray(np.asarray(inputs[k], dtype=np.float32))
        m.update(consts)
        m = {k: v for k, v in m.items() if k in ("x_in",) or k in b.dram and k not in ("y_out",)}
        in_maps.append(m)
    res = run_bass_kernel_spmd(nc, in_maps, core_ids=list(range(8)))
    out = np.concatenate([r["y_out"].reshape(NSEQ, SEQ, D) for r in res.results], axis=0)
    return out


def kernel(**inputs):
    return build_and_run(inputs, ALL_STAGES)


# version 05
# speedup vs baseline: 1.1758x; 1.1377x over previous
import numpy as np
from contextlib import ExitStack

import concourse.bass as bass
import concourse.mybir as mybir
from concourse.bass_utils import run_bass_kernel_spmd
from concourse.alu_op_type import AluOpType as ALU

F32 = mybir.dt.float32
U32 = mybir.dt.uint32
AX = mybir.AxisListType
AF = mybir.ActivationFunctionType

D = 1024
NSEQ = 4
SEQ = 2048
T = NSEQ * SEQ
TILE = 512
NT = T // TILE
NCH = D // 128
ROWW = 1028
NCHUNK = 40
CSLOT = 512
NSLOT = NCHUNK * CSLOT

ENGINES = ("sync", "tensor", "vector", "scalar", "gpsimd")


class Sem:
    __slots__ = ("h", "n", "name")
    def __init__(self, h, name):
        self.h, self.n, self.name = h, 0, name
    def inc(self, k):
        self.n += k
        return self.n


class Prog:
    def __init__(self):
        self.q = {e: [] for e in ENGINES}
    def emit(self, eng, build, waits=(), inc=None):
        for (s, v) in waits:
            assert v <= s.n, f"wait on future event {s.name}:{v}>{s.n} eng={eng}"
        self.q[eng].append((tuple(waits), build, inc))
    def play(self, block):
        for eng in ENGINES:
            ops = self.q[eng]
            def body(E, ops=ops):
                for waits, build, inc in ops:
                    seen = {}
                    for s, v in waits:
                        if v > seen.get(s, -1):
                            seen[s] = v
                    for s, v in seen.items():
                        E.wait_ge(s.h, v)
                    ins = build(E)
                    if inc is not None:
                        s, k = inc
                        if ins is None:
                            E.sem_inc(s.h, k)
                        else:
                            ins.then_inc(s.h, k)
            getattr(block, eng)(body)


class B:

    def __init__(self, stages):
        self.stages = stages
        self.nc = bass.Bass("TRN2", target_bir_lowering=False)
        self.P = Prog()
        self.es = ExitStack()
        self.dram = {}
        self.sb = {}

    def mksem(self, name):
        return Sem(self.es.enter_context(self.nc.semaphore(name)), name)

    def dma(self, eng, out, in_, waits=(), slow=False, ch="misc", gather_idx=None, scatter_idx=None, scatter_bound=None):
        if ch not in self.dch:
            self.dch[ch] = self.mksem(f"ch_{ch}")
        s = self.dch[ch]
        v = s.inc(16)
        def build(E, o=out, i=in_, gi=gather_idx, si=scatter_idx, slow=slow, sb=scatter_bound):
            if callable(o):
                o = o()
            if callable(i):
                i = i()
            if gi is not None:
                return E.indirect_dma_start(out=o, out_offset=None, in_=i,
                                            in_offset=bass.IndirectOffsetOnAxis(ap=gi, axis=0))
            if si is not None:
                if sb is None:
                    return E.indirect_dma_start(out=o, out_offset=bass.IndirectOffsetOnAxis(ap=si, axis=0),
                                                in_=i, in_offset=None)
                return E.indirect_dma_start(out=o, out_offset=bass.IndirectOffsetOnAxis(ap=si, axis=0),
                                            in_=i, in_offset=None,
                                            bounds_check=sb, oob_is_err=False)
            if slow:
                return E.dma_start(out=o, in_=i, allow_slow_non_contiguous=True)
            return E.dma_start(out=o, in_=i)
        self.P.emit(eng, build, waits=[w for w in waits if w], inc=(s, 16))
        return (s, v)

    def op(self, eng, build, waits=(), track=False, serial=True):
        waits = [w for w in waits if w]
        if eng in ("vector", "scalar", "gpsimd"):
            s = self.csem[eng]
            if serial and s.n > 0:
                waits.append((s, s.n))
            v = s.inc(1)
            self.P.emit(eng, build, waits=waits, inc=(s, 1))
            return (s, v)
        if track:
            s = self.csem[eng]
            v = s.inc(1)
            self.P.emit(eng, build, waits=waits, inc=(s, 1))
            return (s, v)
        self.P.emit(eng, build, waits=waits)
        return None

    def declare(self):
        nc = self.nc
        dr = self.dram
        def di(name, shape):
            dr[name] = nc.dram_tensor(name, list(shape), F32, kind="ExternalInput").ap()
        di("x_in", (T, D))
        di("L0_attn_Wqkv", (D, D)); di("L0_attn_temp", (16, 1))
        di("L0_attn_Wout", (D, D)); di("L0_attn_bout", (D,))
        di("L0_ffn_W1", (16, D, 2048)); di("L0_ffn_b1", (16, 2048))
        di("L0_ffn_W2", (16, 2048, D)); di("L0_ffn_b2", (16, D))
        di("L1_attn_qkv_w", (3 * D, D)); di("L1_attn_proj_w", (D, D)); di("L1_attn_proj_b", (D,))
        di("L1_ffn_Wr", (8, D)); di("L1_ffn_br", (8,))
        di("L1_ffn_W1", (8, 4096, D)); di("L1_ffn_b1", (8, 4096))
        di("L1_ffn_W2", (8, D, 4096)); di("L1_ffn_b2", (8, D))
        di("L1_ffn_gamma", (D,)); di("L1_ffn_beta", (D,))
        di("c_id128", (128, 128))
        di("c_ones_col", (128, 1))
        di("c_ones_row", (1, 128))
        di("c_triu", (128, 128))
        di("c_pid", (128, 1))
        di("c_erow", (128, 8))
        di("c_sel", (128, 128))
        di("c_selT", (16, 1024))
        di("c_crow", (1, NCHUNK))
        di("c_tid", (128, 64))
        dr["y_out"] = nc.dram_tensor("y_out", [T, D], F32, kind="ExternalOutput").ap()
        def ds(name, shape, dt=F32):
            dr[name] = nc.dram_tensor(name, list(shape), dt, kind="Internal").ap()
        ds("x_fm", (D, T))
        ds("w_fm", (D, T))
        ds("wp_fm", (D, T))
        ds("tgt_rows", (T + 1, ROWW))
        ds("buf", (2 * T + 130, D))
        ds("list_hbm", (NSLOT, 1), U32)
        ds("WqkvT", (D, D)); ds("WoutT", (D, D)); ds("qkvT", (D, 3 * D)); ds("projT", (D, D))
        ds("WrT", (D, 8))
        ds("W1T", (8, D, 4096)); ds("W2T", (8, 4096, D))
        ds("stgW1", (2, D, 4096)); ds("stgW2", (2, 4096, D))
        ds("dbgL", (16, T)); ds("dbgS", (128, 8 * TILE))

    def alloc(self):
        nc, es = self.nc, self.es
        sb = self.sb
        def t(name, shape, dt=F32):
            sb[name] = es.enter_context(nc.sbuf_tensor(name, list(shape), dt))
            return sb[name]
        t("id128", (128, 128)); t("ones_col", (128, 1)); t("ones_row", (1, 128))
        t("triu", (128, 128)); t("pid", (128, 1)); t("erow", (128, 8))
        t("l0b1", (128, 16)); t("l0hb2", (128, 8))
        t("l1b1", (128, 32 * 8))
        t("l1hb2", (128, 8 * 8))
        t("gam_h", (128, 8)); t("bet_h", (128, 8))
        t("bout_c", (128, 8)); t("projb_c", (128, 8))
        t("temp16", (16, 1)); t("br8", (1, 8))
        t("xt", (128, 2, NCH, TILE))
        t("tgt", (128, NCH, TILE))
        t("big", (128, 32 * TILE))
        t("y16", (128, NCH, TILE))
        t("wsl", (128, 3, 4, 128))
        t("gat", (128, 8, ROWW))
        t("st1", (1, 5, TILE))
        t("sA", (128, TILE)); t("sB", (128, TILE))
        t("rt", (128, 4, 16))
        t("lg16", (128, 16))
        t("acc8", (128, 16)); t("inv8", (128, 16)); t("s81", (128, 8))
        t("lgtm", (128, 16))
        t("sel", (128, 128)); t("selT", (16, 1024))
        t("glist", (128, NSLOT // 128), U32)
        t("e8", (1, 8)); t("e8b", (1, 8)); t("e8c", (1, 8))
        t("zrow", (1, ROWW))
        t("wrs", (128, 8, 8)); t("brb", (128, 8)); t("goffb", (128, 8))
        t("ri0", (128, 64)); t("ri1", (128, 64))
        t("ps0", (128, 64)); t("ps1", (128, 64))
        t("m0", (128, 64)); t("m1", (128, 64)); t("mk", (128, 64))
        t("sc", (128, 64)); t("mm0", (128, 64))
        t("ech", (1, NCHUNK)); t("echu", (1, NCHUNK), U32); t("crow", (1, NCHUNK))
        t("ctid", (128, 64)); t("tidu", (128, 64), U32)
        t("pu0", (128, 64), U32); t("pu1", (128, 64), U32); t("sidxu", (128, 2), U32)
        t("b1c", (128, 2, 32)); t("b2c", (128, 2, 8))
        t("ctxs", (128, 8, 66)); t("bs", (128, 64)); t("kcum", (128, 8))
        sb["L16"] = sb["sA"][0:16, :]
        sb["Pi"] = sb["big"][0:16, 0:SEQ]
        sb["cb0"] = sb["big"][:, 0:1024]
        sb["cb1"] = sb["big"][:, 1024:2048]
        sb["cbt"] = sb["big"][:, 2048:3072]
        self.pb = [es.enter_context(nc.psum_tensor(f"pb{i}", [128, 512], F32)) for i in range(8)]
        self.dch = {}
        self.csem = {e: self.mksem(f"c_{e}") for e in ENGINES}
        self._bank_free = [None] * 4
        self._wslab_free = [None, None, None]
        self._wb = 0
        self._tgt_guard = None
        self._sq_guard = None
        self._lg_guard = None
        self._rt_guard = [None, None, None, None]
        self._tp_guard = [None, None]
        self._tp2_guard = [None, None]
        self._gat_guard = [None] * 8
        self._xe_guard = None
        self._eb_guard = None
        self._y16_guard = None
        self._h_guard = None
        self._ps_guard = None
        self._gl_guard = None
        self._creg = None
        self._stg_guard = [None, None]
        self._bias_guard = [None, None]
        self._ctx_guard = None
        self._sj_guard = None
        self._ktm_guard = None
        self._att_free = [None, None]
        self._att_guard = None
        self._big_guard = None
        self._y16res = None

    def s(self, name):
        return self.sb[name]

    def xprev(self, prev_store, j):
        def L(p):
            return (list(p) if isinstance(p, list) else [p]) if p else []
        w = L(prev_store[j])
        if j < 2:
            w += L(prev_store[NT - 2 + (j % 2)])
        return w

    def emit_prep(self):
        dr, s = self.dram, self.s
        ev = []
        for cn, tn in [("c_id128", "id128"), ("c_ones_col", "ones_col"), ("c_ones_row", "ones_row"),
                       ("c_triu", "triu"), ("c_pid", "pid"), ("c_erow", "erow"),
                       ("L0_attn_temp", "temp16"), ("c_sel", "sel"), ("c_selT", "selT"),
                       ("c_crow", "crow"), ("c_tid", "ctid")]:
            ev.append(self.dma("sync", s(tn)[:], dr[cn][:, :] if dr[cn].shape[1:] else dr[cn][:], ch="param"))
        b1v = dr["L0_ffn_b1"][1, :].rearrange("(c p) -> p c", p=128)
        ev.append(self.dma("sync", s("l0b1")[:], b1v, slow=True, ch="param"))
        b2v = dr["L0_ffn_b2"][1, :].rearrange("(c p) -> p c", p=128)
        ev.append(self.dma("sync", s("l0hb2")[:], b2v, slow=True, ch="param"))
        b1l = dr["L1_ffn_b1"][:, :].rearrange("e (c p) -> p (e c)", p=128)
        ev.append(self.dma("sync", s("l1b1")[:], b1l, slow=True, ch="param"))
        b2l = dr["L1_ffn_b2"][:, :].rearrange("e (c p) -> p (e c)", p=128)
        ev.append(self.dma("sync", s("l1hb2")[:], b2l, slow=True, ch="param"))
        gv = dr["L1_ffn_gamma"][:].rearrange("(c p) -> p c", p=128)
        bv = dr["L1_ffn_beta"][:].rearrange("(c p) -> p c", p=128)
        ev.append(self.dma("sync", s("gam_h")[:], gv, slow=True, ch="param"))
        ev.append(self.dma("sync", s("bet_h")[:], bv, slow=True, ch="param"))
        bo = dr["L0_attn_bout"][:].rearrange("(c p) -> p c", p=128)
        ev.append(self.dma("sync", s("bout_c")[:], bo, slow=True, ch="param"))
        pb = dr["L1_attn_proj_b"][:].rearrange("(c p) -> p c", p=128)
        ev.append(self.dma("sync", s("projb_c")[:], pb, slow=True, ch="param"))
        ev.append(self.dma("sync", s("br8")[:], dr["L1_ffn_br"][:].rearrange("(o e) -> o e", o=1), ch="param"))
        self.params_ev = ev[-1]
        last = ev[-1]
        e1 = self.op("scalar", lambda E: E.mul(s("gam_h")[:], s("gam_h")[:], 0.5), waits=[last], track=True)
        self.op("scalar", lambda E: E.mul(s("bet_h")[:], s("bet_h")[:], 0.5))
        self.op("scalar", lambda E: E.mul(s("l0hb2")[:], s("l0hb2")[:], 0.5))
        self.op("vector", lambda E: E.memset(s("zrow")[:], 0.0))
        ev0 = self.op("vector", lambda E: E.memset(s("zrow")[:, D + 3:D + 4], float(T)), track=True)
        self.zrow_ev = self.dma("gpsimd", dr["tgt_rows"][T:T + 1, :], s("zrow")[:], waits=[ev0], ch="zrow")

        self.prep_ev = self.op("scalar", lambda E: E.copy(s("sA")[0:1, 0:1], s("gam_h")[0:1, 0:1]),
                               waits=[e1], track=True)

    def emit_pretranspose(self):
        dr, s = self.dram, self.s
        jobs = [(dr["L0_attn_Wqkv"], dr["WqkvT"], D, D)]
        jobs.append((dr["L0_attn_Wout"], dr["WoutT"], D, D))
        jobs.append((dr["L1_attn_qkv_w"], dr["qkvT"], 3 * D, D))
        jobs.append((dr["L1_attn_proj_w"], dr["projT"], D, D))
        if "l1ffn1" in self.stages or "l1ffn2" in self.stages:
            jobs.append((dr["L1_ffn_Wr"], dr["WrT"], 8, D))
            for e in range(8):
                jobs.append((dr["L1_ffn_W1"][e], dr["W1T"][e], 4096, D))
                jobs.append((dr["L1_ffn_W2"][e], dr["W2T"][e], D, 4096))
        big = s("big")
        stg_in = [big[:, 0:4096], big[:, 4096:8192]]
        stg_out = [s("y16")[:, :, :].rearrange("p c t -> p (c t)"), s("tgt")[:, :, :].rearrange("p c t -> p (c t)")]
        tp_done = [None, None]
        ev_done = [None, None]
        st_done = [None, None]
        ptbank = [None, None]
        sl = 0
        pev = [self.prep_ev]
        lastw = None
        for (W, WT, FOUT, DIN) in jobs:
            nko = (FOUT + 127) // 128
            for ko in range(nko):
                par = sl % 2
                rows = min(128, FOUT - ko * 128)
                evl = self.dma("sync", stg_in[par][0:rows, 0:DIN],
                               W[ko * 128:ko * 128 + rows, :],
                               waits=pev + [tp_done[par]], ch=f"pt_in{par}")
                evs = []
                for kd in range(DIN // 128):
                    bank = self.pb[kd % 2]
                    evt = self.op("tensor",
                                  lambda E, kd=kd, rows=rows, bank=bank, par=par:
                                  E.transpose(bank[:, 0:rows], stg_in[par][0:rows, kd * 128:(kd + 1) * 128], s("id128")[0:rows, 0:rows]),
                                  waits=[evl, ptbank[kd % 2]] + ([evs[-2][0]] if len(evs) >= 2 else []), track=True)
                    eve = self.op("scalar",
                                  lambda E, kd=kd, rows=rows, bank=bank, par=par:
                                  E.copy(stg_out[par][:, kd * 128:kd * 128 + rows], bank[:, 0:rows]),
                                  waits=[evt, st_done[par] if kd == 0 else None])
                    ptbank[kd % 2] = eve
                    evs.append((evt, eve))
                tp_done[par] = evs[-1][0]
                ev_done[par] = evs[-1][1]
                if rows == 128:
                    st_done[par] = self.dma("scalar", WT[0:DIN, ko * 128:ko * 128 + 128].rearrange("(kk p) r -> p kk r", p=128),
                                            stg_out[par][:, 0:DIN].rearrange("p (kk r) -> p kk r", r=128),
                                            waits=[ev_done[par]], ch=f"pt_out{par}")
                else:
                    for kd in range(DIN // 128):
                        st_done[par] = self.dma("scalar", WT[kd * 128:(kd + 1) * 128, ko * 128:ko * 128 + rows],
                                                stg_out[par][:, kd * 128:kd * 128 + rows],
                                                waits=[ev_done[par], st_done[par]], ch=f"pt_out{par}")
                lastw = st_done[par]
                sl += 1
            pev = []
        self.wT_ev = lastw if lastw else self.prep_ev
        self._sq_guard = st_done[0]
        self._tgt_guard = st_done[1]
    def emit_in_transpose(self):
        dr, s = self.dram, self.s
        xin = dr["x_in"]
        prev_store = [None, None]
        prev_evs = [None, None]
        for j in range(T // 128):
            buf = j % 2
            tm = s("gat")[:, buf, 0:D]
            w = [self.zrow_ev] if j == 0 else []
            if prev_store[buf] is not None:
                w.append(prev_store[buf])
            evl = self.dma("sync", tm, xin[j * 128:(j + 1) * 128, :], waits=w, ch=f"gat{buf}")
            evs = []
            for c in range(NCH):
                bank = self.pb[4 + (c % 2)]
                bg = [evs[-2]] if len(evs) >= 2 else list(prev_evs[c:c + 1])
                evt = self.op("tensor",
                              lambda E, tm=tm, c=c, bank=bank:
                              E.transpose(bank[:, 0:128], tm[:, c * 128:(c + 1) * 128], s("id128")[:]),
                              waits=[evl, self.params_ev] + bg, track=True)
                eve = self.op("vector",
                              lambda E, c=c, bank=bank, buf=buf:
                              E.tensor_copy(s("xt")[:, buf, c, 0:128], bank[:, 0:128]),
                              waits=[evt] + ([prev_store[buf]] if (c == 0 and prev_store[buf]) else []),
                              track=True)
                evs.append(eve)
            st = self.dma("scalar",
                          dr["x_fm"][:, :].rearrange("(c p) t -> p c t", p=128)[:, :, j * 128:(j + 1) * 128],
                          s("xt")[:, buf, :, 0:128], waits=[evs[-1], evs[-2]], ch=f"xst{buf}")
            prev_store[buf] = st
            prev_evs = [evs[-2], evs[-1]]
        self.xstore_ev = [[prev_store[0], prev_store[1]]] * NT

    def emit_ln(self, xtile, evx, banks=(4, 5, 6, 7), out=None, sq=None, track_out=True):
        s = self.s
        out = out if out is not None else s("tgt")
        sq = sq if sq is not None else s("y16")
        sA, sB = s("sA"), s("sB")
        tgt_guard = [self._tgt_guard] if getattr(self, "_tgt_guard", None) else []
        sq_guard = [self._sq_guard] if getattr(self, "_sq_guard", None) else []
        evq = []
        for c in range(NCH):
            evq.append(self.op("scalar", lambda E, c=c: E.activation(sq[:, c, :], xtile[:, c, :], AF.Square),
                               waits=([evx] + sq_guard) if c == 0 else [], track=True))
        ev_s = self.op("vector", lambda E: E.tensor_tensor(sA[:], xtile[:, 0, :], xtile[:, 1, :], op=ALU.add),
                       waits=[evx])
        for c in range(2, NCH):
            ev_s = self.op("vector", lambda E, c=c: E.tensor_tensor(sA[:], sA[:], xtile[:, c, :], op=ALU.add),
                           track=(c == NCH - 1))
        ev_q = self.op("vector", lambda E: E.tensor_tensor(sB[:], sq[:, 0, :], sq[:, 1, :], op=ALU.add),
                       waits=[evq[1]])
        for c in range(2, NCH):
            ev_q = self.op("vector", lambda E, c=c: E.tensor_tensor(sB[:], sB[:], sq[:, c, :], op=ALU.add),
                           waits=[evq[c]], track=(c == NCH - 1))
        b_sx, b_sq, b_r, b_mr = (self.pb[b] for b in banks)
        m1 = self.op("tensor", lambda E: E.matmul(b_sx[0:1, :], s("ones_col")[:], sA[:], start=True, stop=True),
                     waits=[ev_s, getattr(self, "_ln_pe_guard", None)], track=True)
        m2 = self.op("tensor", lambda E: E.matmul(b_sq[0:1, :], s("ones_col")[:], sB[:], start=True, stop=True),
                     waits=[ev_q], track=True)
        st1 = s("st1")
        mrow = st1[:, 0, :]; vrow = st1[:, 1, :]; rrow = st1[:, 2, :]; mrrow = st1[:, 3, :]
        self.op("vector", lambda E: E.tensor_scalar(mrow, b_sx[0:1, :], 1.0 / D, None, op0=ALU.mult), waits=[m1])
        self.op("vector", lambda E: E.tensor_scalar(vrow, b_sq[0:1, :], 1.0 / D, None, op0=ALU.mult), waits=[m2])
        self.op("vector", lambda E: E.tensor_tensor(s("st1")[:, 4, :], mrow, mrow, op=ALU.mult))
        self.op("vector", lambda E: E.tensor_tensor(vrow, vrow, s("st1")[:, 4, :], op=ALU.subtract))
        ev_v = self.op("vector", lambda E: E.tensor_scalar(vrow, vrow, 1e-5, None, op0=ALU.add), track=True)
        ev_sq = self.op("scalar", lambda E: E.activation(vrow, vrow, AF.Sqrt), waits=[ev_v], track=True)
        ev_r = self.op("vector", lambda E: E.reciprocal(rrow, vrow), waits=[ev_sq], track=True)
        ev_mr = self.op("vector", lambda E: E.tensor_tensor(mrrow, mrow, rrow, op=ALU.mult), track=True)
        mb1 = self.op("tensor", lambda E: E.matmul(b_r[:, :], s("ones_row")[:], rrow, start=True, stop=True),
                      waits=[ev_r], track=True)
        mb2 = self.op("tensor", lambda E: E.matmul(b_mr[:, :], s("ones_row")[:], mrrow, start=True, stop=True),
                      waits=[ev_mr], track=True)
        evs = []
        for c in range(NCH):
            self.op("vector", lambda E, c=c: E.tensor_tensor(out[:, c, :], xtile[:, c, :], b_r[:, :], op=ALU.mult),
                    waits=([mb1] + tgt_guard) if c == 0 else [])
            evs.append(self.op("vector", lambda E, c=c: E.tensor_tensor(out[:, c, :], out[:, c, :], b_mr[:, :], op=ALU.subtract),
                               waits=[mb2] if c == 0 else [], track=(track_out and c >= NCH - 2)))
        return [e for e in evs if e]

    def load_x(self, j, buf, extra_waits=()):
        dr, s = self.dram, self.s
        src = dr["x_fm"][:, :].rearrange("(c p) t -> p c t", p=128)[:, :, j * TILE:(j + 1) * TILE]
        return self.dma("sync", s("xt")[:, buf, :, :], src, waits=list(extra_waits), ch=f"xld{buf}")

    def store_x(self, j, buf, waits):
        dr, s = self.dram, self.s
        dst = dr["x_fm"][:, :].rearrange("(c p) t -> p c t", p=128)[:, :, j * TILE:(j + 1) * TILE]
        return self.dma("scalar", dst, s("xt")[:, buf, :, :], waits=waits, ch=f"xst{buf}")

    def emit_l0ffn(self):
        dr, s = self.dram, self.s
        W1 = dr["L0_ffn_W1"][1]
        W2 = dr["L0_ffn_W2"][1]
        h32 = s("big").rearrange("p (c t) -> p c t", t=TILE)
        prev_store = getattr(self, "xstore_ev", [None] * NT)
        xst = [None] * NT
        for j in range(NT):
            buf = j % 2
            w = self.xprev(prev_store, j)
            if j >= 2:
                w.append(xst[j - 2])
            evx = self.load_x(j, buf, extra_waits=w)
            ln_ev = self.emit_ln(s("xt")[:, buf, :, :], evx)
            hevac = [None] * 16
            for g in range(4):
                mmev = []
                for k in range(NCH):
                    wb = self._wb = (getattr(self, "_wb", 0) + 1) % 3
                    evw = self.dma("sync", s("wsl")[:, wb, :, :],
                                   W1[k * 128:(k + 1) * 128, g * 512:(g + 1) * 512].rearrange("p (i q) -> p i q", q=128),
                                   waits=[self._wslab_free[wb]], ch=f"wsl{wb}")
                    for i in range(4):
                        bw = [evw] + (ln_ev if k == 0 else []) + ([self._bank_free[i]] if k == 0 else [])
                        mmev.append(self.op("tensor",
                                            lambda E, i=i, k=k, wb=wb:
                                            E.matmul(self.pb[i][:, :], s("wsl")[:, wb, i, :], s("tgt")[:, k, :],
                                                     start=(k == 0), stop=(k == NCH - 1)),
                                            waits=bw, track=(i == 3)))
                    self._wslab_free[wb] = mmev[-1]
                if g == 3:
                    self._tgt_guard = mmev[-1]
                for i in range(4):
                    c = g * 4 + i
                    hevac[c] = self._bank_free[i] = self.op(
                        "scalar",
                        lambda E, i=i, c=c: E.activation(h32[:, c, :], self.pb[i][:, :], AF.Relu,
                                                         bias=s("l0b1")[:, c:c + 1]),
                        waits=[mmev[NCH * 4 - 1]], track=True)
            yev = []
            for g in range(2):
                mmev = []
                for k in range(16):
                    wb = self._wb = (self._wb + 1) % 3
                    evw = self.dma("sync", s("wsl")[:, wb, :, :],
                                   W2[k * 128:(k + 1) * 128, g * 512:(g + 1) * 512].rearrange("p (i q) -> p i q", q=128),
                                   waits=[self._wslab_free[wb]], ch=f"wsl{wb}")
                    for i in range(4):
                        bw = [evw, self._bank_free[i] if k == 0 else None, hevac[k]]
                        mmev.append(self.op("tensor",
                                            lambda E, i=i, k=k, wb=wb:
                                            E.matmul(self.pb[i][:, :], s("wsl")[:, wb, i, :], h32[:, k, :],
                                                     start=(k == 0), stop=(k == 15)),
                                            waits=bw, track=(i == 3)))
                    self._wslab_free[wb] = mmev[-1]
                for i in range(4):
                    c = g * 4 + i
                    self._bank_free[i] = self.op(
                        "vector",
                        lambda E, i=i, c=c: E.tensor_scalar(s("y16")[:, c, :], self.pb[i][:, :],
                                                            0.5, s("l0hb2")[:, c:c + 1],
                                                            op0=ALU.mult, op1=ALU.add),
                        waits=[mmev[63]], track=True)
                    yev.append(self._bank_free[i])
            evu = []
            for c in range(NCH):
                evu.append(self.op("vector",
                                   lambda E, c=c, buf=buf: E.tensor_tensor(s("xt")[:, buf, c, :], s("xt")[:, buf, c, :],
                                                                           s("y16")[:, c, :], op=ALU.add),
                                   waits=[yev[c]], track=(c >= NCH - 2)))
            self._sq_guard = evu[-1]
            xst[j] = self.store_x(j, buf, waits=[e for e in evu if e])
        self.xstore_ev = xst

    def vn(self):
        return (self.csem["vector"], self.csem["vector"].n)

    def an(self):
        return (self.csem["scalar"], self.csem["scalar"].n)

    def emit_l0attn(self):
        dr, s = self.dram, self.s
        prev_store = self.xstore_ev
        xst = [None] * NT
        w_fm = dr["w_fm"][:, :].rearrange("(c p) t -> p c t", p=128)
        wp_fm = dr["wp_fm"][:, :].rearrange("(c p) t -> p c t", p=128)
        TPS = SEQ // TILE
        pb = self.pb
        for q in range(NSEQ):
            ev_z = self.op("vector", lambda E: E.memset(s("acc8")[:], 0.0))
            wstall = [None] * TPS
            for tj in range(TPS):
                j = q * TPS + tj
                buf = j % 2
                evx = self.load_x(j, buf, extra_waits=self.xprev(prev_store, j) + [self.vn(), self.an(),
                                                                                    xst[j - 2] if j >= 2 else None])
                ln_ev = self.emit_ln(s("xt")[:, buf, :, :], evx)
                stage = s("y16")
                mmev = []
                for g in range(2):
                    for k in range(NCH):
                        wb = self._wb = (self._wb + 1) % 3
                        evw = self.dma("sync", s("wsl")[:, wb, :, :],
                                       dr["WqkvT"][k * 128:(k + 1) * 128, g * 512:(g + 1) * 512].rearrange("p (i q) -> p i q", q=128),
                                       waits=[self._wslab_free[wb],
                                              self.wT_ev if (q == 0 and tj == 0 and k == 0) else None], ch=f"wsl{wb}")
                        for i in range(4):
                            bw = [evw] + (ln_ev if k == 0 else []) + ([self._bank_free[i]] if k == 0 else [])
                            mmev.append(self.op("tensor",
                                                lambda E, i=i, k=k, wb=wb:
                                                E.matmul(pb[i][:, :], s("wsl")[:, wb, i, :], s("tgt")[:, k, :],
                                                         start=(k == 0), stop=(k == NCH - 1)),
                                                waits=bw, track=(i == 3)))
                        self._wslab_free[wb] = mmev[-1]
                    self._tgt_guard = mmev[-1]
                    for i in range(4):
                        c = g * 4 + i
                        self._bank_free[i] = self.op("vector",
                                                     lambda E, i=i, c=c, stage=stage: E.tensor_copy(stage[:, c, :], pb[i][:, :]),
                                                     waits=[mmev[(g * NCH + NCH - 1) * 4 + 3],
                                                            self._sq_guard if c == 0 else None])
                for c in range(NCH):
                    self.op("scalar", lambda E, c=c, stage=stage: E.activation(s("sB")[:], stage[:, c, :], AF.Square,
                                                                  accum_out=s("s81")[:, 0:1]),
                            waits=[self.vn(), ev_z if c == 0 else None])
                    self.op("vector", lambda E, c=c: E.tensor_tensor(s("acc8")[:, c:c + 1], s("acc8")[:, c:c + 1],
                                                                     s("s81")[:, 0:1], op=ALU.add),
                            waits=[self.an()])
                wstall[tj] = self.dma("scalar", w_fm[:, :, j * TILE:(j + 1) * TILE], stage[:, :, :],
                                      waits=[self.vn(), self.an()], ch=f"wst{tj}")
                self._sq_guard = wstall[tj]
            self.op("scalar", lambda E: E.activation(s("acc8")[:, 0:8], s("acc8")[:, 0:8], AF.Sqrt), waits=[self.vn()])
            self.op("vector", lambda E: E.tensor_scalar(s("acc8")[:, 0:8], s("acc8")[:, 0:8], 1e-12, None, op0=ALU.max),
                    waits=[self.an()])
            self.op("vector", lambda E: E.tensor_tensor(s("acc8")[:, 0:8], s("acc8")[:, 0:8], s("acc8")[:, 0:8], op=ALU.mult))
            ev_inv = self.op("vector", lambda E: E.reciprocal(s("inv8")[:, 0:8], s("acc8")[:, 0:8]))
            wpstall = [None] * TPS
            tpg = [None, None]
            for tj in range(TPS):
                j = q * TPS + tj
                buf = j % 2
                evw = self.dma("sync", s("xt")[:, buf, :, :], w_fm[:, :, j * TILE:(j + 1) * TILE],
                               waits=[wstall[tj], self.vn(), self.an()], ch=f"xld{buf}")
                wtile = s("xt")[:, buf, :, :]
                stage = s("y16")
                selev = None
                for c in range(NCH):
                    self.op("scalar", lambda E, c=c, wtile=wtile: E.activation(s("sB")[:], wtile[:, c, :], AF.Square),
                            waits=[evw, ev_inv, self.vn(), selev])
                    self.op("vector", lambda E, c=c: E.tensor_scalar(s("sB")[:], s("sB")[:], s("inv8")[:, c:c + 1], None, op0=ALU.mult),
                            waits=[self.an()])
                    selev = self.op("tensor",
                                    lambda E, c=c: E.matmul(pb[4][0:16, :], s("sel")[:, c * 16:(c + 1) * 16], s("sB")[:],
                                                            start=(c == 0), stop=(c == NCH - 1)),
                                    waits=[self.vn(), selev], track=True)
                ev_L = self.op("vector", lambda E: E.tensor_scalar(s("L16")[:, :], pb[4][0:16, :], s("temp16")[:, 0:1], None, op0=ALU.mult),
                               waits=[selev, getattr(self, "_dbg_ev", None)])
                if getattr(self, "debug", False):
                    self._dbg_ev = self.dma("gpsimd", dr["dbgL"][:, j * TILE:(j + 1) * TILE], s("L16")[:, :],
                                            waits=[ev_L], ch="dbg")
                for gq in range(4):
                    tp1 = self.op("tensor", lambda E, gq=gq: E.transpose(pb[5][:, 0:16], s("L16")[:, gq * 128:(gq + 1) * 128], s("id128")[0:16, 0:16]),
                                  waits=[ev_L, tpg[0]], track=True)
                    self.op("vector", lambda E: E.tensor_copy(s("lgtm")[:, 0:16], pb[5][:, 0:16]), waits=[tp1])
                    tpg[0] = self.vn()
                    self.op("vector", lambda E: E.reduce_max(s("s81")[:, 0:1], s("lgtm")[:, 0:16], axis=AX.X))
                    self.op("vector", lambda E: E.tensor_scalar(s("s81")[:, 1:2], s("s81")[:, 0:1], -1.0, None, op0=ALU.mult))
                    self.op("scalar", lambda E: E.activation(s("lgtm")[:, 0:16], s("lgtm")[:, 0:16], AF.Exp, bias=s("s81")[:, 1:2]),
                            waits=[self.vn()])
                    self.op("vector", lambda E: E.reduce_sum(s("s81")[:, 2:3], s("lgtm")[:, 0:16], axis=AX.X), waits=[self.an()])
                    self.op("vector", lambda E: E.reciprocal(s("s81")[:, 3:4], s("s81")[:, 2:3]))
                    evp = self.op("vector", lambda E: E.tensor_scalar(s("lgtm")[:, 0:16], s("lgtm")[:, 0:16], s("s81")[:, 3:4], None, op0=ALU.mult))
                    tp2 = self.op("tensor", lambda E: E.transpose(pb[6][0:16, 0:128], s("lgtm")[:, 0:16], s("id128")[:]),
                                  waits=[evp, tpg[1]], track=True)
                    tpg[1] = self.op("vector", lambda E, tj=tj, gq=gq: E.tensor_copy(s("Pi")[:, tj * TILE + gq * 128:tj * TILE + (gq + 1) * 128], pb[6][0:16, 0:128]),
                                     waits=[tp2])
                exv = None
                for c in range(NCH):
                    exv = self.op("tensor", lambda E, c=c, tj=tj: E.matmul(pb[7][:, :], s("selT")[:, c * 128:(c + 1) * 128],
                                                                           s("Pi")[:, tj * TILE:(tj + 1) * TILE], start=True, stop=True),
                                  waits=[self.vn(), exv], track=True)
                    self.op("vector", lambda E, c=c, stage=stage, wtile=wtile: E.tensor_tensor(stage[:, c, :], wtile[:, c, :], pb[7][:, :], op=ALU.mult),
                            waits=[exv, self._sq_guard if c == 0 else None])
                    self.op("vector", lambda E, c=c, stage=stage, wtile=wtile: E.tensor_tensor(s("sB")[:], stage[:, c, :], wtile[:, c, :], op=ALU.mult))
                    self.op("vector", lambda E, c=c: E.tensor_reduce(s("s81")[:, 4:5], s("sB")[:], axis=AX.X, op=ALU.add))
                    self.op("vector", lambda E, c=c: E.tensor_tensor(s("acc8")[:, 8 + c:9 + c], s("acc8")[:, 8 + c:9 + c],
                                                                     s("s81")[:, 4:5], op=ALU.add))
                wpstall[tj] = self.dma("scalar", wp_fm[:, :, j * TILE:(j + 1) * TILE], stage[:, :, :],
                                       waits=[self.vn()], ch=f"wst{tj}")
                self._sq_guard = wpstall[tj]
            self.op("vector", lambda E: E.reduce_sum(s("s81")[0:16, 5:6], s("Pi")[:, :], axis=AX.X))
            self.op("vector", lambda E: E.tensor_scalar(s("s81")[0:16, 5:6], s("s81")[0:16, 5:6], 1e-8, None, op0=ALU.add))
            ev_ps = self.op("vector", lambda E: E.reciprocal(s("s81")[0:16, 6:7], s("s81")[0:16, 5:6]))
            exv = None
            for c in range(NCH):
                exv = self.op("tensor", lambda E, c=c: E.matmul(pb[7][:, 0:1], s("selT")[:, c * 128:(c + 1) * 128],
                                                                s("s81")[0:16, 6:7], start=True, stop=True),
                              waits=[ev_ps, self.vn(), exv], track=True)
                self.op("vector", lambda E, c=c: E.tensor_tensor(s("inv8")[:, 8 + c:9 + c], s("acc8")[:, 8 + c:9 + c],
                                                                 pb[7][:, 0:1], op=ALU.mult), waits=[exv])
                self.op("vector", lambda E, c=c: E.tensor_scalar(s("inv8")[:, 8 + c:9 + c], s("inv8")[:, 8 + c:9 + c], 1.0, None, op0=ALU.add))
                self.op("vector", lambda E, c=c: E.reciprocal(s("inv8")[:, 8 + c:9 + c], s("inv8")[:, 8 + c:9 + c]))
            ev_att = self.op("vector", lambda E: E.tensor_scalar(s("inv8")[:, 8:16], s("inv8")[:, 8:16], -1.0, None, op0=ALU.mult))
            for tj in range(TPS):
                j = q * TPS + tj
                buf = j % 2
                evwp = self.dma("sync", s("y16")[:, :, :], wp_fm[:, :, j * TILE:(j + 1) * TILE],
                                waits=[wpstall[tj], self.vn()] + ([wpstall[2], wpstall[3]] if tj == 0 else []),
                                ch=f"wld0")
                evx = self.load_x(j, buf, extra_waits=self.xprev(prev_store, j) + [self.vn(), xst[j - 2] if j >= 2 else None])
                oev = None
                for c in range(NCH):
                    oev = self.op("vector", lambda E, c=c: E.tensor_scalar(s("tgt")[:, c, :], s("y16")[:, c, :],
                                                                           s("inv8")[:, 8 + c:9 + c], None, op0=ALU.mult),
                                  waits=[evwp, ev_att, self._tgt_guard] if c == 0 else [])
                mmev = []
                for g in range(2):
                    for k in range(NCH):
                        wb = self._wb = (self._wb + 1) % 3
                        evw = self.dma("sync", s("wsl")[:, wb, :, :],
                                       dr["WoutT"][k * 128:(k + 1) * 128, g * 512:(g + 1) * 512].rearrange("p (i q) -> p i q", q=128),
                                       waits=[self._wslab_free[wb]], ch=f"wsl{wb}")
                        for i in range(4):
                            bw = [evw] + ([oev] if k == 0 else []) + ([self._bank_free[i]] if k == 0 else [])
                            mmev.append(self.op("tensor",
                                                lambda E, i=i, k=k, wb=wb:
                                                E.matmul(pb[i][:, :], s("wsl")[:, wb, i, :], s("tgt")[:, k, :],
                                                         start=(k == 0), stop=(k == NCH - 1)),
                                                waits=bw, track=(i == 3)))
                        self._wslab_free[wb] = mmev[-1]
                    self._tgt_guard = mmev[-1]
                    for i in range(4):
                        c = g * 4 + i
                        self._bank_free[i] = self.op("vector",
                                                     lambda E, i=i, c=c: E.tensor_scalar(s("sB")[:], pb[i][:, :], s("bout_c")[:, c:c + 1], None, op0=ALU.add),
                                                     waits=[mmev[(g * NCH + NCH - 1) * 4 + 3], evx if c == 0 else None])
                        self.op("vector", lambda E, c=c, buf=buf: E.tensor_tensor(s("xt")[:, buf, c, :], s("xt")[:, buf, c, :],
                                                                                  s("sB")[:], op=ALU.add))
                xst[j] = self.store_x(j, buf, waits=[self.vn()])
        self.xstore_ev = xst
        self._sq_guard = xst[-1]

    def emit_l1_gathers(self, c, evgl, gstore):
        dr, s = self.dram, self.s
        if not hasattr(self, "_gath_ev"):
            self._gath_ev = [[None] * 4, [None] * 4]
        for sg in range(4):
            gb = 4 * (c % 2) + sg
            self._gath_ev[c % 2][sg] = self.dma(
                "gpsimd", s("gat")[:, gb, :], dr["tgt_rows"][:, :],
                waits=[evgl, self.zrow_ev, self._gat_guard[gb]] +
                      ([gstore[0], gstore[1]] if c == 0 else []),
                ch=f"gath{gb}", gather_idx=s("glist")[:, 4 * c + sg:4 * c + sg + 1])

    def emit_l1ffn(self, cidx):
        dr, s = self.dram, self.s
        pb = self.pb
        prev_store = self.xstore_ev
        TR = dr["tgt_rows"]
        tag = f"f{cidx}"
        evwr = self.dma("sync", s("wrs")[:], dr["WrT"][:, :].rearrange("(k p) e -> p k e", p=128),
                        waits=[self.wT_ev], ch="wrs", slow=True)
        evbr = self.op("tensor", lambda E: E.matmul(pb[3][:, 0:8], s("ones_row")[:], s("br8")[:, :], start=True, stop=True),
                       waits=[self.params_ev, self._bank_free[3], self.vn()], track=True)
        evbrb = self.op("vector", lambda E: E.tensor_copy(s("brb")[:], pb[3][:, 0:8]), waits=[evbr])
        self._bank_free[3] = evbrb
        gstore = [None, None]
        for j in range(NT):
            buf = j % 2
            w = self.xprev(prev_store, j) + [self.vn(), self.an()]
            evx = self.load_x(j, buf, extra_waits=w)
            ln_ev = self.emit_ln(s("xt")[:, buf, :, :], evx)
            lgev = None
            for sg in range(4):
                for k in range(NCH):
                    lgev = self.op("tensor",
                                   lambda E, sg=sg, k=k: E.matmul(pb[0][:, sg * 8:(sg + 1) * 8],
                                                                  s("tgt")[:, k, sg * 128:(sg + 1) * 128],
                                                                  s("wrs")[:, k, :],
                                                                  start=(sg == 0 and k == 0), stop=(sg == 3 and k == NCH - 1)),
                                   waits=[evwr, self._lg_guard if (sg == 0 and k == 0) else None,
                                          self._bank_free[0] if (sg == 0 and k == 0) else None] +
                                         (ln_ev if (sg == 0 and k == 0) else []),
                                   track=(k == NCH - 1))
            for sg in range(4):
                rb = s("rt")[:, sg, :]
                self.op("vector", lambda E, sg=sg: E.tensor_tensor(s("lg16")[:, 0:8], pb[0][:, sg * 8:(sg + 1) * 8], s("brb")[:], op=ALU.add),
                        waits=[lgev, self._rt_guard[sg]])
                self.op("vector", lambda E: E.reduce_max(s("s81")[:, 0:1], s("lg16")[:, 0:8], axis=AX.X))
                self.op("vector", lambda E: E.tensor_scalar(s("lg16")[:, 8:16], s("lg16")[:, 0:8], s("s81")[:, 0:1], None, op0=ALU.is_equal))
                self.op("vector", lambda E: E.tensor_tensor(s("sB")[:, 0:8], s("lg16")[:, 8:16], s("erow")[:], op=ALU.mult))
                self.op("vector", lambda E, rb=rb: E.reduce_sum(rb[:, 2:3], s("sB")[:, 0:8], axis=AX.X))
                self.op("vector", lambda E: E.scalar_tensor_tensor(s("lg16")[:, 8:16], s("lg16")[:, 8:16], -1e30, s("lg16")[:, 0:8], op0=ALU.mult, op1=ALU.add))
                self.op("vector", lambda E: E.reduce_max(s("s81")[:, 1:2], s("lg16")[:, 8:16], axis=AX.X))
                self.op("vector", lambda E: E.tensor_scalar(s("lg16")[:, 8:16], s("lg16")[:, 8:16], s("s81")[:, 1:2], None, op0=ALU.is_equal))
                self.op("vector", lambda E: E.tensor_tensor(s("sB")[:, 0:8], s("lg16")[:, 8:16], s("erow")[:], op=ALU.mult))
                self.op("vector", lambda E, j=j, sg=sg: E.reduce_sum(s("ri1")[:, 4 * j + sg:4 * j + sg + 1], s("sB")[:, 0:8], axis=AX.X))
                self.op("vector", lambda E: E.tensor_tensor(s("s81")[:, 2:3], s("s81")[:, 0:1], s("s81")[:, 1:2], op=ALU.subtract))
                self.op("scalar", lambda E, rb=rb: E.activation(rb[:, 0:1], s("s81")[:, 2:3], AF.Sigmoid), waits=[self.vn()])
                self.op("vector", lambda E: E.tensor_scalar(s("s81")[:, 3:4], s("s81")[:, 2:3], -1.0, None, op0=ALU.mult), waits=[self.an()])
                self.op("scalar", lambda E, rb=rb: E.activation(rb[:, 1:2], s("s81")[:, 3:4], AF.Sigmoid), waits=[self.vn()])
                self.op("vector", lambda E, rb=rb, j=j, sg=sg: E.tensor_scalar(rb[:, 3:4], s("pid")[:], float(128 * (4 * j + sg)), None, op0=ALU.add),
                        waits=[self.an()])
                self.op("vector", lambda E, rb=rb, j=j, sg=sg: E.tensor_copy(s("ri0")[:, 4 * j + sg:4 * j + sg + 1], rb[:, 2:3]))
                buf2 = sg % 2
                tev = []
                for c in range(NCH):
                    bank = pb[1 + (c % 2)]
                    tp = self.op("tensor",
                                 lambda E, c=c, sg=sg, bank=bank: E.transpose(bank[:, 0:128], s("tgt")[:, c, sg * 128:(sg + 1) * 128], s("id128")[:]),
                                 waits=[self._tp2_guard[c % 2], lgev,
                                        self._bank_free[1 + (c % 2)] if (j == 0 and sg == 0 and c < 2) else None],
                                 track=True)
                    ev = self.op("vector",
                                 lambda E, c=c, buf2=buf2, bank=bank:
                                 E.tensor_copy(s("gat")[:, buf2, c * 128:(c + 1) * 128], bank[:, 0:128]),
                                 waits=[tp, gstore[buf2] if c == 0 else None])
                    self._tp2_guard[c % 2] = ev
                    tev.append(ev)
                t0 = j * TILE + sg * 128
                self.dma("scalar", TR[t0:t0 + 128, 0:D], s("gat")[:, sg % 2, 0:D],
                         waits=[self.vn()], ch=f"trow{sg % 2}")
                g2 = self.dma("scalar", TR[t0:t0 + 128, D:D + 4], s("rt")[:, sg, 0:4],
                              waits=[self.vn()], ch=f"trow{sg % 2}")
                gstore[sg % 2] = g2
                self._rt_guard[sg] = g2
            self._tgt_guard = self._tp2_guard[1]
            self._lg_guard = self.vn()
        ev_pre0 = self.op("vector", lambda E: E.memset(s("sB")[:, 0:160], 8192.0), waits=[self._ps_guard])
        ev_pre1 = self.op("vector", lambda E: E.tensor_copy(s("glist")[:], s("sB")[:, 0:160]))
        ev_pre = self.dma("gpsimd", dr["list_hbm"][0:NSLOT // 2, :].rearrange("(g p) o -> p (g o)", p=128),
                          s("glist")[:, 0:80], waits=[ev_pre1], ch="lpre", slow=True)
        ev_pre = self.dma("gpsimd", dr["list_hbm"][NSLOT // 2:, :].rearrange("(g p) o -> p (g o)", p=128),
                          s("glist")[:, 80:160], waits=[ev_pre1, ev_pre], ch="lpre", slow=True)
        self.op("vector", lambda E: E.memset(s("ps0")[:], 0.0))
        self.op("vector", lambda E: E.memset(s("ps1")[:], 0.0))
        evz = self.op("vector", lambda E: E.memset(s("e8")[:, :], 0.0))
        for e in range(8):
            self.op("vector", lambda E, e=e: E.tensor_scalar(s("m0")[:], s("ri0")[:], float(e), None, op0=ALU.is_equal), waits=[evz])
            self.op("vector", lambda E, e=e: E.tensor_scalar(s("m1")[:], s("ri1")[:], float(e), None, op0=ALU.is_equal))
            self.op("vector", lambda E: E.tensor_tensor(s("mk")[:], s("m0")[:], s("m1")[:], op=ALU.add))
            self.op("vector", lambda E: E.tensor_tensor_scan(s("sc")[:], s("mk")[:], s("mk")[:], 0.0, op0=ALU.add, op1=ALU.bypass))
            evtot = self.op("vector", lambda E: E.reduce_sum(s("s81")[:, 0:1], s("mk")[:], axis=AX.X))
            evpf = self.op("tensor", lambda E: E.matmul(pb[3][:, 0:1], s("triu")[:], s("s81")[:, 0:1], start=True, stop=True),
                           waits=[evtot, self._bank_free[3]], track=True)
            evne = self.op("tensor", lambda E: E.matmul(pb[3][0:1, 1:2], s("ones_col")[:], s("s81")[:, 0:1], start=False, stop=True,
                                                        skip_group_check=True),
                           waits=[evtot], track=True)
            self.op("vector", lambda E: E.tensor_tensor(s("sc")[:], s("sc")[:], s("mk")[:], op=ALU.subtract), waits=[evpf])
            self.op("vector", lambda E: E.tensor_scalar(s("sc")[:], s("sc")[:], pb[3][:, 0:1], None, op0=ALU.add))
            self.op("vector", lambda E: E.tensor_tensor(s("mm0")[:], s("sc")[:], s("m0")[:], op=ALU.mult))
            self.op("vector", lambda E: E.tensor_tensor(s("ps0")[:], s("ps0")[:], s("mm0")[:], op=ALU.add))
            self.op("vector", lambda E: E.tensor_tensor(s("mm0")[:], s("sc")[:], s("m1")[:], op=ALU.mult))
            self.op("vector", lambda E: E.tensor_tensor(s("ps1")[:], s("ps1")[:], s("mm0")[:], op=ALU.add))
            evcp = self.op("vector", lambda E, e=e: E.tensor_copy(s("e8")[0:1, e:e + 1], pb[3][0:1, 1:2]), waits=[evne])
            self._bank_free[3] = evcp
        self.op("vector", lambda E: E.memset(s("e8b")[:, :], 0.0))
        for kk in range(1, 17):
            self.op("vector", lambda E, kk=kk: E.scalar_tensor_tensor(s("e8b")[:, :], s("e8")[:, :], float(512 * (kk - 1)),
                                                                      s("e8b")[:, :], op0=ALU.is_gt, op1=ALU.add))
        self.op("vector", lambda E: E.tensor_tensor_scan(s("e8c")[:, :], s("e8b")[:, :], s("e8b")[:, :], 0.0, op0=ALU.add, op1=ALU.bypass))
        self.op("vector", lambda E: E.tensor_tensor(s("e8")[:, :], s("e8c")[:, :], s("e8b")[:, :], op=ALU.subtract))
        evgo = self.op("vector", lambda E: E.tensor_scalar(s("e8")[:, :], s("e8")[:, :], 512.0, None, op0=ALU.mult))
        self.op("vector", lambda E: E.memset(s("ech")[:, :], 0.0))
        for e in range(8):
            self.op("vector", lambda E, e=e: E.scalar_tensor_tensor(s("ech")[:, :], s("crow")[:, :], s("e8c")[0:1, e:e + 1],
                                                                    s("ech")[:, :], op0=ALU.is_ge, op1=ALU.add))
        self.op("vector", lambda E: E.tensor_scalar(s("ech")[:, :], s("ech")[:, :], 7.0, None, op0=ALU.min))
        evecu = self.op("vector", lambda E: E.tensor_copy(s("echu")[:, :], s("ech")[:, :]))
        evgb = self.op("tensor", lambda E: E.matmul(pb[3][:, 2:10], s("ones_row")[:], s("e8")[0:1, :], start=False, stop=True,
                                                    skip_group_check=True),
                       waits=[evgo], track=True)
        evgbc = self.op("vector", lambda E: E.tensor_copy(s("goffb")[:], pb[3][:, 2:10]), waits=[evgb])
        self._bank_free[3] = evgbc
        for e in range(8):
            self.op("vector", lambda E, e=e: E.tensor_scalar(s("m0")[:], s("ri0")[:], float(e), None, op0=ALU.is_equal))
            self.op("vector", lambda E, e=e: E.scalar_tensor_tensor(s("ps0")[:], s("m0")[:], s("goffb")[:, e:e + 1], s("ps0")[:],
                                                                    op0=ALU.mult, op1=ALU.add))
            self.op("vector", lambda E, e=e: E.tensor_scalar(s("m1")[:], s("ri1")[:], float(e), None, op0=ALU.is_equal))
            self.op("vector", lambda E, e=e: E.scalar_tensor_tensor(s("ps1")[:], s("m1")[:], s("goffb")[:, e:e + 1], s("ps1")[:],
                                                                    op0=ALU.mult, op1=ALU.add))
        evq0 = self.op("vector", lambda E: E.tensor_copy(s("pu0")[:], s("ps0")[:]))
        evq1 = self.op("vector", lambda E: E.tensor_copy(s("pu1")[:], s("ps1")[:]))
        evtd = self.op("vector", lambda E: E.tensor_copy(s("tidu")[:], s("ctid")[:]), waits=[self.params_ev])
        scev = []
        for g in range(64):
            scev.append(self.dma("gpsimd", dr["list_hbm"][:, :], s("tidu")[:, g:g + 1],
                                 waits=[evq0, evtd, ev_pre], ch=f"lsc{g % 4}",
                                 scatter_idx=s("pu0")[:, g:g + 1]))
        for g in range(64):
            scev.append(self.dma("gpsimd", dr["list_hbm"][:, :], s("tidu")[:, g:g + 1],
                                 waits=[evq1, evtd, ev_pre], ch=f"lsc{g % 4}",
                                 scatter_idx=s("pu1")[:, g:g + 1]))
        evgl = self.dma("sync", s("glist")[:, 0:80],
                        dr["list_hbm"][0:NSLOT // 2, :].rearrange("(g p) o -> p (g o)", p=128),
                        waits=scev[-4:] + [ev_pre1], ch="glud", slow=True)
        evgl = self.dma("sync", s("glist")[:, 80:160],
                        dr["list_hbm"][NSLOT // 2:, :].rearrange("(g p) o -> p (g o)", p=128),
                        waits=scev[-4:] + [evgl], ch="glud", slow=True)
        self._ps_guard = evgl
        ysc = [None, None]
        h32 = s("big").rearrange("p (cc t) -> p cc t", t=TILE)
        xe = s("tgt")
        for pp in range(2):
            if f"stg{pp}" not in self.dch:
                self.dch[f"stg{pp}"] = self.mksem(f"ch_stg{pp}")
        stg_ev = [None, None]
        stg_free = [None, None]
        bias_ev = [None, None]
        def emit_stage(c, waits):
            par = c % 2
            stg = self.dch[f"stg{par}"]
            v = stg.inc(16 * 2)
            if f"bia{par}" not in self.dch:
                self.dch[f"bia{par}"] = self.mksem(f"ch_bia{par}")
            bch = self.dch[f"bia{par}"]
            bv = bch.inc(16 * 2)
            def build(E, c=c, par=par):
                if self._creg is None:
                    self._creg = E.alloc_register()
                E.reg_load(self._creg, s("echu")[0:1, c:c + 1])
                for e in range(8):
                    with E.If_eq(self._creg, e):
                        E.dma_start(out=dr["stgW1"][par], in_=dr["W1T"][e]).then_inc(stg.h, 16)
                        E.dma_start(out=dr["stgW2"][par], in_=dr["W2T"][e]).then_inc(stg.h, 16)
                        E.dma_start(out=s("b1c")[:, par, :], in_=dr["L1_ffn_b1"][e, :].rearrange("(cc p) -> p cc", p=128),
                                    allow_slow_non_contiguous=True).then_inc(bch.h, 16)
                        E.dma_start(out=s("b2c")[:, par, :], in_=dr["L1_ffn_b2"][e, :].rearrange("(cc p) -> p cc", p=128),
                                    allow_slow_non_contiguous=True).then_inc(bch.h, 16)
            self.P.emit("sync", build, waits=[w for w in waits if w])
            return (stg, v), (bch, bv)
        ev0, bv0 = emit_stage(0, [evecu, self._stg_guard[0]])
        stg_ev[0], bias_ev[0] = ev0, bv0
        for c in range(NCHUNK):
            par = c % 2
            if c + 1 < NCHUNK:
                stg_ev[1 - par], bias_ev[1 - par] = emit_stage(
                    c + 1, [evecu, stg_free[1 - par], self._stg_guard[1 - par] if c == 0 else None,
                            self._bias_guard[1 - par]])
            if c == 0:
                self.emit_l1_gathers(0, evgl, gstore)
            gtev = []
            for sg in range(4):
                gb = 4 * (c % 2) + sg
                evg = self._gath_ev[c % 2][sg]
                evrt = self.op("vector", lambda E, sg=sg, gb=gb: E.tensor_copy(s("rt")[:, sg, 0:4], s("gat")[:, gb, D:D + 4]),
                               waits=[evg, self._rt_guard[sg]])
                self._rt_guard[sg] = evrt
                for cc in range(NCH):
                    bank = pb[4 + (cc % 2)]
                    tp = self.op("tensor",
                                 lambda E, cc=cc, gb=gb, bank=bank:
                                 E.transpose(bank[:, 0:128], s("gat")[:, gb, cc * 128:(cc + 1) * 128], s("id128")[:]),
                                 waits=[evg, self._tp_guard[cc % 2], self._xe_guard if (cc == 0 and sg == 0) else None],
                                 track=True)
                    ev = self.op("vector",
                                 lambda E, cc=cc, sg=sg, bank=bank: E.tensor_copy(xe[:, cc, sg * 128:(sg + 1) * 128], bank[:, 0:128]),
                                 waits=[tp])
                    self._tp_guard[cc % 2] = ev
                    gtev.append(ev)
                self._gat_guard[gb] = gtev[-1]
            mmev = []
            for g in range(8):
                for k in range(NCH):
                    wb = self._wb = (self._wb + 1) % 3
                    evw = self.dma("sync", s("wsl")[:, wb, :, :],
                                   dr["stgW1"][par][k * 128:(k + 1) * 128, g * 512:(g + 1) * 512].rearrange("p (i q) -> p i q", q=128),
                                   waits=[self._wslab_free[wb], stg_ev[par] if (g == 0 and k == 0) else None], ch=f"wsl{wb}")
                    for i in range(4):
                        bw = [evw] + ([gtev[-1]] if k == 0 else []) + ([self._bank_free[i]] if k == 0 else [])
                        mmev.append(self.op("tensor",
                                            lambda E, i=i, k=k, wb=wb: E.matmul(pb[i][:, :], s("wsl")[:, wb, i, :], xe[:, k, :],
                                                                                start=(k == 0), stop=(k == NCH - 1)),
                                            waits=bw, track=(i == 3)))
                    self._wslab_free[wb] = mmev[-1]
                for i in range(4):
                    cc = g * 4 + i
                    self._bank_free[i] = self.op(
                        "scalar", lambda E, i=i, cc=cc, par=par: E.activation(h32[:, cc, :], pb[i][:, :], AF.Relu,
                                                                              bias=s("b1c")[:, par, cc:cc + 1]),
                        waits=[mmev[(g * NCH + NCH - 1) * 4 + 3], bias_ev[par], self._h_guard if cc == 0 else None,
                               ysc[0] if cc < 6 else None, ysc[1] if cc < 6 else None])
            self._xe_guard = mmev[-1]
            if c + 1 < NCHUNK:
                self.emit_l1_gathers(c + 1, evgl, gstore)
            mmev2 = []
            for g in range(2):
                for k in range(32):
                    wb = self._wb = (self._wb + 1) % 3
                    evw = self.dma("sync", s("wsl")[:, wb, :, :],
                                   dr["stgW2"][par][k * 128:(k + 1) * 128, g * 512:(g + 1) * 512].rearrange("p (i q) -> p i q", q=128),
                                   waits=[self._wslab_free[wb]], ch=f"wsl{wb}")
                    for i in range(4):
                        bw = [evw, self._bank_free[i] if k == 0 else None,
                              (self.an() if k == 0 else None)]
                        mmev2.append(self.op("tensor",
                                             lambda E, i=i, k=k, wb=wb: E.matmul(pb[i][:, :], s("wsl")[:, wb, i, :], h32[:, k, :],
                                                                                 start=(k == 0), stop=(k == 31)),
                                             waits=bw, track=(i == 3)))
                    self._wslab_free[wb] = mmev2[-1]
                for i in range(4):
                    cc = g * 4 + i
                    self._bank_free[i] = self.op("vector",
                                                 lambda E, i=i, cc=cc, par=par: E.tensor_scalar(s("y16")[:, cc, :], pb[i][:, :],
                                                                                                s("b2c")[:, par, cc:cc + 1], None, op0=ALU.add),
                                                 waits=[mmev2[(g * 32 + 31) * 4 + 3], bias_ev[par],
                                                        self._y16_guard if (g == 0 and i == 0) else None])
            self._h_guard = mmev2[-1]
            stg_free[par] = mmev2[-1]
            self._bias_guard[par] = self._bank_free[3]
            evebc = self.op("tensor", lambda E, c=c: E.matmul(pb[6][:, 0:1], s("ones_row")[:], s("ech")[0:1, c:c + 1], start=True, stop=True),
                            waits=[evecu, self._eb_guard], track=True)
            ylast_tp = None
            ybig = s("big")
            for sg in range(4):
                b2 = sg % 2
                yst = ybig[:, b2 * 1088:b2 * 1088 + 1088]
                for cc in range(NCH):
                    bank = pb[4 + (cc % 2)]
                    tp = self.op("tensor",
                                 lambda E, cc=cc, sg=sg, bank=bank: E.transpose(bank[:, 0:128], s("y16")[:, cc, sg * 128:(sg + 1) * 128], s("id128")[:]),
                                 waits=[self.vn() if cc == 0 else None, self._tp_guard[cc % 2],
                                        (self._h_guard if (sg == 0 and cc == 0) else None)], track=True)
                    ev = self.op("vector",
                                 lambda E, cc=cc, yst=yst, bank=bank:
                                 E.tensor_copy(yst[:, cc * 128:(cc + 1) * 128], bank[:, 0:128]),
                                 waits=[tp, ysc[b2] if cc == 0 else None])
                    self._tp_guard[cc % 2] = ev
                    ylast_tp = tp
                rb = s("rt")[:, sg, :]
                self.op("vector", lambda E, rb=rb: E.tensor_tensor(s("s81")[:, 0:1], rb[:, 2:3], pb[6][:, 0:1], op=ALU.is_equal),
                        waits=[evebc])
                self.op("vector", lambda E, rb=rb: E.tensor_tensor(s("s81")[:, 1:2], rb[:, 0:1], rb[:, 1:2], op=ALU.subtract))
                self.op("vector", lambda E: E.tensor_tensor(s("s81")[:, 2:3], s("s81")[:, 0:1], s("s81")[:, 1:2], op=ALU.mult))
                self.op("vector", lambda E, rb=rb: E.tensor_tensor(s("s81")[:, 2:3], s("s81")[:, 2:3], rb[:, 1:2], op=ALU.add))
                evgt = self.op("vector", lambda E, yst=yst: E.tensor_scalar(yst[:, 0:D], yst[:, 0:D],
                                                                            s("s81")[:, 2:3], None, op0=ALU.mult))
                self.op("vector", lambda E: E.tensor_scalar(s("s81")[:, 3:4], s("s81")[:, 0:1], -1.0, 1.0, op0=ALU.mult, op1=ALU.add))
                self.op("vector", lambda E, rb=rb: E.tensor_scalar(s("s81")[:, 4:5], rb[:, 3:4], 2.0, s("s81")[:, 3:4], op0=ALU.mult, op1=ALU.add))
                self.op("vector", lambda E, rb=rb: E.tensor_scalar(s("s81")[:, 5:6], rb[:, 3:4], 8191.5, None, op0=ALU.is_gt))
                self.op("vector", lambda E: E.tensor_scalar(s("s81")[:, 6:7], s("pid")[:], float(2 * T), None, op0=ALU.add))
                self.op("vector", lambda E: E.tensor_tensor(s("s81")[:, 6:7], s("s81")[:, 6:7], s("s81")[:, 4:5], op=ALU.subtract))
                self.op("vector", lambda E: E.tensor_tensor(s("s81")[:, 6:7], s("s81")[:, 6:7], s("s81")[:, 5:6], op=ALU.mult))
                self.op("vector", lambda E: E.tensor_tensor(s("s81")[:, 4:5], s("s81")[:, 4:5], s("s81")[:, 6:7], op=ALU.add))
                evsx = self.op("vector", lambda E, b2=b2: E.tensor_copy(s("sidxu")[:, b2:b2 + 1], s("s81")[:, 4:5]),
                               waits=[ysc[b2]])
                prev_sc = ysc[0] if (ysc[0] or (0,))[-1] >= (ysc[1] or (0,))[-1] else ysc[1]
                ysc[b2] = self.dma("gpsimd", dr["buf"][:, :], yst[:, 0:D],
                                   waits=[evgt, evsx, prev_sc], ch="ysc",
                                   scatter_idx=s("sidxu")[:, b2:b2 + 1])
                self._rt_guard[sg] = self.vn()
            self._eb_guard = self.vn()
            self._y16_guard = ylast_tp
        BU = dr["buf"][:, :].rearrange("(t two) d -> t two d", two=2)
        xst = [None] * NT
        for j in range(NT):
            buf = j % 2
            evx = self.load_x(j, buf, extra_waits=[self.vn(), xst[j - 2] if j >= 2 else None])
            for sg in range(4):
                t0 = j * TILE + sg * 128
                first = (j == 0 and sg == 0)
                evb0 = self.dma("sync", s("cb0")[:], BU[t0:t0 + 128, 0, :],
                                waits=[self.vn(), ysc[0] if first else None, ysc[1] if first else None,
                                       self._h_guard if first else None], ch="cb0")
                evb1_ = self.dma("sync", s("cb1")[:], BU[t0:t0 + 128, 1, :],
                                 waits=[self.vn(), ysc[0] if first else None, ysc[1] if first else None,
                                        self._h_guard if first else None], ch="cb1")
                evtg = self.dma("sync", s("cbt")[:], TR[t0:t0 + 128, 0:D], waits=[self.vn()], ch="cbt")
                self.op("vector", lambda E: E.tensor_tensor(s("cb0")[:], s("cb0")[:], s("cb1")[:], op=ALU.add),
                        waits=[evb0, evb1_])
                self.op("vector", lambda E: E.tensor_tensor(s("cb0")[:], s("cb0")[:], s("cbt")[:], op=ALU.add), waits=[evtg])
                self.op("vector", lambda E: E.reduce_sum(s("s81")[:, 0:1], s("cb0")[:], axis=AX.X))
                self.op("vector", lambda E: E.tensor_scalar(s("s81")[:, 0:1], s("s81")[:, 0:1], 1.0 / D, None, op0=ALU.mult))
                self.op("vector", lambda E: E.tensor_scalar(s("cb0")[:], s("cb0")[:], s("s81")[:, 0:1], None, op0=ALU.subtract))
                self.op("scalar", lambda E: E.activation(s("cb1")[:], s("cb0")[:], AF.Square, accum_out=s("s81")[:, 1:2]),
                        waits=[self.vn()])
                self.op("vector", lambda E: E.tensor_scalar(s("s81")[:, 1:2], s("s81")[:, 1:2], 1.0 / D, 1e-5, op0=ALU.mult, op1=ALU.add),
                        waits=[self.an()])
                self.op("scalar", lambda E: E.activation(s("s81")[:, 1:2], s("s81")[:, 1:2], AF.Sqrt), waits=[self.vn()])
                self.op("vector", lambda E: E.reciprocal(s("s81")[:, 2:3], s("s81")[:, 1:2]), waits=[self.an()])
                evz2 = self.op("vector", lambda E: E.tensor_scalar(s("cb0")[:], s("cb0")[:], s("s81")[:, 2:3], None, op0=ALU.mult))
                for cc in range(NCH):
                    bank = pb[4 + (cc % 2)]
                    tp = self.op("tensor", lambda E, cc=cc, bank=bank: E.transpose(bank[:, 0:128], s("cb0")[:, cc * 128:(cc + 1) * 128], s("id128")[:]),
                                 waits=[evz2, self._tp_guard[cc % 2]], track=True)
                    ev = self.op("vector", lambda E, cc=cc, bank=bank: E.tensor_scalar(s("sB")[:, 0:128], bank[:, 0:128],
                                                                                       s("gam_h")[:, cc:cc + 1], s("bet_h")[:, cc:cc + 1],
                                                                                       op0=ALU.mult, op1=ALU.add),
                                 waits=[tp, evx if cc == 0 else None])
                    self._tp_guard[cc % 2] = ev
                    self.op("vector", lambda E, cc=cc, sg=sg, buf=buf: E.tensor_tensor(s("xt")[:, buf, cc, sg * 128:(sg + 1) * 128],
                                                                                       s("xt")[:, buf, cc, sg * 128:(sg + 1) * 128],
                                                                                       s("sB")[:, 0:128], op=ALU.add))
            xst[j] = self.store_x(j, buf, waits=[self.vn()])
        self.xstore_ev = xst
        self._sq_guard = xst[-1]
        self._gl_guard = None

    def emit_l1attn(self):
        dr, s = self.dram, self.s
        pb = self.pb
        prev_store = self.xstore_ev
        xst = [None] * NT
        TPS = SEQ // TILE
        qkv = s("big").rearrange("p (c t) -> p c t", t=TILE)
        self.sb["attn_tm"] = self.sb["tgt"][0:64, 0:2, :].rearrange("p a b -> p (a b)")
        ktm = s("gat")[0:64, 0, 0:1024]
        vtm = s("gat")[0:64, 1, 0:1024]
        atm = s("attn_tm")
        ctxs = s("ctxs")
        BS = 64
        for q in range(NSEQ):
            evz = self.op("vector", lambda E: E.memset(s("kcum")[:, 0:8], 0.0))
            evz2 = self.op("vector", lambda E: E.memset(ctxs[:, :, :], 0.0), waits=[self._ctx_guard])
            for tj in range(TPS):
                j = q * TPS + tj
                buf = j % 2
                w = self.xprev(prev_store, j) + [self.vn(), self.an(), xst[j - 2] if j >= 2 else None]
                evx = self.load_x(j, buf, extra_waits=w)
                self._ln_pe_guard = self._att_guard
                ln_ev = self.emit_ln(s("xt")[:, buf, :, :], evx)
                self._ln_pe_guard = None
                mmev = []
                for g in range(12):
                    for k in range(NCH):
                        wb = self._wb = (self._wb + 1) % 3
                        evw = self.dma("sync", s("wsl")[:, wb, 0:2, :],
                                       dr["qkvT"][k * 128:(k + 1) * 128, g * 256:(g + 1) * 256].rearrange("p (i q) -> p i q", q=128),
                                       waits=[self._wslab_free[wb], self.wT_ev if (q == 0 and tj == 0 and g == 0 and k == 0) else None],
                                       ch=f"wsl{wb}")
                        for i in range(2):
                            bw = [evw] + (ln_ev if k == 0 else []) + ([self._bank_free[i]] if k == 0 else [])
                            mmev.append(self.op("tensor",
                                                lambda E, i=i, k=k, wb=wb, g=g:
                                                E.matmul(pb[g % 2][:, i * 256:(i + 1) * 256][:, 0:512] if False else pb[i][:, :],
                                                         s("wsl")[:, wb, i, :], s("tgt")[:, k, :],
                                                         start=(k == 0), stop=(k == NCH - 1)),
                                                waits=bw, track=(i == 1)))
                        self._wslab_free[wb] = mmev[-1]
                    self._tgt_guard = mmev[-1]
                    for i in range(2):
                        cc = g * 2 + i
                        fn = AF.Exp if 8 <= cc < 16 else AF.Copy
                        self._bank_free[i] = self.op(
                            "scalar", lambda E, i=i, cc=cc, fn=fn: E.activation(qkv[:, cc, :], pb[i][:, :], fn),
                            waits=[mmev[(g * NCH + NCH - 1) * 2 + 1], self._big_guard if cc == 0 else None])
                for cc in range(NCH):
                    for sb in range(4):
                        bank = pb[4 + (sb % 2)]
                        tp = self.op("tensor",
                                     lambda E, cc=cc, sb=sb, bank=bank:
                                     E.transpose(bank[:, 0:128], qkv[:, cc, sb * 128:(sb + 1) * 128], s("id128")[:]),
                                     waits=[self.vn() if (cc == 0 and sb == 0) else None, self._tp_guard[sb % 2],
                                            (self.an() if (cc == 0 and sb == 0) else None)], track=True)
                        self.op("vector", lambda E, bank=bank: E.tensor_copy(s("sB")[:, 0:128], bank[:, 0:128]), waits=[tp])
                        for hh in range(2):
                            sl = s("sB")[:, hh * 64:(hh + 1) * 64]
                            self.op("vector", lambda E, sl=sl: E.reduce_max(s("s81")[:, 0:1], sl, axis=AX.X))
                            self.op("vector", lambda E: E.tensor_scalar(s("s81")[:, 1:2], s("s81")[:, 0:1], -1.0, None, op0=ALU.mult))
                            self.op("scalar", lambda E, sl=sl: E.activation(sl, sl, AF.Exp, bias=s("s81")[:, 1:2]), waits=[self.vn()])
                            self.op("vector", lambda E, sl=sl: E.reduce_sum(s("s81")[:, 2:3], sl, axis=AX.X), waits=[self.an()])
                            self.op("vector", lambda E: E.reciprocal(s("s81")[:, 3:4], s("s81")[:, 2:3]))
                            self.op("vector", lambda E: E.tensor_scalar(s("s81")[:, 3:4], s("s81")[:, 3:4], 0.125, None, op0=ALU.mult))
                            self.op("vector", lambda E, sl=sl: E.tensor_scalar(sl, sl, s("s81")[:, 3:4], None, op0=ALU.mult))
                        tp2 = self.op("tensor",
                                      lambda E, bank=bank: E.transpose(bank[:, 0:128], s("sB")[:, 0:128], s("id128")[:]),
                                      waits=[self.vn()], track=True)
                        ev = self.op("vector", lambda E, cc=cc, sb=sb, bank=bank:
                                     E.tensor_copy(qkv[:, cc, sb * 128:(sb + 1) * 128], bank[:, 0:128]), waits=[tp2])
                        self._tp_guard[sb % 2] = ev
                evbs = None
                for cc in range(NCH):
                    evbs = self.op("vector", lambda E, cc=cc: E.tensor_reduce(
                        s("bs")[:, cc * 8:(cc + 1) * 8],
                        qkv[:, 8 + cc, :].rearrange("p (b u) -> p b u", u=BS), axis=AX.X, op=ALU.add))
                for j0 in range(TILE // BS):
                    glob_first = (tj == 0 and j0 == 0)
                    cpev = []
                    if not glob_first:
                        for cc in range(NCH):
                            bko = pb[2 + cc // 4]
                            co = (cc % 4) * 128
                            cpev.append(self.op("vector", lambda E, cc=cc, bko=bko, co=co:
                                                E.tensor_copy(ctxs[0:64, cc, 0:64], bko[0:64, co:co + 64]),
                                                waits=[self._sj_guard if cc == 0 else None]))
                            cpev.append(self.op("vector", lambda E, cc=cc, bko=bko, co=co:
                                                E.tensor_copy(ctxs[64:128, cc, 0:64], bko[64:128, co + 64:co + 128])))
                    for cc in range(NCH):
                        cpev.append(self.op("vector", lambda E, cc=cc: E.tensor_copy(ctxs[:, cc, 64:65], s("kcum")[:, cc:cc + 1]),
                                            waits=[evz, evz2] if glob_first else []))
                    ev_ctx = cpev[-1]
                    ktv = []
                    for cc in range(NCH):
                        bank = pb[4 + (cc % 2)]
                        tpk = self.op("tensor", lambda E, cc=cc, j0=j0, bank=bank:
                                      E.transpose(bank[0:64, 0:128], qkv[:, 8 + cc, j0 * BS:(j0 + 1) * BS], s("id128")[:]),
                                      waits=[self._tp_guard[cc % 2], evbs if cc == 0 else None], track=True)
                        ev = self.op("vector", lambda E, cc=cc, bank=bank: E.tensor_copy(ktm[:, cc * 128:(cc + 1) * 128], bank[0:64, 0:128]),
                                     waits=[tpk, self._ktm_guard if cc == 0 else None])
                        self._tp_guard[cc % 2] = ev
                        tpv = self.op("tensor", lambda E, cc=cc, j0=j0, bank=bank:
                                      E.transpose(bank[0:64, 0:128], qkv[:, 16 + cc, j0 * BS:(j0 + 1) * BS], s("id128")[:]),
                                      waits=[self._tp_guard[cc % 2]], track=True)
                        ev = self.op("vector", lambda E, cc=cc, bank=bank: E.tensor_copy(vtm[:, cc * 128:(cc + 1) * 128], bank[0:64, 0:128]),
                                     waits=[tpv])
                        self._tp_guard[cc % 2] = ev
                        ktv.append(ev)
                    sjev = []
                    for cc in range(NCH):
                        bko = pb[2 + cc // 4]
                        co = (cc % 4) * 128
                        sjev.append(self.op("tensor",
                                            lambda E, cc=cc, bko=bko, co=co, st_=(glob_first and cc % 4 == 0),
                                            sp_=(tj == TPS - 1 and j0 == TILE // BS - 1):
                                            E.matmul(bko[:, co:co + 128], ktm[:, cc * 128:(cc + 1) * 128],
                                                     vtm[:, cc * 128:(cc + 1) * 128],
                                                     start=st_, stop=sp_,
                                                     skip_group_check=True),
                                            waits=[ktv[cc], (self.vn() if cc == 0 else None)], track=(cc >= NCH - 2)))
                    self._sj_guard = sjev[-1]
                    for cc in range(NCH):
                        for hh in range(2):
                            base = hh * 64
                            bk = pb[6 + ((cc * 2 + hh) % 2)]
                            mm = self.op("tensor",
                                         lambda E, cc=cc, hh=hh, base=base, bk=bk, j0=j0:
                                         E.matmul(bk[0:64, 0:65], qkv[base:base + 64, cc, j0 * BS:(j0 + 1) * BS],
                                                  ctxs[base:base + 64, cc, 0:65], start=True, stop=True),
                                         waits=[ev_ctx, self._att_free[(cc * 2 + hh) % 2]], track=True)
                            self.op("vector", lambda E, bk=bk: E.tensor_scalar(s("s81")[:, 4:5][0:64, :], bk[0:64, 64:65], 0.001, None, op0=ALU.max),
                                    waits=[mm])
                            self.op("vector", lambda E: E.reciprocal(s("s81")[0:64, 5:6], s("s81")[0:64, 4:5]))
                            ev = self.op("vector", lambda E, cc=cc, hh=hh, bk=bk:
                                         E.tensor_scalar(atm[:, cc * 128 + hh * 64:cc * 128 + (hh + 1) * 64], bk[0:64, 0:64],
                                                         s("s81")[0:64, 5:6], None, op0=ALU.mult),
                                         waits=[self._tgt_guard if (j0 == 0 and cc == 0 and hh == 0) else None])
                            self._att_free[(cc * 2 + hh) % 2] = ev
                    for cc in range(NCH):
                        bank = pb[4 + (cc % 2)]
                        tp = self.op("tensor", lambda E, cc=cc, bank=bank:
                                     E.transpose(bank[:, 0:64], atm[:, cc * 128:(cc + 1) * 128], s("id128")[0:64, 0:64]),
                                     waits=[self._tp_guard[cc % 2], (self.vn() if cc == 0 else None)], track=True)
                        self._last_atp = tp
                        ev = self.op("vector", lambda E, cc=cc, j0=j0, bank=bank:
                                     E.tensor_copy(s("y16")[:, cc, j0 * BS:(j0 + 1) * BS], bank[:, 0:64]),
                                     waits=[tp, (self._y16res if (j0 == 0 and cc == 0) else None)])
                        self._tp_guard[cc % 2] = ev
                    self._ktm_guard = sjev[-1]
                    self.op("vector", lambda E, j0=j0: E.tensor_tensor(
                        s("kcum")[:, 0:8], s("kcum")[:, 0:8],
                        s("bs")[:, :].rearrange("p (c b) -> p c b", b=8)[:, :, j0], op=ALU.add),
                        waits=[ev_ctx])
                mmev2 = []
                for g in range(4):
                    for k in range(NCH):
                        wb = self._wb = (self._wb + 1) % 3
                        evw = self.dma("sync", s("wsl")[:, wb, 0:2, :],
                                       dr["projT"][k * 128:(k + 1) * 128, g * 256:(g + 1) * 256].rearrange("p (i q) -> p i q", q=128),
                                       waits=[self._wslab_free[wb]], ch=f"wsl{wb}")
                        for i in range(2):
                            bw = [evw, (self.vn() if k == 0 else None), (self._bank_free[i] if k == 0 else None)]
                            mmev2.append(self.op("tensor",
                                                 lambda E, i=i, k=k, wb=wb:
                                                 E.matmul(pb[i][:, :], s("wsl")[:, wb, i, :], s("y16")[:, k, :],
                                                          start=(k == 0), stop=(k == NCH - 1)),
                                                 waits=bw, track=(i == 1)))
                        self._wslab_free[wb] = mmev2[-1]
                    for i in range(2):
                        cc = g * 2 + i
                        self._bank_free[i] = self.op("vector",
                                                     lambda E, i=i, cc=cc: E.tensor_scalar(s("sB")[:], pb[i][:, :],
                                                                                           s("projb_c")[:, cc:cc + 1], None, op0=ALU.add),
                                                     waits=[mmev2[(g * NCH + NCH - 1) * 2 + 1], evx if cc == 0 else None])
                        self.op("vector", lambda E, cc=cc, buf=buf: E.tensor_tensor(s("xt")[:, buf, cc, :], s("xt")[:, buf, cc, :],
                                                                                    s("sB")[:], op=ALU.add))
                self._y16res = mmev2[-1]
                self._big_guard = mmev2[-1]
                self._att_guard = mmev2[-1]
                self._tgt_guard = self._last_atp
                xst[j] = self.store_x(j, buf, waits=[self.vn()])
            self._ctx_guard = self._sj_guard
        self.xstore_ev = xst
        self._sq_guard = xst[-1]

    def emit_out_transpose(self):
        dr, s = self.dram, self.s
        prev = getattr(self, "xstore_ev", [None] * NT)
        laststore = None
        self._ot_st = []
        ot_evs = [None, None]
        ot_prev_evs = [None, None]
        for j in range(T // 128):
            buf = j % 2
            p = prev[(j * 128) // TILE]
            w = (list(p) if isinstance(p, list) else [p]) if p else []
            if j < 2:
                q = prev[NT - 2 + buf]
                w += (list(q) if isinstance(q, list) else [q]) if q else []
            if ot_evs[buf] is not None:
                w.append(ot_evs[buf])
            evl = self.dma("sync",
                           s("xt")[:, buf, :, 0:128],
                           dr["x_fm"][:, :].rearrange("(c p) t -> p c t", p=128)[:, :, j * 128:(j + 1) * 128],
                           waits=w, ch=f"xld{buf}")
            evs = []
            for c in range(NCH):
                bank = self.pb[4 + (c % 2)]
                bg = [evs[-2]] if len(evs) >= 2 else list(ot_prev_evs[c:c + 1])
                if j == 0 and c < 2:
                    bg.append(self._tp_guard[c % 2])
                evt = self.op("tensor",
                              lambda E, c=c, bank=bank, buf=buf:
                              E.transpose(bank[:, 0:128], s("xt")[:, buf, c, 0:128], s("id128")[:]),
                              waits=[evl] + bg, track=True)
                gw = [evt]
                if c == 0 and j >= 2:
                    gw.append(self._ot_st[j - 2])
                eve = self.op("vector",
                              lambda E, c=c, bank=bank, buf=buf:
                              E.tensor_copy(s("gat")[:, buf, c * 128:(c + 1) * 128],
                                            bank[:, 0:128]),
                              waits=gw, track=True)
                evs.append(eve)
            st = self.dma("gpsimd", dr["y_out"][j * 128:(j + 1) * 128, :],
                          s("gat")[:, buf, 0:D],
                          waits=[evs[-1], evs[-2]], ch=f"yout{buf}")
            self._ot_st.append(st)
            ot_evs[buf] = evs[-1]
            ot_prev_evs = [evs[-2], evs[-1]]
            laststore = st
        self.final_ev = laststore

    def build(self):
        self.declare()
        with self.es:
            self.alloc()
            self.emit_prep()
            self.emit_in_transpose()
            need_T = any(st in self.stages for st in ("l0attn", "l1ffn1", "l1ffn2", "l1attn"))
            if need_T:
                self.emit_pretranspose()
            for st in self.stages:
                if st == "l0ffn1" or st == "l0ffn2":
                    self.emit_l0ffn()
                elif st == "l0attn":
                    self.emit_l0attn()
                elif st == "l1ffn1":
                    self.emit_l1ffn(0)
                elif st == "l1ffn2":
                    self.emit_l1ffn(1)
                elif st == "l1attn":
                    self.emit_l1attn()
            self.emit_out_transpose()
            self.P.emit("gpsimd", lambda E: None, waits=[self.final_ev])
            block = self.es.enter_context(self.nc.Block())
            self.P.play(block)
        return self.nc


ALL_STAGES = ("l0ffn1", "l0attn", "l0ffn2", "l1ffn1", "l1attn", "l1ffn2")


def host_consts():
    c = {}
    c["c_id128"] = np.eye(128, dtype=np.float32)
    c["c_ones_col"] = np.ones((128, 1), np.float32)
    c["c_ones_row"] = np.ones((1, 128), np.float32)
    c["c_triu"] = np.triu(np.ones((128, 128), np.float32), 1)
    c["c_pid"] = np.arange(128, dtype=np.float32).reshape(128, 1)
    c["c_erow"] = np.tile(np.arange(8, dtype=np.float32), (128, 1))
    sel = np.zeros((128, 128), np.float32)
    selT = np.zeros((16, 1024), np.float32)
    for cc in range(8):
        for p in range(128):
            h = 2 * cc + p // 64
            sel[p, cc * 16 + h] = 1.0
            selT[h, cc * 128 + p] = 1.0
    c["c_sel"] = sel
    c["c_selT"] = selT
    c["c_crow"] = np.arange(NCHUNK, dtype=np.float32).reshape(1, NCHUNK)
    c["c_tid"] = (np.arange(128)[:, None] + 128 * np.arange(64)[None, :]).astype(np.float32)
    return c


def build_and_run(inputs, stages=ALL_STAGES):
    b = B(list(stages))
    nc = b.build()
    x = np.ascontiguousarray(inputs["L0_attn_x"].astype(np.float32))
    consts = host_consts()
    wnames = [k for k in inputs if k != "L0_attn_x"]
    in_maps = []
    for core in range(8):
        m = {"x_in": np.ascontiguousarray(x[core * NSEQ:(core + 1) * NSEQ].reshape(T, D))}
        for k in wnames:
            m[k] = np.ascontiguousarray(np.asarray(inputs[k], dtype=np.float32))
        m.update(consts)
        m = {k: v for k, v in m.items() if k in ("x_in",) or k in b.dram and k not in ("y_out",)}
        in_maps.append(m)
    res = run_bass_kernel_spmd(nc, in_maps, core_ids=list(range(8)))
    out = np.concatenate([r["y_out"].reshape(NSEQ, SEQ, D) for r in res.results], axis=0)
    return out


def kernel(**inputs):
    return build_and_run(inputs, ALL_STAGES)
